# version 01
# baseline (speedup 1.0000x reference)
import contextlib
import math
import numpy as np
import ml_dtypes
import concourse.bass as bass
import concourse.mybir as mybir
from concourse.bass_utils import run_bass_kernel_spmd
from concourse.bass_core_rust import IndirectOffsetOnAxis

F32 = mybir.dt.float32
BF16 = mybir.dt.bfloat16
U32 = mybir.dt.uint32
I32 = mybir.dt.int32
AF = mybir.ActivationFunctionType
ALU = mybir.AluOpType
AX = mybir.AxisListType

BF = ml_dtypes.bfloat16

B, N, D = 2, 8192, 1024
CORES = 8
TOK = N // 4
NT = TOK // 128
DC = D // 128
E_MOE, CAP = 16, 1024
FF_MOE = 4096
HALO = 513
WIN = 256
NWIN = TOK // WIN
HEADS1, DH1 = 16, 64
F1P = 5632
F2P = 2816
DEPTH1 = 6
_L1F_DEPTH = 6
NEG = -3.38e38
MSKV = -30000.0
OOBV = float(1 << 24)

ENGINES = ("v", "s", "t", "g", "y")


class Buf:
    __slots__ = ("name", "writer", "readers")

    def __init__(self, name=""):
        self.name = name
        self.writer = None
        self.readers = []


class DmaSem:
    __slots__ = ("name", "count", "waited", "step", "sem")

    def __init__(self, name, step=16):
        self.name = name
        self.count = 0
        self.waited = 0
        self.step = step
        self.sem = None


class PlannedOp:
    __slots__ = ("ename", "fn", "waits", "inc", "lineno")

    def __init__(self, ename, fn, waits, inc):
        self.ename = ename
        self.fn = fn
        self.waits = waits
        self.inc = inc
        self.lineno = 0


class Plan:
    def __init__(self):
        self.ops = {e: [] for e in ENGINES}
        self.ticks = {e: 0 for e in ENGINES}
        self.marks = {}
        self.dma_sems = []
        self.n_ops = 0
        self.pending = {e: ([], []) for e in ENGINES}

    def dma_sem(self, name, step=16):
        d = DmaSem(name, step)
        self.dma_sems.append(d)
        return d

    def _need_wait(self, ename, marker):
        if marker is None:
            return None
        key, val = marker
        if isinstance(key, DmaSem):
            val = key.count
            if val > key.waited:
                key.waited = val
        wm = self.marks.get((ename, key), 0)
        if val <= wm:
            return None
        self.marks[(ename, key)] = val
        return (key, val)

    def _guard_pending(self, ename, reads, writes):
        for e2 in ENGINES:
            if e2 == ename:
                continue
            pr, pw = self.pending[e2]
            for b in reads:
                assert b is None or b not in pw, \
                    f"read of {b.name} while quiet-written on {e2}"
            for b in writes:
                assert b is None or (b not in pw and b not in pr), \
                    f"write of {b.name} while quiet-pending on {e2}"

    def op(self, ename, fn, reads=(), writes=(), dma=None, quiet=False):
        self._guard_pending(ename, reads, writes)
        waits = []
        for b in reads:
            if b is not None:
                w = self._need_wait(ename, b.writer)
                if w:
                    waits.append(w)
        for b in writes:
            if b is None:
                continue
            w = self._need_wait(ename, b.writer)
            if w:
                waits.append(w)
            for m in b.readers:
                w = self._need_wait(ename, m)
                if w:
                    waits.append(w)
        if quiet:
            assert dma is None
            self.ops[ename].append(PlannedOp(ename, fn, self._coalesce(waits), None))
            self.n_ops += 1
            pr, pw = self.pending[ename]
            pr.extend(b for b in reads if b is not None)
            pw.extend(b for b in writes if b is not None)
            return None
        if dma is not None and dma.waited > 0 and dma.count > 0:
            w = self._need_wait(ename, (dma, dma.count))
            if w:
                waits.append(w)
        if dma is not None:
            dma.count += dma.step
            inc = (dma, dma.step)
            marker = (dma, dma.count)
        else:
            self.ticks[ename] += 1
            inc = (ename, 1)
            marker = (ename, self.ticks[ename])
        po = PlannedOp(ename, fn, self._coalesce(waits), inc)
        import sys as _sys
        po.lineno = _sys._getframe(1).f_lineno
        self.ops[ename].append(po)
        self.n_ops += 1
        if dma is None:
            pr, pw = self.pending[ename]
            for b in pr:
                b.readers.append(marker)
            for b in pw:
                b.writer = marker
            self.pending[ename] = ([], [])
        for b in reads:
            if b is not None:
                b.readers.append(marker)
        for b in writes:
            if b is not None:
                b.writer = marker
                b.readers = []
        return marker

    @staticmethod
    def _coalesce(waits):
        best = {}
        for key, val in waits:
            if val > best.get(id(key), (None, -1))[1]:
                best[id(key)] = (key, val)
        return list(best.values())

    def raw_wait(self, ename, marker):
        w = self._need_wait(ename, marker)
        if w:
            self.ops[ename].append(PlannedOp(ename, None, [w], None))
            self.n_ops += 1

    def barrier(self, ename, bufs):
        for b in bufs:
            if b is None:
                continue
            for m in [b.writer] + b.readers:
                w = self._need_wait(ename, m)
                if w:
                    self.ops[ename].append(PlannedOp(ename, None, [w], None))
                    self.n_ops += 1

    def group_end(self, bufs, dma):
        m = (dma, dma.count)
        for b in bufs:
            b.writer = m
            b.readers = []

    def fence(self):
        for e in ENGINES:
            assert not self.pending[e][0] and not self.pending[e][1], \
                f"fence with pending quiet effects on {e}"
        for e in ENGINES:
            for e2 in ENGINES:
                if self.ticks[e2] > 0:
                    w = self._need_wait(e, (e2, self.ticks[e2]))
                    if w:
                        self.ops[e].append(PlannedOp(e, None, [w], None))
                        self.n_ops += 1
            for d in self.dma_sems:
                if d.count > 0:
                    w = self._need_wait(e, (d, d.count))
                    if w:
                        self.ops[e].append(PlannedOp(e, None, [w], None))
                        self.n_ops += 1

    def emit(self, nc, ctx, block):
        tick_sems = {}
        for e in ENGINES:
            if self.ops[e]:
                tick_sems[e] = ctx.enter_context(nc.semaphore(f"tick_{e}"))
        for d in self.dma_sems:
            if d.count > 0:
                d.sem = ctx.enter_context(nc.semaphore(f"dma_{d.name}"))

        def sem_of(key):
            return tick_sems[key] if isinstance(key, str) else key.sem

        def run(eng, ename):
            for po in self.ops[ename]:
                for key, val in po.waits:
                    eng.wait_ge(sem_of(key), val)
                if po.fn is None:
                    continue
                inst = po.fn(eng)
                if po.inc is not None:
                    key, step = po.inc
                    if step == 1 and not isinstance(key, str):
                        inst.then_inc(sem_of(key))
                    else:
                        inst.then_inc(sem_of(key), step)

        section = {"v": block.vector, "s": block.scalar, "t": block.tensor,
                   "g": block.gpsimd, "y": block.sync}
        for e in ENGINES:
            if self.ops[e]:
                section[e](lambda eng, _e=e: run(eng, _e))



def _rot_tables(lo, hi):
    n = N
    half = np.arange(0, DH1, 2, dtype=np.float64)
    inv_freq = 1.0 / 10000.0 ** (half / DH1)
    t = np.arange(n, dtype=np.float64)
    freqs = t[:, None] * inv_freq[None, :]
    base_scale = (half + 0.4 * DH1) / (1.4 * DH1)
    power = (t - n // 2) / (WIN // 2)
    scale = base_scale[None, :] ** power[:, None]
    cos, sin = np.cos(freqs), np.sin(freqs)
    pos = np.arange(lo, hi)
    posc = np.clip(pos, 0, n - 1)
    cs = (cos * scale)[posc].T
    ss = (sin * scale)[posc].T
    ci = (cos / scale)[posc].T
    si = (sin / scale)[posc].T

    def tile4(a):
        return np.ascontiguousarray(np.tile(a, (4, 1)).astype(np.float32))
    return tile4(cs), tile4(ss), tile4(ci), tile4(si)


def _win_masks(first):
    out = []
    for sub in range(2):
        p = np.arange(128) + 128 * sub
        c = np.arange(512)
        allow = (c[None, :] >= p[:, None]) & (c[None, :] <= p[:, None] + WIN)
        if first:
            allow &= c[None, :] >= WIN
        m = np.where(allow, 0.0, MSKV).astype(np.float32)
        out.append(m)
    return out


def _perm_mat():
    P = np.zeros((128, 128), np.float32)
    for pp in range(128):
        if pp % 64 < 32:
            P[pp + 32, pp] = -1.0
        else:
            P[pp - 32, pp] = 1.0
    return P


def prep_inputs(I):
    x = np.ascontiguousarray(np.asarray(I["L0_attn_x"], np.float32))
    x_full = np.ascontiguousarray(x.reshape(B * N, D))
    wqkvT0 = np.ascontiguousarray(np.asarray(I["L0_attn_W_qkv"]).T)
    wg0 = np.ascontiguousarray(np.asarray(I["L0_ffn_Wg"]))
    w1 = np.asarray(I["L0_ffn_w1"])
    w2 = np.asarray(I["L0_ffn_w2"])
    l1a_w = np.ascontiguousarray(np.asarray(I["L1_attn_W_qkv"]))
    l1a_wo = np.ascontiguousarray(np.asarray(I["L1_attn_W_out"]))
    l1a_bo = np.asarray(I["L1_attn_b_out"]).reshape(DC, 128).T.copy()
    f_wqkv = np.ascontiguousarray(np.asarray(I["L1_ffn_Wqkv"]))
    f_qs = np.asarray(I["L1_ffn_q_scale"])
    f_ks = np.asarray(I["L1_ffn_k_scale"])
    f_wo = np.ascontiguousarray(np.asarray(I["L1_ffn_Wo"]))
    f_lng = np.asarray(I["L1_ffn_ln_g"]).reshape(DEPTH1, DC, 128).transpose(0, 2, 1).copy()
    f_lnb = np.asarray(I["L1_ffn_ln_b"]).reshape(DEPTH1, DC, 128).transpose(0, 2, 1).copy()
    W1 = np.asarray(I["L1_ffn_W1"])
    b1 = np.asarray(I["L1_ffn_b1"])
    W2 = np.asarray(I["L1_ffn_W2"])
    b2 = np.asarray(I["L1_ffn_b2"]).reshape(DEPTH1, DC, 128).transpose(0, 2, 1).copy()

    DI = 2730
    nfull = DI // 128
    w1p = np.zeros((DEPTH1, D, F1P), np.float32)
    b1p = np.zeros((DEPTH1, F1P), np.float32)
    for t in range(nfull + 1):
        a0, a1 = t * 128, min((t + 1) * 128, DI)
        wdt = a1 - a0
        col = t * 256
        w1p[:, :, col:col + wdt] = W1[:, :, a0:a1]
        b1p[:, col:col + wdt] = b1[:, a0:a1]
        w1p[:, :, col + 128:col + 128 + wdt] = W1[:, :, DI + a0:DI + a1]
        b1p[:, col + 128:col + 128 + wdt] = b1[:, DI + a0:DI + a1]
    b1p_c = b1p.reshape(DEPTH1, F1P // 128, 128).transpose(0, 2, 1).copy()
    w2p = np.zeros((DEPTH1, F2P, D), np.float32)
    w2p[:, :DI, :] = W2
    b2_c = b2

    qs128 = np.tile(f_qs, (1, 2))
    ks128 = np.tile(f_ks, (1, 2))

    perm = _perm_mat()
    ident = np.eye(128, dtype=np.float32)
    hsel = np.zeros((DC, 128, 16), np.float32)
    for ch in range(DC):
        for r in range(2):
            hsel[ch, r * 64:(r + 1) * 64, 2 * ch + r] = 1.0
    hbc = np.zeros((DC, 16, 128), np.float32)
    for ch in range(DC):
        for r in range(2):
            hbc[ch, 2 * ch + r, r * 64:(r + 1) * 64] = 1.0
    ut = np.triu(np.ones((128, 128), np.float32))
    ones_col = np.ones((128, 1), np.float32)
    ones_row = np.ones((1, 128), np.float32)

    in_maps = []
    msk_int = _win_masks(False)
    msk_fst = _win_masks(True)
    for c in range(CORES):
        b, j = divmod(c, 4)
        t0 = j * TOK
        g0 = b * N + t0
        d = {}
        d["x_own"] = np.ascontiguousarray(x[b, t0:t0 + TOK])
        d["x_full"] = x_full
        d["wqkvT0"] = wqkvT0
        d["wg0"] = wg0
        d["moe_w1"] = np.ascontiguousarray(w1[2 * c:2 * c + 2])
        d["moe_w2"] = np.ascontiguousarray(w2[2 * c:2 * c + 2])
        d["prefmask"] = np.asarray([[1.0] if r < c else [0.0] for r in range(8)],
                                   np.float32)
        d["gid1"] = (np.arange(TOK, dtype=np.float32).reshape(NT, 128).T + g0 + 1.0).copy()
        d["l1a_w"] = l1a_w
        d["l1a_wo"] = l1a_wo
        d["l1a_bo"] = l1a_bo
        d["f_wqkv"] = f_wqkv
        d["f_wo"] = f_wo
        d["f_w1p"] = w1p
        d["f_b1p"] = b1p_c
        d["f_w2p"] = w2p
        d["f_b2"] = b2_c
        d["f_lng"] = f_lng
        d["f_lnb"] = f_lnb
        d["f_qs"] = qs128[:, None, :].copy()
        d["f_ks"] = ks128[:, None, :].copy()
        d["f_qs_c"] = qs128[:, :, None].copy()
        d["f_ks_c"] = ks128[:, :, None].copy()
        cs, ss, ci, si = _rot_tables(t0 - HALO, t0 + TOK)
        d["rot_cs"], d["rot_ss"], d["rot_ci"], d["rot_si"] = cs, ss, ci, si
        m0 = msk_fst if j == 0 else msk_int
        d["msk_w0"] = np.stack(m0)
        d["msk_int"] = np.stack(msk_int)
        d["msk_ext"] = (np.full((128, 512), MSKV, np.float32) if j == 0
                        else msk_int[1])
        d["halo_gate"] = np.full((128, 1), 0.0 if j == 0 else 1.0, np.float32)
        blk = (j - 1) % 4
        d["halo_rows"] = (np.arange(128, dtype=np.float32)[:, None]
                          + (blk * 1024 + 128 * np.arange(DC, dtype=np.float32))[None, :]).copy()
        d["perm"] = perm
        d["ident"] = ident
        d["hsel"] = np.ascontiguousarray(hsel.transpose(1, 0, 2).reshape(128, DC * 16))
        d["hbc"] = np.ascontiguousarray(hbc.transpose(1, 0, 2).reshape(16, DC * 128))
        d["ut"] = ut
        d["ones_col"] = ones_col
        d["ones_row"] = ones_row
        in_maps.append(d)
    return in_maps



class K:

    def __init__(self, nc, ctx, P):
        self.nc = nc
        self.ctx = ctx
        self.P = P
        self.scopes = [ctx]

    def push_scope(self):
        es = contextlib.ExitStack()
        self.scopes.append(es)

    def pop_scope(self, fence=True):
        es = self.scopes.pop()
        es.close()
        if fence:
            self.P.fence()

    def sb(self, name, shape, dt):
        self._uid = getattr(self, "_uid", 0) + 1
        return self.scopes[-1].enter_context(
            self.nc.sbuf_tensor(f"{name}_u{self._uid}", shape, dt))

    def breg(self, g, val):
        if not hasattr(self, "_regs"):
            self._regs = {}
        if val not in self._regs:
            self._regs[val] = g.to_reg(val)
        return self._regs[val]

    def ps(self, name, shape, dt=F32):
        self._uid = getattr(self, "_uid", 0) + 1
        return self.scopes[-1].enter_context(
            self.nc.psum_tensor(f"{name}_u{self._uid}", shape, dt))

    def inp(self, name, shape):
        return self.nc.dram_tensor(name, shape, F32, kind="ExternalInput").ap()

    def outp(self, name, shape, dt=F32):
        return self.nc.dram_tensor(name, shape, dt, kind="ExternalOutput").ap()

    def dram(self, name, shape, dt=F32):
        return self.nc.dram_tensor(name, shape, dt, kind="Internal").ap()



def emit_l0(k, A, grp):
    nc, P = k.nc, k.P
    ld = P.dma_sem("l0ld")
    lw = P.dma_sem("l0lw")
    lx = P.dma_sem("l0lx")
    stk = P.dma_sem("l0stk")
    stv = P.dma_sem("l0stv")
    cck = P.dma_sem("l0cck", step=1)
    ccv = P.dma_sem("l0ccv", step=1)
    lkb = P.dma_sem("l0lkb")
    lvb = P.dma_sem("l0lvb")
    sta = P.dma_sem("l0sta")
    strt = P.dma_sem("l0strt")

    ident = k.sb("id_bf", [128, 128], BF16); bI = Buf("id")
    P.op("g", lambda g: g.dma_start(out=ident[:, :], in_=A["ident"]), writes=[bI], dma=ld)
    q_fm = [k.sb(f"qfm_{i}", [128, TOK], BF16) for i in range(DC)]
    bq = [Buf(f"qfm_{i}") for i in range(DC)]

    k.push_scope()
    w0 = [k.sb(f"w0_{i}", [128, 3072], BF16) for i in range(DC)]
    bw0 = [Buf(f"w0_{i}") for i in range(DC)]
    for i in range(DC):
        P.op("g", lambda g, i=i: g.dma_start(out=w0[i][:, :],
                                             in_=A["wqkvT0"][128 * i:128 * (i + 1), :]),
             writes=[bw0[i]], dma=lw)
    wg = k.sb("wg", [128, DC, 16], BF16); bwg = Buf("wg")
    P.op("g", lambda g: g.dma_start(out=wg[:, :, :],
                                    in_=A["wg0"].rearrange("(c p) e -> p c e", p=128)),
         writes=[bwg], dma=lw)

    xn_fm = [k.sb(f"xnfm_{i}", [128, TOK], BF16) for i in range(DC)]
    bxn = [Buf(f"xnfm_{i}") for i in range(DC)]
    xt = [k.sb(f"xt_{i}", [128, D], F32) for i in range(2)]
    bxt = [Buf(f"xt_{i}") for i in range(2)]
    xnt = [k.sb(f"xnt_{i}", [128, D], BF16) for i in range(2)]
    bxnt = [Buf(f"xnt_{i}") for i in range(2)]
    sqs = k.sb("sqs", [128, D], F32); bsq = Buf("sq")
    st1 = k.sb("st1", [128, 8], F32); bst = Buf("st")
    tr_ps = k.ps("tr_ps", [128, 512], BF16); btr = Buf("tr")

    def ln_tile(xbuf, bxb, xnb, bxnb):
        P.op("v", lambda v: v.reduce_sum(st1[:, 0:1], xbuf[:, :], axis=AX.X),
             reads=[bxb], writes=[bst])
        P.op("s", lambda s: s.activation(sqs[:, :], xbuf[:, :], AF.Square,
                                         accum_out=st1[:, 1:2]),
             reads=[bxb], writes=[bsq, bst])
        P.op("v", lambda v: v.tensor_scalar_mul(st1[:, 2:3], st1[:, 0:1], 1.0 / D),
             reads=[bst], writes=[bst])
        P.op("v", lambda v: v.tensor_tensor(st1[:, 3:4], st1[:, 2:3], st1[:, 2:3], ALU.mult),
             reads=[bst], writes=[bst])
        P.op("v", lambda v: v.scalar_tensor_tensor(st1[:, 3:4], st1[:, 1:2], 1.0 / D,
                                                   st1[:, 3:4], ALU.mult, ALU.subtract),
             reads=[bst], writes=[bst])
        P.op("v", lambda v: v.tensor_scalar_add(st1[:, 3:4], st1[:, 3:4], 1e-5),
             reads=[bst], writes=[bst])
        P.op("s", lambda s: s.activation(st1[:, 4:5], st1[:, 3:4], AF.Sqrt),
             reads=[bst], writes=[bst])
        P.op("v", lambda v: v.reciprocal(st1[:, 4:5], st1[:, 4:5]), reads=[bst], writes=[bst])
        P.op("v", lambda v: v.scalar_tensor_tensor(st1[:, 5:6], st1[:, 2:3], -1.0,
                                                   st1[:, 4:5], ALU.mult, ALU.mult),
             reads=[bst], writes=[bst])
        P.op("v", lambda v: v.tensor_scalar(xnb[:, :], xbuf[:, :], st1[:, 4:5],
                                            st1[:, 5:6], ALU.mult, ALU.add),
             reads=[bxb, bst], writes=[bxnb])

    for t in range(NT):
        sl = t % 2
        P.op("y", lambda y, t=t, sl=sl: y.dma_start(
                out=xt[sl][:, :], in_=A["x_own"][128 * t:128 * (t + 1), :]),
             writes=[bxt[sl]], dma=lx)
        ln_tile(xt[sl], bxt[sl], xnt[sl], bxnt[sl])
        for ch in range(DC):
            q = ch % 4
            P.op("t", lambda te, ch=ch, sl=sl, q=q: te.transpose(
                    tr_ps[:, 128 * q:128 * (q + 1)],
                    xnt[sl][:, 128 * ch:128 * (ch + 1)], ident[:, :]),
                 reads=[bxnt[sl], bI], writes=[btr])
            P.op("v", lambda v, ch=ch, t=t, q=q: v.tensor_copy(
                    xn_fm[ch][:, 128 * t:128 * (t + 1)],
                    tr_ps[:, 128 * q:128 * (q + 1)]),
                 reads=[btr], writes=[bxn[ch]])

    gl_ps = k.ps("gl_ps", [128, 16], F32); bgl = Buf("gl")
    glx = k.sb("glx", [128, 16], F32); bglx = Buf("glx")
    mx8 = k.sb("mx8", [128, 8], F32); bmx = Buf("mx8")
    ix8 = k.sb("ix8", [128, 8], U32); bix = Buf("ix8")
    exs = k.sb("exs", [128, 16], F32); bex = Buf("exs")
    rts = k.sb("rts", [128, 32], F32); brt = Buf("rts")
    for t in range(NT):
        for ch in range(DC):
            P.op("t", lambda te, ch=ch, t=t: te.matmul(
                    gl_ps[:, :], xn_fm[ch][:, 128 * t:128 * (t + 1)], wg[:, ch, :],
                    start=(ch == 0), stop=(ch == DC - 1)),
                 reads=[bxn[ch], bwg], writes=[bgl], quiet=(ch != DC - 1))
        P.op("v", lambda v: v.tensor_copy(glx[:, :], gl_ps[:, :]), reads=[bgl], writes=[bglx])
        P.op("v", lambda v: v.max(mx8[:, :], glx[:, :]), reads=[bglx], writes=[bmx])
        P.op("v", lambda v: v.max_index(ix8[:, :], mx8[:, :], glx[:, :]),
             reads=[bmx, bglx], writes=[bix])
        P.op("v", lambda v: v.tensor_scalar_mul(st1[:, 6:7], mx8[:, 0:1], -1.0),
             reads=[bmx], writes=[bst])
        P.op("s", lambda s: s.activation(exs[:, :], glx[:, :], AF.Exp,
                                         bias=st1[:, 6:7], accum_out=st1[:, 7:8]),
             reads=[bglx, bst], writes=[bex, bst])
        P.op("v", lambda v, t=t: v.reciprocal(rts[:, t:t + 1], st1[:, 7:8]),
             reads=[bst], writes=[brt])
        P.op("v", lambda v, t=t: v.tensor_copy(rts[:, 16 + t:17 + t], ix8[:, 0:1]),
             reads=[bix], writes=[brt])
    P.op("y", lambda y: y.dma_start(out=A["rt_d"], in_=rts[:, :]), reads=[brt], dma=strt)

    pj_ps = [k.ps(f"pj_ps{i}", [128, 512], F32) for i in range(2)]
    bpj = [Buf(f"pj{i}") for i in range(2)]

    def proj_chunk(dst_sl, bdst, col0, m):
        for n in range(TOK // 512):
            sl = n % 2
            for kk in range(DC):
                P.op("t", lambda te, m=m, n=n, kk=kk, sl=sl: te.matmul(
                        pj_ps[sl][:, :],
                        w0[kk][:, col0 + 128 * m:col0 + 128 * (m + 1)],
                        xn_fm[kk][:, 512 * n:512 * (n + 1)],
                        start=(kk == 0), stop=(kk == DC - 1)),
                     reads=[bw0[kk], bxn[kk]], writes=[bpj[sl]], quiet=(kk != DC - 1))
            P.op("v", lambda v, n=n, sl=sl: v.tensor_copy(
                    dst_sl[:, 512 * n:512 * (n + 1)], pj_ps[sl][:, :]),
                 reads=[bpj[sl]], writes=[bdst])

    for m in range(DC):
        proj_chunk(q_fm[m], bq[m], 0, m)

    kch = [k.sb(f"kch_{i}", [128, TOK], BF16) for i in range(2)]
    bkch = [Buf(f"kch_{i}") for i in range(2)]
    for m in range(DC):
        sl = m % 2
        proj_chunk(kch[sl], bkch[sl], D, m)
        P.op("y", lambda y, m=m, sl=sl: y.dma_start(
                out=A["kag_in"][128 * m:128 * (m + 1), :], in_=kch[sl][:, :]),
             reads=[bkch[sl]], dma=stk)

    vt = [k.sb(f"vt_{i}", [128, D], BF16) for i in range(2)]
    bvt = [Buf(f"vt_{i}") for i in range(2)]
    vp_ps = [k.ps(f"vp_ps{i}", [128, 512], F32) for i in range(2)]
    bvp = [Buf(f"vp{i}") for i in range(2)]
    for t in range(NT):
        sl = t % 2
        for nn in range(2):
            for kk in range(DC):
                P.op("t", lambda te, t=t, nn=nn, kk=kk, sl=sl: te.matmul(
                        vp_ps[nn][:, :],
                        xn_fm[kk][:, 128 * t:128 * (t + 1)],
                        w0[kk][:, 2 * D + 512 * nn:2 * D + 512 * (nn + 1)],
                        start=(kk == 0), stop=(kk == DC - 1)),
                     reads=[bxn[kk], bw0[kk]], writes=[bvp[nn]], quiet=(kk != DC - 1))
            P.op("v", lambda v, nn=nn, sl=sl: v.tensor_copy(
                    vt[sl][:, 512 * nn:512 * (nn + 1)], vp_ps[nn][:, :]),
                 reads=[bvp[nn]], writes=[bvt[sl]])
        P.op("y", lambda y, t=t, sl=sl: y.dma_start(
                out=A["vag_in"][128 * t:128 * (t + 1), :], in_=vt[sl][:, :]),
             reads=[bvt[sl]], dma=stv)

    bkagi = Buf("kagi"); bvagi = Buf("vagi")
    P.group_end([bkagi], stk)
    P.group_end([bvagi], stv)
    bkag = Buf("kag"); bvag = Buf("vag")
    P.op("g", lambda g: g.collective_compute("AllGather", ALU.bypass,
                                             replica_groups=grp,
                                             ins=[A["kag_in"]], outs=[A["kag"]]),
         reads=[bkagi], writes=[bkag], dma=cck)
    P.op("g", lambda g: g.collective_compute("AllGather", ALU.bypass,
                                             replica_groups=grp,
                                             ins=[A["vag_in"]], outs=[A["vag"]]),
         reads=[bvagi], writes=[bvag], dma=ccv)

    k.pop_scope()

    Kb = [k.sb(f"Kb_{i}", [128, TOK], BF16) for i in range(DC)]
    bKb = [Buf(f"Kb_{i}") for i in range(DC)]
    Vb = [k.sb(f"Vb_{i}", [128, D], BF16) for i in range(NT)]
    bVb = [Buf(f"Vb_{i}") for i in range(NT)]
    acc = [k.sb(f"acc_{i}", [128, D], F32) for i in range(NT)]
    bacc = [Buf(f"acc_{i}") for i in range(NT)]
    ml = k.sb("ml", [128, 64], F32)
    bml = [Buf(f"ml_{i}") for i in range(NT)]
    fsc = k.sb("fsc", [128, 8], F32); bfsc = Buf("fsc")
    s_ps = k.ps("s_ps", [128, TOK], F32); bS = Buf("s_ps")
    av_ps = k.ps("av_ps", [128, D], F32); bAV = Buf("av")
    pt_ps = k.ps("pt_ps", [128, 512], BF16); bPT = Buf("ptp")
    Pt = k.sb("Pt", [128, TOK], BF16); bPt = Buf("Pt")
    PTs = k.sb("PTs", [128, TOK], BF16); bPTs = Buf("PTs")

    for qt in range(NT):
        P.op("v", lambda v, qt=qt: v.memset(acc[qt][:, :], 0.0), writes=[bacc[qt]])
        P.op("v", lambda v, qt=qt: v.memset(ml[:, qt:qt + 1], NEG), writes=[bml[qt]])
        P.op("v", lambda v, qt=qt: v.memset(ml[:, 16 + qt:17 + qt], 0.0), writes=[bml[qt]])

    for kc in range(4):
        for i in range(DC):
            P.op("y", lambda y, i=i, kc=kc: y.dma_start(
                    out=Kb[i][:, :], in_=A["kag"][1024 * kc + 128 * i:1024 * kc + 128 * (i + 1), :]),
                 reads=[bkag], writes=[bKb[i]], dma=lkb)
        for i in range(NT):
            P.op("g", lambda g, i=i, kc=kc: g.dma_start(
                    out=Vb[i][:, :], in_=A["vag"][TOK * kc + 128 * i:TOK * kc + 128 * (i + 1), :]),
                 reads=[bvag], writes=[bVb[i]], dma=lvb)
        for qt in range(NT):
            for g4 in range(4):
                for kk in range(DC):
                    P.op("t", lambda te, qt=qt, g4=g4, kk=kk: te.matmul(
                            s_ps[:, 512 * g4:512 * (g4 + 1)],
                            q_fm[kk][:, 128 * qt:128 * (qt + 1)],
                            Kb[kk][:, 512 * g4:512 * (g4 + 1)],
                            start=(kk == 0), stop=(kk == DC - 1)),
                         reads=[bq[kk], bKb[kk]], writes=[bS],
                         quiet=not (kk == DC - 1 and g4 == 3))
            P.op("v", lambda v: v.reduce_max(fsc[:, 0:1], s_ps[:, :], axis=AX.X),
                 reads=[bS], writes=[bfsc])
            P.op("v", lambda v, qt=qt: v.scalar_tensor_tensor(
                    fsc[:, 1:2], fsc[:, 0:1], 1.0 / 32.0, ml[:, qt:qt + 1],
                    ALU.mult, ALU.max),
                 reads=[bfsc, bml[qt]], writes=[bfsc])
            P.op("v", lambda v, qt=qt: v.tensor_tensor(
                    fsc[:, 2:3], ml[:, qt:qt + 1], fsc[:, 1:2], ALU.subtract),
                 reads=[bml[qt], bfsc], writes=[bfsc])
            P.op("s", lambda s: s.activation(fsc[:, 2:3], fsc[:, 2:3], AF.Exp),
                 reads=[bfsc], writes=[bfsc])
            P.op("v", lambda v: v.tensor_scalar_mul(fsc[:, 3:4], fsc[:, 1:2], -1.0),
                 reads=[bfsc], writes=[bfsc])
            P.op("v", lambda v, qt=qt: v.tensor_copy(ml[:, qt:qt + 1], fsc[:, 1:2]),
                 reads=[bfsc], writes=[bml[qt]])
            P.op("s", lambda s: s.activation(Pt[:, :], s_ps[:, :], AF.Exp,
                                             bias=fsc[:, 3:4], scale=1.0 / 32.0,
                                             accum_out=fsc[:, 4:5]),
                 reads=[bS, bfsc], writes=[bPt, bfsc])
            P.op("v", lambda v, qt=qt: v.tensor_tensor(
                    ml[:, 16 + qt:17 + qt], ml[:, 16 + qt:17 + qt], fsc[:, 2:3], ALU.mult),
                 reads=[bfsc], writes=[bml[qt]])
            P.op("v", lambda v, qt=qt: v.tensor_tensor(
                    ml[:, 16 + qt:17 + qt], ml[:, 16 + qt:17 + qt], fsc[:, 4:5], ALU.add),
                 reads=[bfsc], writes=[bml[qt]])
            for r in range(4):
                for u in range(4):
                    i = 4 * r + u
                    P.op("t", lambda te, i=i, u=u: te.transpose(
                            pt_ps[:, 128 * u:128 * (u + 1)],
                            Pt[:, 128 * i:128 * (i + 1)], ident[:, :]),
                         reads=[bPt, bI], writes=[bPT], quiet=(u != 3))
                P.op("v", lambda v, r=r: v.tensor_copy(
                        PTs[:, 512 * r:512 * (r + 1)], pt_ps[:, :]),
                     reads=[bPT], writes=[bPTs])
            for g2 in range(2):
                for i in range(NT):
                    P.op("t", lambda te, i=i, g2=g2: te.matmul(
                            av_ps[:, 512 * g2:512 * (g2 + 1)],
                            PTs[:, 128 * i:128 * (i + 1)],
                            Vb[i][:, 512 * g2:512 * (g2 + 1)],
                            start=(i == 0), stop=(i == NT - 1)),
                         reads=[bPTs, bVb[i]], writes=[bAV],
                         quiet=not (i == NT - 1 and g2 == 1))
            P.op("v", lambda v, qt=qt: v.scalar_tensor_tensor(
                    acc[qt][:, :], acc[qt][:, :], fsc[:, 2:3], av_ps[:, :],
                    ALU.mult, ALU.add),
                 reads=[bfsc, bAV, bacc[qt]], writes=[bacc[qt]])

    att = [k.sb(f"att_{i}", [128, D], F32) for i in range(2)]
    batt = [Buf(f"att_{i}") for i in range(2)]
    for qt in range(NT):
        sl = qt % 2
        P.op("v", lambda v, qt=qt: v.reciprocal(fsc[:, 5:6], ml[:, 16 + qt:17 + qt]),
             reads=[bml[qt]], writes=[bfsc])
        P.op("v", lambda v, qt=qt, sl=sl: v.tensor_scalar_mul(
                att[sl][:, :], acc[qt][:, :], fsc[:, 5:6]),
             reads=[bacc[qt], bfsc], writes=[batt[sl]])
        P.op("y", lambda y, qt=qt, sl=sl: y.dma_start(
                out=A["a0_d"][128 * qt:128 * (qt + 1), :], in_=att[sl][:, :]),
             reads=[batt[sl]], dma=sta)
    P.raw_wait("y", (sta, sta.count))
    P.raw_wait("y", (strt, strt.count))




def emit_moe(k, A):
    nc, P = k.nc, k.P
    ld = P.dma_sem("mld")
    lcn = P.dma_sem("mlcn")
    stc = P.dma_sem("mstc")
    ccc = P.dma_sem("mccc", step=1)
    stz = P.dma_sem("mstz")
    sct = P.dma_sem("msct")
    ccr = P.dma_sem("mccr", step=1)
    ltb = P.dma_sem("mltb")
    gx = P.dma_sem("mgx")
    lw1 = P.dma_sem("mlw1")
    lw2 = P.dma_sem("mlw2")
    seo = P.dma_sem("mseo")
    cce = P.dma_sem("mcce", step=1)
    geo = P.dma_sem("mgeo")
    stm = P.dma_sem("mstm")

    ldgi = P.dma_sem("mldgi")
    ident = k.sb("mid_bf", [128, 128], BF16); bI = Buf("mid")
    P.op("g", lambda g: g.dma_start(out=ident[:, :], in_=A["ident"]), writes=[bI], dma=ldgi)
    rts = k.sb("mrts", [128, 32], F32); brt = Buf("mrts")
    P.op("y", lambda y: y.dma_start(out=rts[:, :], in_=A["rt_d"]), writes=[brt], dma=ld)
    slot_u = k.sb("mslotu", [128, NT], U32); bslu = Buf("mslotu")
    gidu = k.sb("mgidu", [128, NT], U32); bgu = Buf("mgidu")
    xgT = [k.sb(f"mxgT_{i}", [128, TOK], BF16) for i in range(DC)]
    bxgT = [Buf(f"mxgT_{i}") for i in range(DC)]

    k.push_scope()
    utc = k.sb("mut", [128, 128], F32); but = Buf("mut")
    P.op("y", lambda y: y.dma_start(out=utc[:, :], in_=A["ut"]), writes=[but], dma=ld)
    onec = k.sb("monec", [128, 1], F32); bo1 = Buf("monec")
    P.op("y", lambda y: y.dma_start(out=onec[:, :], in_=A["ones_col"]), writes=[bo1], dma=ld)
    oner = k.sb("moner", [1, 128], F32); bor = Buf("moner")
    P.op("y", lambda y: y.dma_start(out=oner[:, :], in_=A["ones_row"]), writes=[bor], dma=ld)
    pfm = k.sb("mpfm", [8, 1], F32); bpf = Buf("mpfm")
    P.op("y", lambda y: y.dma_start(out=pfm[:, :], in_=A["prefmask"]), writes=[bpf], dma=ld)
    gid1 = k.sb("mgid", [128, NT], F32); bgi = Buf("mgid")
    P.op("y", lambda y: y.dma_start(out=gid1[:, :], in_=A["gid1"]), writes=[bgi], dma=ld)

    idxf = rts[:, 16:32]
    ohs = k.sb("mohs", [128, 16 * NT], F32); boh = Buf("mohs")
    lr = k.sb("mlr", [128, 16 * NT], F32); blr = Buf("mlr")
    cnt = k.sb("mcnt", [1, 16], F32); bcn = Buf("mcnt")
    scr = k.sb("mscr", [1, 64], F32); bsc = Buf("mscr")
    cum_ps = k.ps("mcum", [128, NT], F32); bcp = Buf("mcum")
    cs_ps = k.ps("mcs", [1, NT], F32); bcs = Buf("mcs")
    for e in range(16):
        P.op("v", lambda v, e=e: v.tensor_single_scalar(
                ohs[:, NT * e:NT * (e + 1)], idxf, float(e), ALU.is_equal),
             reads=[brt], writes=[boh])
        P.op("t", lambda t, e=e: t.matmul(cs_ps[:, :], onec[:, :],
                                          ohs[:, NT * e:NT * (e + 1)],
                                          start=True, stop=True),
             reads=[bo1, boh], writes=[bcs])
        P.op("v", lambda v: v.tensor_copy(scr[0:1, 32:32 + NT], cs_ps[0:1, :]),
             reads=[bcs], writes=[bsc])
        P.op("v", lambda v: v.tensor_tensor_scan(scr[0:1, 0:NT], oner[0:1, 0:NT],
                                                 scr[0:1, 32:32 + NT], 0.0,
                                                 ALU.mult, ALU.add),
             reads=[bsc, bor], writes=[bsc])
        P.op("v", lambda v: v.tensor_tensor(scr[0:1, 16:16 + NT], scr[0:1, 0:NT],
                                            scr[0:1, 32:32 + NT], ALU.subtract),
             reads=[bsc], writes=[bsc])
        P.op("v", lambda v, e=e: v.tensor_copy(cnt[0:1, e:e + 1], scr[0:1, NT - 1:NT]),
             reads=[bsc], writes=[bcn])
        P.op("t", lambda t, e=e: t.matmul(cum_ps[:, :], utc[:, :],
                                          ohs[:, NT * e:NT * (e + 1)],
                                          start=True, stop=False),
             reads=[but, boh], writes=[bcp], quiet=True)
        P.op("t", lambda t, e=e: t.matmul(cum_ps[:, :], oner[:, :], scr[0:1, 16:16 + NT],
                                          start=False, stop=True),
             reads=[bor, bsc], writes=[bcp])
        P.op("v", lambda v, e=e: v.tensor_copy(lr[:, NT * e:NT * (e + 1)], cum_ps[:, :]),
             reads=[bcp], writes=[blr])

    P.op("y", lambda y: y.dma_start(out=A["cnt_in"], in_=cnt[0:1, :]), reads=[bcn], dma=stc)
    bcnag = Buf("cntag")
    P.op("g", lambda g: g.collective_compute("AllGather", ALU.bypass,
                                             replica_groups=GRP_ALL,
                                             ins=[A["cnt_in"]], outs=[A["cnt_ag"]]),
         reads=[bcn], writes=[bcnag], dma=ccc)
    cnts = k.sb("mcnts", [8, 16], F32); bcns = Buf("mcnts")
    P.op("y", lambda y: y.dma_start(out=cnts[:, :], in_=A["cnt_ag"]),
         reads=[bcnag], writes=[bcns], dma=lcn)
    prf_ps = k.ps("mprf", [128, 16], F32); bpp = Buf("mprf")
    P.op("t", lambda t: t.matmul(prf_ps[0:1, :], pfm[:, :], cnts[:, :],
                                 start=True, stop=True),
         reads=[bpf, bcns], writes=[bpp])
    prow = k.sb("mprow", [1, 16], F32); bpr = Buf("mprow")
    P.op("v", lambda v: v.tensor_copy(prow[0:1, :], prf_ps[0:1, :]),
         reads=[bpp], writes=[bpr])
    P.op("t", lambda t: t.matmul(prf_ps[:, :], oner[:, :], prow[0:1, :],
                                 start=True, stop=True),
         reads=[bor, bpr], writes=[bpp])
    pb = k.sb("mpb", [128, 16], F32); bpb = Buf("mpb")
    P.op("v", lambda v: v.tensor_copy(pb[:, :], prf_ps[:, :]), reads=[bpp], writes=[bpb])

    sel = k.sb("msel", [128, NT], F32); bsl = Buf("msel")
    tmp = k.sb("mtmp", [128, NT], F32); btm = Buf("mtmp")
    P.op("v", lambda v: v.memset(sel[:, :], -1.0), writes=[bsl])
    for e in range(16):
        P.op("v", lambda v, e=e: v.tensor_scalar_add(tmp[:, :], lr[:, NT * e:NT * (e + 1)],
                                                     pb[:, e:e + 1]),
             reads=[blr, bpb], writes=[btm])
        P.op("v", lambda v, e=e: v.tensor_tensor(tmp[:, :], tmp[:, :],
                                                 ohs[:, NT * e:NT * (e + 1)], ALU.mult),
             reads=[btm, boh], writes=[btm])
        P.op("v", lambda v: v.tensor_tensor(sel[:, :], sel[:, :], tmp[:, :], ALU.add),
             reads=[btm, bsl], writes=[bsl])
    vm = k.sb("mvm", [128, NT], F32); bvm = Buf("mvm")
    P.op("v", lambda v: v.tensor_single_scalar(vm[:, :], sel[:, :], float(CAP), ALU.is_ge),
         reads=[bsl], writes=[bvm])
    slot = k.sb("mslot", [128, NT], F32); bslo = Buf("mslot")
    P.op("v", lambda v: v.scalar_tensor_tensor(slot[:, :], idxf, float(CAP), sel[:, :],
                                               ALU.mult, ALU.add),
         reads=[brt, bsl], writes=[bslo])
    P.op("v", lambda v: v.scalar_tensor_tensor(slot[:, :], vm[:, :], OOBV, slot[:, :],
                                               ALU.mult, ALU.add),
         reads=[bvm, bslo], writes=[bslo])
    P.op("v", lambda v: v.tensor_copy(slot_u[:, :], slot[:, :]), reads=[bslo], writes=[bslu])
    if "dbg_d" in A:
        P.op("y", lambda y: y.dma_start(out=A["dbg_d"][:, 0:NT], in_=sel[:, :]),
             reads=[bsl], dma=stm)
        P.op("y", lambda y: y.dma_start(out=A["dbg_d"][:, NT:2 * NT], in_=slot[:, :]),
             reads=[bslo], dma=stm)

    zero = k.sb("mzero", [128, 1024], F32); bz = Buf("mzero")
    P.op("v", lambda v: v.memset(zero[:, :], 0.0), writes=[bz])
    btbl = Buf("mtbl")
    P.op("y", lambda y: y.dma_start(out=A["tbl_d"].rearrange("(a p) w -> p a w", p=128),
                                    in_=zero[:, :].rearrange("p (a w) -> p a w", w=8)),
         reads=[bz], writes=[btbl], dma=stz)
    gval = k.sb("mgval", [128, NT, 8], F32); bgv = Buf("mgval")
    P.op("v", lambda v: v.tensor_copy(gval[:, :, :],
                                      gid1[:, :].unsqueeze(2).broadcast_to([128, NT, 8])),
         reads=[bgi], writes=[bgv])
    for t in range(NT):
        P.op("g", lambda g, t=t: g.indirect_dma_start(
                out=A["tbl_d"],
                out_offset=IndirectOffsetOnAxis(ap=slot_u[:, t:t + 1], axis=0),
                in_=gval[:, t, :], in_offset=None,
                bounds_check=k.breg(g, B * N - 1), oob_is_err=False),
             reads=[bslu, bgv, btbl], writes=[btbl], dma=sct)
    btrs = Buf("mtrs")
    P.op("g", lambda g: g.collective_compute("ReduceScatter", ALU.add,
                                             replica_groups=GRP_ALL,
                                             ins=[A["tbl_d"]], outs=[A["tbl_rs"]]),
         reads=[btbl], writes=[btrs], dma=ccr)

    sts = k.sb("msts", [128, NT, 8], F32); bsts = Buf("msts")
    P.op("y", lambda y: y.dma_start(out=sts[:, :, :],
                                    in_=A["tbl_rs"].rearrange("(a p) w -> p a w", p=128)),
         reads=[btrs], writes=[bsts], dma=ltb)
    gidx = k.sb("mgidx", [128, NT], F32); bgx2 = Buf("mgidx")
    for t in range(NT):
        P.op("v", lambda v, t=t: v.tensor_single_scalar(gidx[:, t:t + 1], sts[:, t, 0:1],
                                                        0.0, ALU.is_equal),
             reads=[bsts], writes=[bgx2])
        P.op("v", lambda v, t=t: v.scalar_tensor_tensor(gidx[:, t:t + 1], gidx[:, t:t + 1],
                                                        OOBV, sts[:, t, 0:1],
                                                        ALU.mult, ALU.add),
             reads=[bgx2, bsts], writes=[bgx2])
        P.op("v", lambda v, t=t: v.tensor_scalar_add(gidx[:, t:t + 1], gidx[:, t:t + 1], -1.0),
             reads=[bgx2], writes=[bgx2])
    P.op("v", lambda v: v.tensor_copy(gidu[:, :], gidx[:, :]), reads=[bgx2], writes=[bgu])
    if "dbg_d" in A:
        P.op("y", lambda y: y.dma_start(out=A["dbg_d"][:, 2 * NT:3 * NT], in_=gidx[:, :]),
             reads=[bgx2], dma=stm)
    k.pop_scope()

    k.push_scope()
    xg = [k.sb(f"mxg_{i}", [128, D], F32) for i in range(2)]
    bxg = [Buf(f"mxg_{i}") for i in range(2)]
    xng = [k.sb(f"mxng_{i}", [128, D], BF16) for i in range(2)]
    bxng = [Buf(f"mxng_{i}") for i in range(2)]
    sqs = k.sb("msqs", [128, D], F32); bsq = Buf("msq")
    st1 = k.sb("mst1", [128, 8], F32); bst = Buf("mst")
    tr_ps = k.ps("mtr", [128, 512], BF16); btr = Buf("mtr")

    def ln_tile(xbuf, bxb, xnb, bxnb):
        P.op("v", lambda v: v.reduce_sum(st1[:, 0:1], xbuf[:, :], axis=AX.X),
             reads=[bxb], writes=[bst])
        P.op("s", lambda s: s.activation(sqs[:, :], xbuf[:, :], AF.Square,
                                         accum_out=st1[:, 1:2]),
             reads=[bxb], writes=[bsq, bst])
        P.op("v", lambda v: v.tensor_scalar_mul(st1[:, 2:3], st1[:, 0:1], 1.0 / D),
             reads=[bst], writes=[bst])
        P.op("v", lambda v: v.tensor_tensor(st1[:, 3:4], st1[:, 2:3], st1[:, 2:3], ALU.mult),
             reads=[bst], writes=[bst])
        P.op("v", lambda v: v.scalar_tensor_tensor(st1[:, 3:4], st1[:, 1:2], 1.0 / D,
                                                   st1[:, 3:4], ALU.mult, ALU.subtract),
             reads=[bst], writes=[bst])
        P.op("v", lambda v: v.tensor_scalar_add(st1[:, 3:4], st1[:, 3:4], 1e-5),
             reads=[bst], writes=[bst])
        P.op("s", lambda s: s.activation(st1[:, 4:5], st1[:, 3:4], AF.Sqrt),
             reads=[bst], writes=[bst])
        P.op("v", lambda v: v.reciprocal(st1[:, 4:5], st1[:, 4:5]), reads=[bst], writes=[bst])
        P.op("v", lambda v: v.scalar_tensor_tensor(st1[:, 5:6], st1[:, 2:3], -1.0,
                                                   st1[:, 4:5], ALU.mult, ALU.mult),
             reads=[bst], writes=[bst])
        P.op("v", lambda v: v.tensor_scalar(xnb[:, :], xbuf[:, :], st1[:, 4:5],
                                            st1[:, 5:6], ALU.mult, ALU.add),
             reads=[bxb, bst], writes=[bxnb])

    for t in range(NT):
        sl = t % 2
        P.op("v", lambda v, sl=sl: v.memset(xg[sl][:, :], 0.0), writes=[bxg[sl]])
        P.op("g", lambda g, t=t, sl=sl: g.indirect_dma_start(
                out=xg[sl][:, :], out_offset=None,
                in_=A["x_full"],
                in_offset=IndirectOffsetOnAxis(ap=gidu[:, t:t + 1], axis=0),
                bounds_check=k.breg(g, B * N - 1), oob_is_err=False),
             reads=[bgu, bxg[sl]], writes=[bxg[sl]], dma=gx)
        ln_tile(xg[sl], bxg[sl], xng[sl], bxng[sl])
        for ch in range(DC):
            q = ch % 4
            P.op("t", lambda te, ch=ch, sl=sl, q=q: te.transpose(
                    tr_ps[:, 128 * q:128 * (q + 1)],
                    xng[sl][:, 128 * ch:128 * (ch + 1)], ident[:, :]),
                 reads=[bxng[sl], bI], writes=[btr])
            P.op("v", lambda v, ch=ch, t=t, q=q: v.tensor_copy(
                    xgT[ch][:, 128 * t:128 * (t + 1)],
                    tr_ps[:, 128 * q:128 * (q + 1)]),
                 reads=[btr], writes=[bxgT[ch]])

    k.pop_scope()

    k.push_scope()
    w1c = [k.sb(f"mw1_{i}", [128, FF_MOE], BF16) for i in range(DC)]
    bw1 = [Buf(f"mw1_{i}") for i in range(DC)]
    h_sl = k.sb("mh", [128, 32 * 512], BF16); bh = Buf("mh")
    w2c = [k.sb(f"mw2_{i}", [128, D], BF16) for i in range(2)]
    bw2 = [Buf(f"mw2_{i}") for i in range(2)]
    h_ps = [k.ps(f"mhp_{i}", [128, 512], F32) for i in range(2)]
    bhp = [Buf(f"mhp_{i}") for i in range(2)]
    o_ps = [k.ps(f"mop_{i}", [128, D], F32) for i in range(2)]
    bop = [Buf(f"mop_{i}") for i in range(2)]
    eo = [k.sb(f"meo_{i}", [128, D], F32) for i in range(2)]
    beo = [Buf(f"meo_{i}") for i in range(2)]

    for e01 in range(2):
        for i in range(DC):
            P.op("g", lambda g, e01=e01, i=i: g.dma_start(
                    out=w1c[i][:, :], in_=A["moe_w1"][e01, 128 * i:128 * (i + 1), :]),
                 writes=[bw1[i]], dma=lw1)
        for sg in range(2):
            soff = 1024 * e01 + 512 * sg
            for f in range(32):
                sl = f % 2
                for kk in range(DC):
                    P.op("t", lambda t, f=f, kk=kk, sl=sl, soff=soff: t.matmul(
                            h_ps[sl][:, :], w1c[kk][:, 128 * f:128 * (f + 1)],
                            xgT[kk][:, soff:soff + 512],
                            start=(kk == 0), stop=(kk == DC - 1)),
                         reads=[bw1[kk], bxgT[kk]], writes=[bhp[sl]],
                         quiet=(kk != DC - 1))
                P.op("s", lambda s, f=f, sl=sl: s.activation(
                        h_sl[:, 512 * f:512 * (f + 1)], h_ps[sl][:, :], AF.Relu),
                     reads=[bhp[sl]], writes=[bh])
            for sp in range(2):
                for f in range(32):
                    sl = f % 2
                    P.op("g", lambda g, e01=e01, f=f, sl=sl: g.dma_start(
                            out=w2c[sl][:, :],
                            in_=A["moe_w2"][e01, 128 * f:128 * (f + 1), :]),
                         writes=[bw2[sl]], dma=lw2)
                    for hp in range(2):
                        st4 = 2 * sp + hp
                        for nn in range(2):
                            P.op("t", lambda t, f=f, st4=st4, hp=hp, nn=nn, sl=sl: t.matmul(
                                    o_ps[hp][:, 512 * nn:512 * (nn + 1)],
                                    h_sl[:, 512 * f + 128 * st4:512 * f + 128 * (st4 + 1)],
                                    w2c[sl][:, 512 * nn:512 * (nn + 1)],
                                    start=(f == 0), stop=(f == 31)),
                                 reads=[bh, bw2[sl]], writes=[bop[hp]],
                                 quiet=(nn == 0))
                for hp in range(2):
                    st4 = 2 * sp + hp
                    P.op("v", lambda v, hp=hp: v.tensor_copy(
                            eo[hp][:, :], o_ps[hp][:, :]),
                         reads=[bop[hp]], writes=[beo[hp]])
                    row = 1024 * e01 + 512 * sg + 128 * st4
                    P.op("y", lambda y, row=row, hp=hp: y.dma_start(
                            out=A["eo_d"][row:row + 128, :], in_=eo[hp][:, :]),
                         reads=[beo[hp]], dma=seo)
                    if "eo_dbg" in A:
                        P.op("y", lambda y, row=row, hp=hp: y.dma_start(
                                out=A["eo_dbg"][row:row + 128, :], in_=eo[hp][:, :]),
                             reads=[beo[hp]], dma=seo)

    beod = Buf("meod"); beag = Buf("meag")
    P.group_end([beod], seo)
    P.op("g", lambda g: g.collective_compute("AllGather", ALU.bypass,
                                             replica_groups=GRP_ALL,
                                             ins=[A["eo_d"]], outs=[A["eo_ag"]]),
         reads=[beod], writes=[beag], dma=cce)

    k.pop_scope()

    cg = [k.sb(f"mcg_{i}", [128, D], F32) for i in range(2)]
    bcg = [Buf(f"mcg_{i}") for i in range(2)]
    mo = [k.sb(f"mmo_{i}", [128, D], F32) for i in range(2)]
    bmo = [Buf(f"mmo_{i}") for i in range(2)]
    for t in range(NT):
        sl = t % 2
        P.op("v", lambda v, sl=sl: v.memset(cg[sl][:, :], 0.0), writes=[bcg[sl]])
        P.op("g", lambda g, t=t, sl=sl: g.indirect_dma_start(
                out=cg[sl][:, :], out_offset=None,
                in_=A["eo_ag"],
                in_offset=IndirectOffsetOnAxis(ap=slot_u[:, t:t + 1], axis=0),
                bounds_check=k.breg(g, B * N - 1), oob_is_err=False),
             reads=[bslu, bcg[sl], beag], writes=[bcg[sl]], dma=geo)
        P.op("v", lambda v, t=t, sl=sl: v.tensor_scalar_mul(
                mo[sl][:, :], cg[sl][:, :], rts[:, t:t + 1]),
             reads=[bcg[sl], brt], writes=[bmo[sl]])
        P.op("y", lambda y, t=t, sl=sl: y.dma_start(
                out=A["moe_d"][128 * t:128 * (t + 1), :], in_=mo[sl][:, :]),
             reads=[bmo[sl]], dma=stm)
    P.raw_wait("y", (stm, stm.count))



def emit_l1a(k, A):
    nc, P = k.nc, k.P
    ld = P.dma_sem("ald")
    lx = P.dma_sem("alx")
    sx1 = P.dma_sem("asx1")
    sxn = P.dma_sem("asxn")
    sct = P.dma_sem("asct")
    ccx = P.dma_sem("accx", step=1)
    lct = P.dma_sem("alct")
    spt = P.dma_sem("aspt")

    ldy = P.dma_sem("aldy")
    identb = k.sb("aid_b", [128, 128], BF16); bIb = Buf("aidb")
    P.op("g", lambda g: g.dma_start(out=identb[:, :], in_=A["ident"]), writes=[bIb], dma=ld)
    identf = k.sb("aid_f", [128, 128], F32); bIf = Buf("aidf")
    P.op("y", lambda y: y.dma_start(out=identf[:, :], in_=A["ident"]), writes=[bIf], dma=ldy)
    wl = [k.sb(f"awl_{i}", [128, 1536], BF16) for i in range(DC)]
    bwl = [Buf(f"awl_{i}") for i in range(DC)]
    for i in range(DC):
        P.op("g", lambda g, i=i: g.dma_start(out=wl[i][:, :],
                                             in_=A["l1a_w"][128 * i:128 * (i + 1), :]),
             writes=[bwl[i]], dma=ld)
    wo = [k.sb(f"awo_{i}", [128, D], BF16) for i in range(4)]
    bwo = [Buf(f"awo_{i}") for i in range(4)]
    for i in range(4):
        P.op("g", lambda g, i=i: g.dma_start(out=wo[i][:, :],
                                             in_=A["l1a_wo"][128 * i:128 * (i + 1), :]),
             writes=[bwo[i]], dma=ld)
    bo = k.sb("abo", [128, DC], F32); bbo = Buf("abo")
    P.op("y", lambda y: y.dma_start(out=bo[:, :], in_=A["l1a_bo"]), writes=[bbo], dma=ldy)

    xn1b = [k.sb(f"axn_{i}", [128, TOK], BF16) for i in range(DC)]
    bxn = [Buf(f"axn_{i}") for i in range(DC)]
    qe = [k.sb(f"aqe_{i}", [64, TOK], BF16) for i in range(8)]
    bqe = [Buf(f"aqe_{i}") for i in range(8)]
    ke = [k.sb(f"ake_{i}", [128, 512], BF16) for i in range(NT)]
    bke = [Buf(f"ake_{i}") for i in range(NT)]
    ve = [k.sb(f"ave_{i}", [128, 512], BF16) for i in range(NT)]
    bve = [Buf(f"ave_{i}") for i in range(NT)]
    inn = [k.sb(f"ainn_{i}", [128, TOK], BF16) for i in range(4)]
    binn = [Buf(f"ainn_{i}") for i in range(4)]
    ctxb = k.sb("actxb", [64, 512], BF16); bcxb = Buf("acxb")

    k.push_scope()
    xa = [k.sb(f"axa_{i}", [128, D], F32) for i in range(2)]
    bxa = [Buf(f"axa_{i}") for i in range(2)]
    xb = k.sb("axb", [128, D], F32); bxb2 = Buf("axb")
    xc = k.sb("axc", [128, D], F32); bxc = Buf("axc")
    xnt_b = [k.sb(f"axnt_{i}", [128, D], BF16) for i in range(2)]
    bxnt = [Buf(f"axnt_{i}") for i in range(2)]
    xnt_f = [k.sb(f"axnf_{i}", [128, D], F32) for i in range(2)]
    bxnf = [Buf(f"axnf_{i}") for i in range(2)]
    sqs = k.sb("asqs", [128, D], F32); bsq = Buf("asq")
    st1 = k.sb("ast1", [128, 8], F32); bst = Buf("ast")
    tr_pb = k.ps("atrb", [128, 512], BF16); btrb = Buf("atrb")
    tr_pf = k.ps("atrf", [128, 512], F32); btrf = Buf("atrf")
    xnf_row = [k.sb(f"axnr_{i}", [128, D], F32) for i in range(2)]
    bxnr = [Buf(f"axnr_{i}") for i in range(2)]

    for t in range(NT):
        sl = t % 2
        P.op("y", lambda y, t=t, sl=sl: y.dma_start(
                out=xa[sl][:, :], in_=A["x_own"][128 * t:128 * (t + 1), :]),
             writes=[bxa[sl]], dma=lx)
        P.op("y", lambda y, t=t: y.dma_start(
                out=xb[:, :], in_=A["a0_d"][128 * t:128 * (t + 1), :]),
             writes=[bxb2], dma=lx)
        P.op("y", lambda y, t=t: y.dma_start(
                out=xc[:, :], in_=A["moe_d"][128 * t:128 * (t + 1), :]),
             writes=[bxc], dma=lx)
        P.op("v", lambda v, sl=sl: v.tensor_tensor(xa[sl][:, :], xa[sl][:, :],
                                                   xb[:, :], ALU.add),
             reads=[bxb2, bxa[sl]], writes=[bxa[sl]])
        P.op("v", lambda v, sl=sl: v.tensor_tensor(xa[sl][:, :], xa[sl][:, :],
                                                   xc[:, :], ALU.add),
             reads=[bxc, bxa[sl]], writes=[bxa[sl]])
        P.op("y", lambda y, t=t, sl=sl: y.dma_start(
                out=A["x1_d"][128 * t:128 * (t + 1), :], in_=xa[sl][:, :]),
             reads=[bxa[sl]], dma=sx1)
        P.op("v", lambda v, sl=sl: v.reduce_sum(st1[:, 0:1], xa[sl][:, :], axis=AX.X),
             reads=[bxa[sl]], writes=[bst])
        P.op("s", lambda s, sl=sl: s.activation(sqs[:, :], xa[sl][:, :], AF.Square,
                                                accum_out=st1[:, 1:2]),
             reads=[bxa[sl]], writes=[bsq, bst])
        P.op("v", lambda v: v.tensor_scalar_mul(st1[:, 2:3], st1[:, 0:1], 1.0 / D),
             reads=[bst], writes=[bst])
        P.op("v", lambda v: v.tensor_tensor(st1[:, 3:4], st1[:, 2:3], st1[:, 2:3], ALU.mult),
             reads=[bst], writes=[bst])
        P.op("v", lambda v: v.scalar_tensor_tensor(st1[:, 3:4], st1[:, 1:2], 1.0 / D,
                                                   st1[:, 3:4], ALU.mult, ALU.subtract),
             reads=[bst], writes=[bst])
        P.op("v", lambda v: v.tensor_scalar_add(st1[:, 3:4], st1[:, 3:4], 1e-5),
             reads=[bst], writes=[bst])
        P.op("s", lambda s: s.activation(st1[:, 4:5], st1[:, 3:4], AF.Sqrt),
             reads=[bst], writes=[bst])
        P.op("v", lambda v: v.reciprocal(st1[:, 4:5], st1[:, 4:5]), reads=[bst], writes=[bst])
        P.op("v", lambda v: v.scalar_tensor_tensor(st1[:, 5:6], st1[:, 2:3], -1.0,
                                                   st1[:, 4:5], ALU.mult, ALU.mult),
             reads=[bst], writes=[bst])
        P.op("v", lambda v, sl=sl: v.tensor_scalar(xnt_b[sl][:, :], xa[sl][:, :],
                                                   st1[:, 4:5], st1[:, 5:6],
                                                   ALU.mult, ALU.add),
             reads=[bxa[sl], bst], writes=[bxnt[sl]])
        P.op("v", lambda v, sl=sl: v.tensor_scalar(xnt_f[sl][:, :], xa[sl][:, :],
                                                   st1[:, 4:5], st1[:, 5:6],
                                                   ALU.mult, ALU.add),
             reads=[bxa[sl], bst], writes=[bxnf[sl]])
        for ch in range(DC):
            q = ch % 4
            P.op("t", lambda te, ch=ch, sl=sl, q=q: te.transpose(
                    tr_pb[:, 128 * q:128 * (q + 1)],
                    xnt_b[sl][:, 128 * ch:128 * (ch + 1)], identb[:, :]),
                 reads=[bxnt[sl], bIb], writes=[btrb])
            P.op("v", lambda v, ch=ch, t=t, q=q: v.tensor_copy(
                    xn1b[ch][:, 128 * t:128 * (t + 1)],
                    tr_pb[:, 128 * q:128 * (q + 1)]),
                 reads=[btrb], writes=[bxn[ch]])
            P.op("t", lambda te, ch=ch, sl=sl, q=q: te.transpose(
                    tr_pf[:, 128 * q:128 * (q + 1)],
                    xnt_f[sl][:, 128 * ch:128 * (ch + 1)], identf[:, :]),
                 reads=[bxnf[sl], bIf], writes=[btrf])
            P.op("v", lambda v, ch=ch, sl=sl, q=q: v.tensor_copy(
                    xnf_row[sl][:, 128 * ch:128 * (ch + 1)],
                    tr_pf[:, 128 * q:128 * (q + 1)]),
                 reads=[btrf], writes=[bxnr[sl]])
        P.op("y", lambda y, t=t, sl=sl: y.dma_start(
                out=A["xn1fm_d"].rearrange("(c p) n -> p c n", p=128)[:, :, 128 * t:128 * (t + 1)],
                in_=xnf_row[sl][:, :].rearrange("p (c n) -> p c n", n=128)),
             reads=[bxnr[sl]], dma=sxn)
    k.pop_scope()

    k.push_scope()
    pj = [k.ps(f"apj_{i}", [128, 512], F32) for i in range(2)]
    bpj = [Buf(f"apj_{i}") for i in range(2)]
    kv_ps = [k.ps(f"akv_{i}", [128, D], F32) for i in range(2)]
    bkv = [Buf(f"akv_{i}") for i in range(2)]
    ea = k.sb("aea", [128, 512], F32); bea = Buf("aea")
    eb = k.sb("aeb", [128, 512], F32); beb = Buf("aeb")
    ec = k.sb("aec", [128, 512], F32); bec = Buf("aec")
    for m in range(4):
        for n in range(4):
            sl = n % 2
            for kk in range(DC):
                P.op("t", lambda te, m=m, n=n, kk=kk, sl=sl: te.matmul(
                        pj[sl][:, :], wl[kk][:, 128 * m:128 * (m + 1)],
                        xn1b[kk][:, 512 * n:512 * (n + 1)],
                        start=(kk == 0), stop=(kk == DC - 1)),
                     reads=[bwl[kk], bxn[kk]], writes=[bpj[sl]], quiet=(kk != DC - 1))
            P.op("v", lambda v, sl=sl: v.tensor_scalar_min(ea[:, :], pj[sl][:, :], 0.0),
                 reads=[bpj[sl]], writes=[bea])
            P.op("v", lambda v, sl=sl: v.tensor_scalar_max(eb[:, :], pj[sl][:, :], 0.0),
                 reads=[bpj[sl]], writes=[beb])
            P.op("s", lambda s: s.activation(ea[:, :], ea[:, :], AF.Exp),
                 reads=[bea], writes=[bea])
            for hh in range(2):
                P.op("v", lambda v, m=m, n=n, hh=hh: v.tensor_tensor(
                        qe[2 * m + hh][:, 512 * n:512 * (n + 1)],
                        ea[64 * hh:64 * (hh + 1), :], eb[64 * hh:64 * (hh + 1), :],
                        ALU.add),
                     reads=[bea, beb], writes=[bqe[2 * m + hh]])
    for t in range(NT):
        sl = t % 2
        for nn in range(2):
            for kk in range(DC):
                P.op("t", lambda te, t=t, nn=nn, kk=kk, sl=sl: te.matmul(
                        kv_ps[sl][:, 512 * nn:512 * (nn + 1)],
                        xn1b[kk][:, 128 * t:128 * (t + 1)],
                        wl[kk][:, 512 + 512 * nn:512 + 512 * (nn + 1)],
                        start=(kk == 0), stop=(kk == DC - 1)),
                     reads=[bxn[kk], bwl[kk]], writes=[bkv[sl]],
                     quiet=not (kk == DC - 1 and nn == 1))
        P.op("v", lambda v, t=t, sl=sl: v.tensor_scalar_min(ea[:, :], kv_ps[sl][:, 0:512], 0.0),
             reads=[bkv[sl]], writes=[bea])
        P.op("v", lambda v, t=t, sl=sl: v.tensor_scalar_max(eb[:, :], kv_ps[sl][:, 0:512], 0.0),
             reads=[bkv[sl]], writes=[beb])
        P.op("s", lambda s: s.activation(ea[:, :], ea[:, :], AF.Exp),
             reads=[bea], writes=[bea])
        P.op("v", lambda v, t=t: v.tensor_tensor(ke[t][:, :], ea[:, :], eb[:, :], ALU.add),
             reads=[bea, beb], writes=[bke[t]])
        P.op("v", lambda v, t=t, sl=sl: v.tensor_copy(ve[t][:, :], kv_ps[sl][:, 512:1024]),
             reads=[bkv[sl]], writes=[bve[t]])

    ctx_ps = k.ps("actx", [64, 512], F32); bctx = Buf("actx")
    for t in range(NT):
        for h in range(8):
            P.op("t", lambda te, t=t, h=h: te.matmul(
                    ctx_ps[0:64, 64 * h:64 * (h + 1)],
                    ke[t][:, 64 * h:64 * (h + 1)], ve[t][:, 64 * h:64 * (h + 1)],
                    start=(t == 0 and h == 0), stop=(t == NT - 1 and h == 7)),
                 reads=[bke[t], bve[t]], writes=[bctx],
                 quiet=not (t == NT - 1 and h == 7))
    ctxs = k.sb("actxs", [64, 512], F32); bcxs = Buf("acxs")
    P.op("v", lambda v: v.tensor_copy(ctxs[:, :], ctx_ps[0:64, :]), reads=[bctx], writes=[bcxs])
    P.op("y", lambda y: y.dma_start(out=A["ctx_in"], in_=ctxs[:, :]), reads=[bcxs], dma=sct)
    bcin = Buf("acin"); bcar = Buf("acar")
    P.group_end([bcin], sct)
    P.op("g", lambda g: g.collective_compute("AllReduce", ALU.add,
                                             replica_groups=GRP_BATCH,
                                             ins=[A["ctx_in"]], outs=[A["ctx_ar"]]),
         reads=[bcin], writes=[bcar], dma=ccx)
    P.op("g", lambda g: g.dma_start(out=ctxb[:, :], in_=A["ctx_ar"]),
         reads=[bcar], writes=[bcxb], dma=lct)
    k.pop_scope()

    k.push_scope()
    oc_ps = [k.ps(f"aoc_{i}", [64, 512], F32) for i in range(2)]
    boc = [Buf(f"aoc_{i}") for i in range(2)]
    for h in range(8):
        for n in range(4):
            sl = n % 2
            P.op("t", lambda te, h=h, n=n, sl=sl: te.matmul(
                    oc_ps[sl][0:64, :], ctxb[0:64, 64 * h:64 * (h + 1)],
                    qe[h][:, 512 * n:512 * (n + 1)],
                    start=True, stop=True),
                 reads=[bcxb, bqe[h]], writes=[boc[sl]])
            P.op("v", lambda v, h=h, n=n, sl=sl: v.tensor_copy(
                    inn[h // 2][64 * (h % 2):64 * (h % 2) + 64, 512 * n:512 * (n + 1)],
                    oc_ps[sl][0:64, :]),
                 reads=[boc[sl]], writes=[binn[h // 2]])
    po = [k.ps(f"apo_{i}", [128, 512], F32) for i in range(2)]
    bpo = [Buf(f"apo_{i}") for i in range(2)]
    pst = [k.sb(f"apst_{i}", [128, 512], F32) for i in range(2)]
    bpst = [Buf(f"apst_{i}") for i in range(2)]
    for m in range(DC):
        for n in range(4):
            sl = n % 2
            for kc in range(4):
                P.op("t", lambda te, m=m, n=n, kc=kc, sl=sl: te.matmul(
                        po[sl][:, :], wo[kc][:, 128 * m:128 * (m + 1)],
                        inn[kc][:, 512 * n:512 * (n + 1)],
                        start=(kc == 0), stop=(kc == 3)),
                     reads=[bwo[kc], binn[kc]], writes=[bpo[sl]], quiet=(kc != 3))
            P.op("v", lambda v, m=m, sl=sl: v.tensor_scalar_add(
                    pst[sl][:, :], po[sl][:, :], bo[:, m:m + 1]),
                 reads=[bpo[sl], bbo], writes=[bpst[sl]])
            P.op("y", lambda y, m=m, n=n, sl=sl: y.dma_start(
                    out=A["part_d"][128 * m:128 * (m + 1), 512 * n:512 * (n + 1)],
                    in_=pst[sl][:, :]),
                 reads=[bpst[sl]], dma=spt)
    k.pop_scope()
    P.raw_wait("y", (spt, spt.count))
    P.raw_wait("y", (sx1, sx1.count))
    P.raw_wait("y", (sxn, sxn.count))


def build_launch_l1a():
    nc = bass.Bass("TRN2", target_bir_lowering=False)
    P = Plan()
    ctx = contextlib.ExitStack()
    k = K(nc, ctx, P)
    A = {}
    A["x_own"] = k.inp("x_own", [TOK, D])
    A["a0_d"] = k.inp("a0_d", [TOK, D])
    A["moe_d"] = k.inp("moe_d", [TOK, D])
    A["l1a_w"] = k.inp("l1a_w", [D, 1536])
    A["l1a_wo"] = k.inp("l1a_wo", [512, D])
    A["l1a_bo"] = k.inp("l1a_bo", [128, DC])
    A["ident"] = k.inp("ident", [128, 128])
    A["x1_d"] = k.outp("x1_d", [TOK, D])
    A["xn1fm_d"] = k.outp("xn1fm_d", [D, TOK])
    A["part_d"] = k.outp("part_d", [D, TOK])
    A["ctx_in"] = k.dram("ctx_in", [64, 512])
    A["ctx_ar"] = k.dram("ctx_ar", [64, 512])
    with ctx:
        emit_l1a(k, A)
        with nc.Block() as block:
            P.emit(nc, ctx, block)
    return nc, ["x_own", "a0_d", "moe_d", "l1a_w", "l1a_wo", "l1a_bo", "ident"], \
        ["x1_d", "xn1fm_d", "part_d"]




SDH = HALO


def emit_l1f(k, A):
    nc, P = k.nc, k.P
    ld = P.dma_sem("fld")
    ldg2 = P.dma_sem("fldg2")
    lhb = P.dma_sem("flhb")
    lwq = P.dma_sem("flwq")
    lwo = P.dma_sem("flwo")
    lw1 = P.dma_sem("flw1")
    lw2 = P.dma_sem("flw2")
    ltb = P.dma_sem("fltb")
    shal = P.dma_sem("fshal")
    cch = P.dma_sem("fcch", step=1)
    ghal = P.dma_sem("fghal")
    swr = P.dma_sem("fswr")
    lsd = P.dma_sem("flsd")
    lres = P.dma_sem("flres")
    sy = P.dma_sem("fsy")
    lfin = P.dma_sem("flfin")

    identb = k.sb("fid_b", [128, 128], BF16); bIb = Buf("fidb")
    P.op("g", lambda g: g.dma_start(out=identb[:, :], in_=A["ident"]), writes=[bIb], dma=ldg2)
    identf = k.sb("fid_f", [128, 128], F32); bIf = Buf("fidf")
    P.op("y", lambda y: y.dma_start(out=identf[:, :], in_=A["ident"]), writes=[bIf], dma=ld)
    onesr = k.sb("fonesr", [1, 128], F32); bor = Buf("fonesr")
    P.op("y", lambda y: y.dma_start(out=onesr[:, :], in_=A["ones_row"]), writes=[bor], dma=ld)
    onesc = k.sb("fonesc", [128, 1], F32); boc = Buf("fonesc")
    P.op("y", lambda y: y.dma_start(out=onesc[:, :], in_=A["ones_col"]), writes=[boc], dma=ld)
    permf = k.sb("fperm", [128, 128], F32); bpm = Buf("fperm")
    P.op("y", lambda y: y.dma_start(out=permf[:, :], in_=A["perm"]), writes=[bpm], dma=ld)
    hself = k.sb("fhsel", [128, DC * 16], F32); bhs = Buf("fhsel")
    P.op("y", lambda y: y.dma_start(out=hself[:, :], in_=A["hsel"]), writes=[bhs], dma=ld)
    hbcf = k.sb("fhbc", [16, DC * 128], F32); bhb = Buf("fhbc")
    P.op("y", lambda y: y.dma_start(out=hbcf[:, :], in_=A["hbc"]), writes=[bhb], dma=ld)
    msk = k.sb("fmsk", [128, 5, 512], BF16); bmk = Buf("fmsk")
    P.op("g", lambda g: g.dma_start(out=msk[:, 0:2, :],
                                    in_=A["msk_w0"].rearrange("a p w -> p a w")),
         writes=[bmk], dma=ldg2)
    P.op("g", lambda g: g.dma_start(out=msk[:, 2:4, :],
                                    in_=A["msk_int"].rearrange("a p w -> p a w")),
         writes=[bmk], dma=ldg2)
    P.op("g", lambda g: g.dma_start(out=msk[:, 4:5, :], in_=A["msk_ext"].unsqueeze(1)),
         writes=[bmk], dma=ldg2)
    hgate = k.sb("fhg", [128, 1], F32); bhg = Buf("fhg")
    P.op("y", lambda y: y.dma_start(out=hgate[:, :], in_=A["halo_gate"]), writes=[bhg], dma=ld)
    hrows = k.sb("fhrows", [128, DC], F32); bhr = Buf("fhrows")
    P.op("y", lambda y: y.dma_start(out=hrows[:, :], in_=A["halo_rows"]), writes=[bhr], dma=ld)
    hrowu = k.sb("fhrowu", [128, DC], U32); bhru = Buf("fhrowu")
    P.op("v", lambda v: v.tensor_copy(hrowu[:, :], hrows[:, :]), reads=[bhr], writes=[bhru])

    sd = A["sd"]
    bsd = Buf("fsd")
    bhal = Buf("fhal")
    bhag = Buf("fhag")

    pq = k.sb("fpq", [128, 128], BF16); bpq = Buf("fpq")
    pk = k.sb("fpk", [128, 128], BF16); bpk = Buf("fpk")
    hbw = k.sb("fhbw", [16, DC * 128], F32); bhw = Buf("fhbw")
    b1c = k.sb("fb1c", [128, F1P // 128], F32); bb1 = Buf("fb1c")
    b2c = k.sb("fb2c", [128, DC], F32); bb2 = Buf("fb2c")
    lngc = k.sb("flng", [128, DC], F32); blg = Buf("flng")
    lnbc = k.sb("flnb", [128, DC], F32); blb = Buf("flnb")
    qsc = k.sb("fqsc", [128, 1], F32); bqsc = Buf("fqsc")
    ksc = k.sb("fksc", [128, 1], F32); bksc = Buf("fksc")
    qsr = k.sb("fqsr", [1, 128], F32); bqsr = Buf("fqsr")

    k.push_scope()
    tmpc = [k.sb(f"ftmp_{i}", [128, TOK], F32) for i in range(2)]
    btmp = [Buf(f"ftmp_{i}") for i in range(2)]
    for ch in range(DC):
        sl = ch % 2
        P.op("y", lambda y, ch=ch, sl=sl: y.dma_start(
                out=tmpc[sl][:, :], in_=A["xn1fm_d"][128 * ch:128 * (ch + 1), :]),
             writes=[btmp[sl]], dma=lfin)
        P.op("y", lambda y, ch=ch, sl=sl: y.dma_start(
                out=sd[128 * ch:128 * (ch + 1), SDH:SDH + TOK], in_=tmpc[sl][:, :]),
             reads=[btmp[sl]], writes=[bsd], dma=swr)
    k.pop_scope(fence=False)

    for li in range(_L1F_DEPTH):
        k.push_scope()
        qs_bc = k.ps("fqsbc", [128, 128], F32); bqsb = Buf("fqsbc")
        P.op("y", lambda y, li=li: y.dma_start(out=qsc[:, :], in_=A["f_qs_c"][li]),
             writes=[bqsc], dma=ld)
        P.op("y", lambda y, li=li: y.dma_start(out=ksc[:, :], in_=A["f_ks_c"][li]),
             writes=[bksc], dma=ld)
        P.op("v", lambda v: v.tensor_scalar_mul(pq[:, :], permf[:, :], qsc[:, 0:1]),
             reads=[bpm, bqsc], writes=[bpq])
        P.op("v", lambda v: v.tensor_scalar_mul(pk[:, :], permf[:, :], ksc[:, 0:1]),
             reads=[bpm, bksc], writes=[bpk])
        P.op("y", lambda y, li=li: y.dma_start(out=qsr[:, :], in_=A["f_ks"][li]),
             writes=[bqsr], dma=ld)
        P.op("t", lambda t: t.matmul(qs_bc[0:16, :], onesr[0:1, 0:16], qsr[:, :],
                                     start=True, stop=True),
             reads=[bor, bqsr], writes=[bqsb])
        P.op("v", lambda v: v.tensor_tensor(
                hbw[:, :].rearrange("h (c n) -> h c n", n=128),
                hbcf[:, :].rearrange("h (c n) -> h c n", n=128),
                qs_bc[0:16, :].unsqueeze(1).broadcast_to([16, DC, 128]), ALU.mult),
             reads=[bhb, bqsb], writes=[bhw])
        P.op("y", lambda y, li=li: y.dma_start(out=b1c[:, :], in_=A["f_b1p"][li]),
             writes=[bb1], dma=ld)
        P.op("y", lambda y, li=li: y.dma_start(out=b2c[:, :], in_=A["f_b2"][li]),
             writes=[bb2], dma=ld)
        P.op("y", lambda y, li=li: y.dma_start(out=lngc[:, :], in_=A["f_lng"][li]),
             writes=[blg], dma=ld)
        P.op("y", lambda y, li=li: y.dma_start(out=lnbc[:, :], in_=A["f_lnb"][li]),
             writes=[blb], dma=ld)
        k.pop_scope()

        k.push_scope()
        hb = [k.sb(f"fhb_{i}", [128, SDH], F32) for i in range(2)]
        bhbb = [Buf(f"fhb_{i}") for i in range(2)]
        for ch in range(DC):
            sl = ch % 2
            P.op("y", lambda y, ch=ch, sl=sl: y.dma_start(
                    out=hb[sl][:, :], in_=sd[128 * ch:128 * (ch + 1), TOK:TOK + SDH]),
                 reads=[bsd], writes=[bhbb[sl]], dma=lhb)
            P.op("y", lambda y, ch=ch, sl=sl: y.dma_start(
                    out=A["hal_in"][128 * ch:128 * (ch + 1), :], in_=hb[sl][:, :]),
                 reads=[bhbb[sl]], writes=[bhal], dma=shal)
        P.op("g", lambda g: g.collective_compute("AllGather", ALU.bypass,
                                                 replica_groups=GRP_BATCH,
                                                 ins=[A["hal_in"]], outs=[A["hal_ag"]]),
             reads=[bhal], writes=[bhag], dma=cch)
        for ch in range(DC):
            sl = ch % 2
            P.op("v", lambda v, sl=sl: v.memset(hb[sl][:, :], 0.0), writes=[bhbb[sl]])
            P.op("g", lambda g, ch=ch, sl=sl: g.indirect_dma_start(
                    out=hb[sl][:, :], out_offset=None,
                    in_=A["hal_ag"],
                    in_offset=IndirectOffsetOnAxis(ap=hrowu[:, ch:ch + 1], axis=0),
                    bounds_check=k.breg(g, 4 * D - 1), oob_is_err=False),
                 reads=[bhru, bhag, bhbb[sl]], writes=[bhbb[sl]], dma=ghal)
            P.op("v", lambda v, sl=sl: v.tensor_scalar_mul(hb[sl][:, :], hb[sl][:, :],
                                                           hgate[:, 0:1]),
                 reads=[bhg, bhbb[sl]], writes=[bhbb[sl]])
            P.op("y", lambda y, ch=ch, sl=sl: y.dma_start(
                    out=sd[128 * ch:128 * (ch + 1), 0:SDH], in_=hb[sl][:, :]),
                 reads=[bhbb[sl]], writes=[bsd], dma=swr)
        k.pop_scope()

        k.push_scope()
        wq = [k.sb(f"fwq_{i}", [128, 3072], BF16) for i in range(DC)]
        bwq = [Buf(f"fwq_{i}") for i in range(DC)]
        for i in range(DC):
            P.op("g", lambda g, i=i, li=li: g.dma_start(
                    out=wq[i][:, :], in_=A["f_wqkv"][li, 128 * i:128 * (i + 1), :]),
                 writes=[bwq[i]], dma=lwq)
        wo = [k.sb(f"fwo_{i}", [128, D], BF16) for i in range(DC)]
        bwo = [Buf(f"fwo_{i}") for i in range(DC)]
        for i in range(DC):
            P.op("g", lambda g, i=i, li=li: g.dma_start(
                    out=wo[i][:, :], in_=A["f_wo"][li, 128 * i:128 * (i + 1), :]),
                 writes=[bwo[i]], dma=lwo)

        xshw = [k.sb(f"fxsh_{i}", [128, WIN], BF16) for i in range(DC)]
        bxw = [Buf(f"fxsh_{i}") for i in range(DC)]
        qsb = [[k.sb(f"fqsb_{p}_{i}", [128, WIN], BF16) for i in range(DC)]
               for p in range(2)]
        bqs = [[Buf(f"fqsb_{p}_{i}") for i in range(DC)] for p in range(2)]
        ksl = [[k.sb(f"fksl_{p}_{i}", [128, WIN], BF16) for i in range(DC)]
               for p in range(3)]
        bks = [[Buf(f"fksl_{p}_{i}") for i in range(DC)] for p in range(3)]
        vsl = [[k.sb(f"fvsl_{p}_{i}", [128, D], BF16) for i in range(2)]
               for p in range(3)]
        bvs = [[Buf(f"fvsl_{p}_{i}") for i in range(2)] for p in range(3)]
        tabs = k.sb("ftabs", [128, 4, WIN], F32); btab = Buf("ftabs")
        rot1 = k.sb("frot1", [128, WIN], F32); br1 = Buf("frot1")
        rot3 = k.sb("frot3", [128, WIN], F32); br3 = Buf("frot3")
        sqsc = k.sb("fsqsc", [128, WIN], F32); bsqc = Buf("fsqsc")
        nrm = k.sb("fnrm", [16, WIN], F32); bnr = Buf("fnrm")
        rkf = k.sb("frkf", [16, WIN], F32); brk = Buf("frkf")
        rqt = [k.sb(f"frqt_{p}", [128, 2, 16], F32) for p in range(2)]
        brq = [Buf(f"frqt_{p}") for p in range(2)]
        stt = k.sb("fstt", [128, 8], F32); bstt = Buf("fstt")
        Pw = k.sb("fPw", [128, 512], BF16); bPw = Buf("fPw")
        PTs = k.sb("fPTs", [128, 512], BF16); bPT = Buf("fPTs")
        aosb = k.sb("faosb", [128, D], BF16); bao = Buf("faosb")
        aoT = [k.sb(f"faoT_{i}", [128, 128], BF16) for i in range(DC)]
        baoT = [Buf(f"faoT_{i}") for i in range(DC)]
        res = [k.sb(f"fres_{i}", [128, WIN], F32) for i in range(DC)]
        bres = [Buf(f"fres_{i}") for i in range(DC)]

        pj_ps = [k.ps(f"fpj_{i}", [128, 512], F32) for i in range(2)]
        bpj = [Buf(f"fpj_{i}") for i in range(2)]
        rh_ps = k.ps("frh", [128, WIN], F32); brh = Buf("frh")
        n2_ps = rh_ps; bn2 = brh
        rb_ps = rh_ps; brb = brh
        s_ps = k.ps("fs", [128, 512], F32); bS = Buf("fs")
        pt_ps = k.ps("fpt", [128, 512], BF16); bpt = Buf("fpt")
        av_ps = k.ps("fav", [128, D], F32); bav = Buf("fav")

        def load_xshw(w):
            for ch in range(DC):
                c0 = SDH + WIN * w - (1 if ch >= 4 else 0)
                P.op("g", lambda g, ch=ch, c0=c0: g.dma_start(
                        out=xshw[ch][:, :], in_=sd[128 * ch:128 * (ch + 1), c0:c0 + WIN]),
                     reads=[bsd], writes=[bxw[ch]], dma=lsd)

        def load_tabs(w):
            c0 = SDH + WIN * w
            for a, nm in enumerate(["rot_cs", "rot_ss", "rot_ci", "rot_si"]):
                P.op("y", lambda y, c0=c0, a=a, nm=nm: y.dma_start(
                        out=tabs[:, a, :], in_=A[nm][:, c0:c0 + WIN]),
                     writes=[btab], dma=ltb)

        def qkv_part(dst, bdst, col0):
            for m in range(DC):
                sl = m % 2
                for kk in range(DC):
                    P.op("t", lambda t, m=m, kk=kk, sl=sl, col0=col0: t.matmul(
                            pj_ps[sl][:, 0:WIN],
                            wq[kk][:, col0 + 128 * m:col0 + 128 * (m + 1)],
                            xshw[kk][:, :], start=(kk == 0), stop=(kk == DC - 1)),
                         reads=[bwq[kk], bxw[kk]], writes=[bpj[sl]], quiet=(kk != DC - 1))
                P.op("v", lambda v, m=m, sl=sl: v.tensor_copy(dst[m][:, :],
                                                              pj_ps[sl][:, 0:WIN]),
                     reads=[bpj[sl]], writes=[bdst[m]])

        def rotary(dst, bdst, permb, bpermb, scol, bscol, ic, isn, is_k, rqslot):
            for ch in range(DC):
                P.op("s", lambda s, ch=ch: s.activation(sqsc[:, :], dst[ch][:, :], AF.Square),
                     reads=[bdst[ch]], writes=[bsqc])
                P.op("t", lambda t, ch=ch: t.matmul(
                        n2_ps[0:16, :], hself[:, 16 * ch:16 * (ch + 1)], sqsc[:, :],
                        start=(ch == 0), stop=(ch == DC - 1)),
                     reads=[bhs, bsqc], writes=[bn2])
            P.op("s", lambda s: s.activation(nrm[:, :], n2_ps[0:16, :], AF.Sqrt),
                 reads=[bn2], writes=[bnr])
            P.op("v", lambda v: v.tensor_scalar_max(nrm[:, :], nrm[:, :], 1e-12),
                 reads=[bnr], writes=[bnr])
            if is_k:
                P.op("v", lambda v: v.reciprocal(rkf[:, :], nrm[:, :]),
                     reads=[bnr], writes=[brk])
            else:
                for q2 in range(2):
                    P.op("t", lambda t, q2=q2: t.transpose(
                            rb_ps[:, 0:16], nrm[0:16, 128 * q2:128 * (q2 + 1)],
                            identf[0:16, 0:16]),
                         reads=[bnr, bIf], writes=[brb])
                    P.op("v", lambda v, q2=q2, rqslot=rqslot: v.tensor_copy(
                            rqt[rqslot][:, q2, :], rb_ps[:, 0:16]),
                         reads=[brb], writes=[brq[rqslot]])
                P.op("v", lambda v, rqslot=rqslot: v.reciprocal(
                        rqt[rqslot][:, :, :], rqt[rqslot][:, :, :]),
                     reads=[brq[rqslot]], writes=[brq[rqslot]])
                P.op("v", lambda v, rqslot=rqslot: v.tensor_scalar_mul(
                        rqt[rqslot][:, :, :], rqt[rqslot][:, :, :], 8.0),
                     reads=[brq[rqslot]], writes=[brq[rqslot]])
                P.op("v", lambda v, rqslot=rqslot: v.tensor_scalar_min(
                        rqt[rqslot][:, :, :], rqt[rqslot][:, :, :], 1e6),
                     reads=[brq[rqslot]], writes=[brq[rqslot]])
            for ch in range(DC):
                P.op("t", lambda t, ch=ch: t.matmul(rh_ps[:, :], permb[:, :], dst[ch][:, :],
                                                    start=True, stop=True),
                     reads=[bpermb, bdst[ch]], writes=[brh])
                P.op("v", lambda v, ch=ch, ic=ic: v.scalar_tensor_tensor(
                        rot1[:, :], dst[ch][:, :], scol[:, 0:1], tabs[:, ic, :],
                        ALU.mult, ALU.mult),
                     reads=[bdst[ch], bscol, btab], writes=[br1])
                P.op("v", lambda v, ch=ch, isn=isn: v.tensor_tensor(
                        rot3[:, :], rh_ps[:, :], tabs[:, isn, :], ALU.mult),
                     reads=[brh, btab], writes=[br3])
                if is_k:
                    P.op("v", lambda v: v.tensor_tensor(rot1[:, :], rot1[:, :],
                                                        rot3[:, :], ALU.add),
                         reads=[br3, br1], writes=[br1])
                    P.op("t", lambda t, ch=ch: t.matmul(
                            rb_ps[:, :], hbw[0:16, 128 * ch:128 * (ch + 1)], rkf[:, :],
                            start=True, stop=True),
                         reads=[bhw, brk], writes=[brb])
                    P.op("v", lambda v, ch=ch: v.tensor_tensor(
                            dst[ch][:, :], rot1[:, :], rb_ps[:, :], ALU.mult),
                         reads=[br1, brb, bdst[ch]], writes=[bdst[ch]])
                else:
                    P.op("v", lambda v, ch=ch: v.tensor_tensor(
                            dst[ch][:, :], rot1[:, :], rot3[:, :], ALU.add),
                         reads=[br3, br1, bdst[ch]], writes=[bdst[ch]])

        def stage(w):
            load_xshw(w)
            load_tabs(w)
            p3, p2 = w % 3, w % 2
            qkv_part(ksl[p3], bks[p3], D)
            rotary(ksl[p3], bks[p3], pk, bpk, ksc, bksc, 2, 3, True, None)
            if w >= -1:
                qkv_part(qsb[p2], bqs[p2], 0)
                rotary(qsb[p2], bqs[p2], pq, bpq, qsc, bqsc, 0, 1, False, p2)
            for t2 in range(2):
                sl = t2 % 2
                for nn in range(2):
                    for kk in range(DC):
                        P.op("t", lambda t, t2=t2, nn=nn, kk=kk, sl=sl: t.matmul(
                                pj_ps[sl][:, :],
                                xshw[kk][:, 128 * t2:128 * (t2 + 1)],
                                wq[kk][:, 2 * D + 512 * nn:2 * D + 512 * (nn + 1)],
                                start=(kk == 0), stop=(kk == DC - 1)),
                             reads=[bxw[kk], bwq[kk]], writes=[bpj[sl]],
                             quiet=(kk != DC - 1))
                    P.op("v", lambda v, t2=t2, nn=nn, sl=sl, p3=p3: v.tensor_copy(
                            vsl[p3][t2][:, 512 * nn:512 * (nn + 1)], pj_ps[sl][:, :]),
                         reads=[bpj[sl]], writes=[bvs[p3][t2]])

        def attn_tile(w, qt2, mski, extra):
            parp, parc = (w - 1) % 3, w % 3
            p2 = w % 2
            for h in range(HEADS1):
                ch, hh = h // 2, h % 2
                r0 = 64 * hh
                P.op("t", lambda t, ch=ch, r0=r0, qt2=qt2, p2=p2, parp=parp: t.matmul(
                        s_ps[:, 0:WIN],
                        qsb[p2][ch][r0:r0 + 64, 128 * qt2:128 * (qt2 + 1)],
                        ksl[parp][ch][r0:r0 + 64, :], start=True, stop=False),
                     reads=[bqs[p2][ch], bks[parp][ch]], writes=[bS])
                P.op("t", lambda t, ch=ch, r0=r0, qt2=qt2, p2=p2, parc=parc: t.matmul(
                        s_ps[:, WIN:2 * WIN],
                        qsb[p2][ch][r0:r0 + 64, 128 * qt2:128 * (qt2 + 1)],
                        ksl[parc][ch][r0:r0 + 64, :], start=False, stop=False),
                     reads=[bqs[p2][ch], bks[parc][ch]], writes=[bS])
                P.op("t", lambda t, mski=mski: t.matmul(
                        s_ps[:, :], identb[:, :], msk[:, mski, :],
                        start=False, stop=True),
                     reads=[bIb, bmk], writes=[bS])
                P.op("v", lambda v: v.reduce_max(stt[:, 0:1], s_ps[:, :], axis=AX.X),
                     reads=[bS], writes=[bstt])
                P.op("v", lambda v, h=h, qt2=qt2, p2=p2: v.scalar_tensor_tensor(
                        stt[:, 1:2], rqt[p2][:, qt2, h:h + 1], -1.0, stt[:, 0:1],
                        ALU.mult, ALU.mult),
                     reads=[brq[p2], bstt], writes=[bstt])
                P.op("s", lambda s, h=h, qt2=qt2, p2=p2: s.activation(
                        Pw[:, :], s_ps[:, :], AF.Exp, bias=stt[:, 1:2],
                        scale=rqt[p2][:, qt2, h:h + 1], accum_out=stt[:, 2:3]),
                     reads=[bS, bstt, brq[p2]], writes=[bPw, bstt])
                P.op("v", lambda v: v.reciprocal(stt[:, 3:4], stt[:, 2:3]),
                     reads=[bstt], writes=[bstt])
                P.op("v", lambda v: v.tensor_scalar_mul(Pw[:, :], Pw[:, :], stt[:, 3:4]),
                     reads=[bstt, bPw], writes=[bPw])
                for u in range(4):
                    P.op("t", lambda t, u=u: t.transpose(
                            pt_ps[:, 128 * u:128 * (u + 1)],
                            Pw[:, 128 * u:128 * (u + 1)], identb[:, :]),
                         reads=[bPw, bIb], writes=[bpt], quiet=(u != 3))
                P.op("v", lambda v: v.tensor_copy(PTs[:, :], pt_ps[:, :]),
                     reads=[bpt], writes=[bPT])
                for u in range(4):
                    par = parp if u < 2 else parc
                    t2 = u % 2
                    P.op("t", lambda t, u=u, h=h, par=par, t2=t2: t.matmul(
                            av_ps[:, 64 * h:64 * (h + 1)], PTs[:, 128 * u:128 * (u + 1)],
                            vsl[par][t2][:, 64 * h:64 * (h + 1)],
                            start=(h % 8 == 0 and u == 0), stop=(h % 8 == 7 and u == 3)),
                         reads=[bPT, bvs[par][t2]], writes=[bav],
                         quiet=not (h == HEADS1 - 1 and u == 3))
            P.op("v", lambda v: v.tensor_copy(aosb[:, :], av_ps[:, :]),
                 reads=[bav], writes=[bao])
            for ch in range(DC):
                P.op("t", lambda t, ch=ch: t.transpose(
                        pt_ps[:, 0:128], aosb[:, 128 * ch:128 * (ch + 1)], identb[:, :]),
                     reads=[bao, bIb], writes=[bpt])
                P.op("v", lambda v, ch=ch: v.tensor_copy(aoT[ch][:, :], pt_ps[:, 0:128]),
                     reads=[bpt], writes=[baoT[ch]])
            for m in range(DC):
                for kk in range(DC):
                    P.op("t", lambda t, m=m, kk=kk: t.matmul(
                            pj_ps[0][:, 0:128], wo[kk][:, 128 * m:128 * (m + 1)],
                            aoT[kk][:, :], start=(kk == 0), stop=(kk == DC - 1)),
                         reads=[bwo[kk], baoT[kk]], writes=[bpj[0]], quiet=(kk != DC - 1))
                if not extra:
                    c0 = SDH + WIN * w + 128 * qt2
                    P.op("y", lambda y, m=m, c0=c0: y.dma_start(
                            out=res[m][:, 0:128], in_=sd[128 * m:128 * (m + 1), c0:c0 + 128]),
                         reads=[bsd], writes=[bres[m]], dma=lres)
                    P.op("v", lambda v, m=m: v.tensor_tensor(
                            res[m][:, 0:128], res[m][:, 0:128], pj_ps[0][:, 0:128], ALU.add),
                         reads=[bpj[0], bres[m]], writes=[bres[m]])
                    P.op("y", lambda y, m=m, c0=c0: y.dma_start(
                            out=sd[128 * m:128 * (m + 1), c0:c0 + 128], in_=res[m][:, 0:128]),
                         reads=[bres[m]], writes=[bsd], dma=swr)
                else:
                    P.op("y", lambda y, m=m: y.dma_start(
                            out=res[m][:, 0:1], in_=sd[128 * m:128 * (m + 1), SDH - 1:SDH],
                            allow_slow_non_contiguous=True),
                         reads=[bsd], writes=[bres[m]], dma=lres)
                    P.op("v", lambda v, m=m: v.tensor_tensor(
                            res[m][:, 0:1], res[m][:, 0:1], pj_ps[0][:, 127:128], ALU.add),
                         reads=[bpj[0], bres[m]], writes=[bres[m]])
                    P.op("y", lambda y, m=m: y.dma_start(
                            out=sd[128 * m:128 * (m + 1), SDH - 1:SDH], in_=res[m][:, 0:1],
                            allow_slow_non_contiguous=True),
                         reads=[bres[m]], writes=[bsd], dma=swr)

        stage(-2)
        stage(-1)
        stage(0)
        attn_tile(-1, 1, 4, True)
        for w in range(NWIN):
            if w + 1 < NWIN:
                stage(w + 1)
            mski0 = 0 if w == 0 else 2
            attn_tile(w, 0, mski0, False)
            attn_tile(w, 1, mski0 + 1, False)
        k.pop_scope()

        k.push_scope()
        xnb = [k.sb(f"fxnb_{i}", [128, TOK], BF16) for i in range(DC)]
        bxnb = [Buf(f"fxnb_{i}") for i in range(DC)]
        aG = k.sb("faG", [128, (F2P // 128) * TOK], BF16); baG = Buf("faG")
        xsh = [k.sb(f"fxshf_{i}", [128, 512], BF16) for i in range(2)]
        bxsh = [Buf(f"fxshf_{i}") for i in range(2)]
        x2t = k.sb("fx2t", [128, 512], BF16); bx2 = Buf("fx2t")
        srow = k.sb("fsrow", [1, 4, TOK], F32); bsr = Buf("fsrow")
        ab_row = k.sb("fabrow", [1, 2, TOK], F32); babr = Buf("fabrow")
        w1b = [k.sb(f"fw1b_{i}", [128, DC, 256], BF16) for i in range(2)]
        bw1b = [Buf(f"fw1b_{i}") for i in range(2)]
        w2b = [k.sb(f"fw2b_{i}", [128, D], BF16) for i in range(2)]
        bw2b = [Buf(f"fw2b_{i}") for i in range(2)]
        glb = k.sb("fglb", [128, 512], BF16); bgl = Buf("fglb")
        ffr = [k.sb(f"ffr_{i}", [128, 512], F32) for i in range(2)]
        bffr = [Buf(f"ffr_{i}") for i in range(2)]

        onescb = k.sb("fonescb", [128, 1], BF16); bocb = Buf("fonescb")
        P.op("v", lambda v: v.tensor_copy(onescb[:, :], onesc[:, :]),
             reads=[boc], writes=[bocb])
        k.push_scope()
        st_ps = k.ps("fst", [1, 512], F32); bst2 = Buf("fst")
        sq_ps = k.ps("fsq", [1, 512], F32); bsq2 = Buf("fsq")
        for t4 in range(4):
            c0 = SDH + 512 * t4
            for ch in range(DC):
                sl = ch % 2
                cs = c0 - (1 if ch >= 4 else 0)
                P.op("g", lambda g, ch=ch, cs=cs, sl=sl: g.dma_start(
                        out=xsh[sl][:, :], in_=sd[128 * ch:128 * (ch + 1), cs:cs + 512]),
                     reads=[bsd], writes=[bxsh[sl]], dma=lsd)
                P.op("v", lambda v, sl=sl: v.tensor_tensor(
                        x2t[:, :], xsh[sl][:, :], xsh[sl][:, :], ALU.mult),
                     reads=[bxsh[sl]], writes=[bx2])
                P.op("t", lambda t, ch=ch, sl=sl: t.matmul(
                        st_ps[0:1, :], onescb[:, :], xsh[sl][:, :],
                        start=(ch == 0), stop=(ch == DC - 1)),
                     reads=[bocb, bxsh[sl]], writes=[bst2])
                P.op("t", lambda t, ch=ch: t.matmul(
                        sq_ps[0:1, :], onescb[:, :], x2t[:, :],
                        start=(ch == 0), stop=(ch == DC - 1)),
                     reads=[bocb, bx2], writes=[bsq2])
                P.op("v", lambda v, ch=ch, t4=t4, sl=sl: v.tensor_copy(
                        xnb[ch][:, 512 * t4:512 * (t4 + 1)], xsh[sl][:, :]),
                     reads=[bxsh[sl]], writes=[bxnb[ch]])
            P.op("v", lambda v, t4=t4: v.tensor_scalar_mul(
                    srow[0:1, 0, 512 * t4:512 * (t4 + 1)], st_ps[0:1, :], 1.0 / D),
                 reads=[bst2], writes=[bsr])
            P.op("v", lambda v, t4=t4: v.tensor_copy(
                    srow[0:1, 1, 512 * t4:512 * (t4 + 1)], sq_ps[0:1, :]),
                 reads=[bsq2], writes=[bsr])
        P.op("v", lambda v: v.tensor_tensor(srow[0:1, 2, :], srow[0:1, 0, :],
                                            srow[0:1, 0, :], ALU.mult),
             reads=[bsr], writes=[bsr])
        P.op("v", lambda v: v.scalar_tensor_tensor(srow[0:1, 2, :], srow[0:1, 1, :],
                                                   1.0 / D, srow[0:1, 2, :],
                                                   ALU.mult, ALU.subtract),
             reads=[bsr], writes=[bsr])
        P.op("v", lambda v: v.tensor_scalar_add(srow[0:1, 2, :], srow[0:1, 2, :], 1e-5),
             reads=[bsr], writes=[bsr])
        P.op("s", lambda s: s.activation(srow[0:1, 2, :], srow[0:1, 2, :], AF.Sqrt),
             reads=[bsr], writes=[bsr])
        P.op("v", lambda v: v.reciprocal(ab_row[0:1, 0, :], srow[0:1, 2, :]),
             reads=[bsr], writes=[babr])
        P.op("v", lambda v: v.scalar_tensor_tensor(ab_row[0:1, 1, :], srow[0:1, 0, :],
                                                   -1.0, ab_row[0:1, 0, :],
                                                   ALU.mult, ALU.mult),
             reads=[babr, bsr], writes=[babr])
        abc_ps = k.ps("fabc", [128, 512], F32); babc = Buf("fabc")
        bbc_ps = k.ps("fbbc", [128, 512], F32); bbbc = Buf("fbbc")
        for t4 in range(4):
            P.op("t", lambda t, t4=t4: t.matmul(
                    abc_ps[:, :], onesr[:, :], ab_row[0:1, 0, 512 * t4:512 * (t4 + 1)],
                    start=True, stop=True),
                 reads=[bor, babr], writes=[babc])
            P.op("t", lambda t, t4=t4: t.matmul(
                    bbc_ps[:, :], onesr[:, :], ab_row[0:1, 1, 512 * t4:512 * (t4 + 1)],
                    start=True, stop=True),
                 reads=[bor, babr], writes=[bbbc])
            for ch in range(DC):
                P.op("v", lambda v, ch=ch, t4=t4: v.tensor_tensor(
                        xnb[ch][:, 512 * t4:512 * (t4 + 1)],
                        xnb[ch][:, 512 * t4:512 * (t4 + 1)], abc_ps[:, :], ALU.mult),
                     reads=[babc, bxnb[ch]], writes=[bxnb[ch]])
                P.op("v", lambda v, ch=ch, t4=t4: v.tensor_tensor(
                        xnb[ch][:, 512 * t4:512 * (t4 + 1)],
                        xnb[ch][:, 512 * t4:512 * (t4 + 1)], bbc_ps[:, :], ALU.add),
                     reads=[bbbc, bxnb[ch]], writes=[bxnb[ch]])
                P.op("v", lambda v, ch=ch, t4=t4: v.tensor_scalar(
                        xnb[ch][:, 512 * t4:512 * (t4 + 1)],
                        xnb[ch][:, 512 * t4:512 * (t4 + 1)],
                        lngc[:, ch:ch + 1], lnbc[:, ch:ch + 1], ALU.mult, ALU.add),
                     reads=[blg, blb, bxnb[ch]], writes=[bxnb[ch]])

        k.pop_scope()

        k.push_scope()
        h_ps = [k.ps(f"fh_{i}", [128, 512], F32) for i in range(2)]
        bh2 = [Buf(f"fh_{i}") for i in range(2)]
        for fp in range(F1P // 256):
            sl = fp % 2
            P.op("g", lambda g, fp=fp, sl=sl, li=li: g.dma_start(
                    out=w1b[sl][:, :, :],
                    in_=A["f_w1p"][li][:, 256 * fp:256 * (fp + 1)].rearrange(
                        "(c p) f -> p c f", p=128)),
                 writes=[bw1b[sl]], dma=lw1)
            for nn in range(4):
                for kk in range(DC):
                    P.op("t", lambda t, nn=nn, kk=kk, sl=sl: t.matmul(
                            h_ps[0][:, :], w1b[sl][:, kk, 0:128],
                            xnb[kk][:, 512 * nn:512 * (nn + 1)],
                            start=(kk == 0), stop=(kk == DC - 1)),
                         reads=[bw1b[sl], bxnb[kk]], writes=[bh2[0]],
                         quiet=(kk != DC - 1))
                    P.op("t", lambda t, nn=nn, kk=kk, sl=sl: t.matmul(
                            h_ps[1][:, :], w1b[sl][:, kk, 128:256],
                            xnb[kk][:, 512 * nn:512 * (nn + 1)],
                            start=(kk == 0), stop=(kk == DC - 1)),
                         reads=[bw1b[sl], bxnb[kk]], writes=[bh2[1]],
                         quiet=(kk != DC - 1))
                P.op("s", lambda s, fp=fp: s.activation(
                        glb[:, :], h_ps[1][:, :], AF.Gelu,
                        bias=b1c[:, 2 * fp + 1:2 * fp + 2]),
                     reads=[bh2[1], bb1], writes=[bgl])
                P.op("v", lambda v, fp=fp, nn=nn: v.scalar_tensor_tensor(
                        aG[:, TOK * fp + 512 * nn:TOK * fp + 512 * (nn + 1)],
                        h_ps[0][:, :], b1c[:, 2 * fp:2 * fp + 1], glb[:, :],
                        ALU.add, ALU.mult),
                     reads=[bh2[0], bb1, bgl], writes=[baG])

        k.pop_scope()

        k.push_scope()
        o_ps = [k.ps(f"fo_{i}", [128, 512], F32) for i in range(DC)]
        bo2 = [Buf(f"fo_{i}") for i in range(DC)]
        NKF = F2P // 128
        for nn in range(4):
            for kf in range(NKF):
                sl = kf % 2
                P.op("g", lambda g, kf=kf, sl=sl, li=li: g.dma_start(
                        out=w2b[sl][:, :], in_=A["f_w2p"][li, 128 * kf:128 * (kf + 1), :]),
                     writes=[bw2b[sl]], dma=lw2)
                for m in range(DC):
                    P.op("t", lambda t, kf=kf, m=m, nn=nn, sl=sl: t.matmul(
                            o_ps[m][:, :], w2b[sl][:, 128 * m:128 * (m + 1)],
                            aG[:, TOK * kf + 512 * nn:TOK * kf + 512 * (nn + 1)],
                            start=(kf == 0), stop=(kf == NKF - 1)),
                         reads=[bw2b[sl], baG], writes=[bo2[m]],
                         quiet=(m != DC - 1))
            for m in range(DC):
                sl = m % 2
                c0 = SDH + 512 * nn
                P.op("y", lambda y, m=m, c0=c0, sl=sl: y.dma_start(
                        out=ffr[sl][:, :], in_=sd[128 * m:128 * (m + 1), c0:c0 + 512]),
                     reads=[bsd], writes=[bffr[sl]], dma=lres)
                P.op("v", lambda v, m=m, sl=sl: v.scalar_tensor_tensor(
                        ffr[sl][:, :], o_ps[m][:, :], 1.0, ffr[sl][:, :],
                        ALU.mult, ALU.add),
                     reads=[bo2[m], bffr[sl]], writes=[bffr[sl]])
                P.op("v", lambda v, m=m, sl=sl: v.tensor_scalar_add(
                        ffr[sl][:, :], ffr[sl][:, :], b2c[:, m:m + 1]),
                     reads=[bb2, bffr[sl]], writes=[bffr[sl]])
                P.op("y", lambda y, m=m, c0=c0, sl=sl: y.dma_start(
                        out=sd[128 * m:128 * (m + 1), c0:c0 + 512], in_=ffr[sl][:, :]),
                     reads=[bffr[sl]], writes=[bsd], dma=swr)
        k.pop_scope()
        k.pop_scope()

    k.push_scope()
    dch = [k.sb(f"fdch_{i}", [128, TOK], F32) for i in range(DC)]
    bdch = [Buf(f"fdch_{i}") for i in range(DC)]
    pch = k.sb("fpch", [128, TOK], F32); bpch = Buf("fpch")
    ytl = [k.sb(f"fytl_{i}", [128, D], F32) for i in range(2)]
    byt = [Buf(f"fytl_{i}") for i in range(2)]
    x1t = k.sb("fx1t", [128, D], F32); bx1t = Buf("fx1t")
    tr_pf = k.ps("ftrf", [128, 512], F32); btrf = Buf("ftrf")
    for ch in range(DC):
        P.op("y", lambda y, ch=ch: y.dma_start(
                out=dch[ch][:, :], in_=sd[128 * ch:128 * (ch + 1), SDH:SDH + TOK]),
             reads=[bsd], writes=[bdch[ch]], dma=lfin)
        P.op("y", lambda y, ch=ch: y.dma_start(
                out=pch[:, :], in_=A["part_d"][128 * ch:128 * (ch + 1), :]),
             writes=[bpch], dma=lfin)
        P.op("v", lambda v, ch=ch: v.tensor_tensor(dch[ch][:, :], dch[ch][:, :],
                                                   pch[:, :], ALU.add),
             reads=[bpch, bdch[ch]], writes=[bdch[ch]])
    for t in range(NT):
        sl = t % 2
        P.op("y", lambda y, t=t: y.dma_start(
                out=x1t[:, :], in_=A["x1_d"][128 * t:128 * (t + 1), :]),
             writes=[bx1t], dma=lfin)
        for ch in range(DC):
            q = ch % 4
            P.op("t", lambda te, ch=ch, t=t, q=q: te.transpose(
                    tr_pf[:, 128 * q:128 * (q + 1)],
                    dch[ch][:, 128 * t:128 * (t + 1)], identf[:, :]),
                 reads=[bdch[ch], bIf], writes=[btrf])
            P.op("v", lambda v, ch=ch, q=q, sl=sl: v.tensor_copy(
                    ytl[sl][:, 128 * ch:128 * (ch + 1)], tr_pf[:, 128 * q:128 * (q + 1)]),
                 reads=[btrf], writes=[byt[sl]])
        P.op("v", lambda v, sl=sl: v.tensor_tensor(ytl[sl][:, :], ytl[sl][:, :],
                                                   x1t[:, :], ALU.add),
             reads=[bx1t, byt[sl]], writes=[byt[sl]])
        P.op("y", lambda y, t=t, sl=sl: y.dma_start(
                out=A["y"][128 * t:128 * (t + 1), :], in_=ytl[sl][:, :]),
             reads=[byt[sl]], dma=sy)
    k.pop_scope(fence=False)
    P.raw_wait("y", (sy, sy.count))


_LAST_PLAN = None


def build_launch_l1f():
    global _LAST_PLAN
    nc = bass.Bass("TRN2", target_bir_lowering=False)
    P = Plan()
    _LAST_PLAN = P
    ctx = contextlib.ExitStack()
    k = K(nc, ctx, P)
    A = {}
    A["xn1fm_d"] = k.inp("xn1fm_d", [D, TOK])
    A["part_d"] = k.inp("part_d", [D, TOK])
    A["x1_d"] = k.inp("x1_d", [TOK, D])
    A["ident"] = k.inp("ident", [128, 128])
    A["ones_row"] = k.inp("ones_row", [1, 128])
    A["ones_col"] = k.inp("ones_col", [128, 1])
    A["perm"] = k.inp("perm", [128, 128])
    A["hsel"] = k.inp("hsel", [128, DC * 16])
    A["hbc"] = k.inp("hbc", [16, DC * 128])
    A["msk_w0"] = k.inp("msk_w0", [2, 128, 512])
    A["msk_int"] = k.inp("msk_int", [2, 128, 512])
    A["msk_ext"] = k.inp("msk_ext", [128, 512])
    A["halo_gate"] = k.inp("halo_gate", [128, 1])
    A["halo_rows"] = k.inp("halo_rows", [128, DC])
    A["rot_cs"] = k.inp("rot_cs", [128, SDH + TOK])
    A["rot_ss"] = k.inp("rot_ss", [128, SDH + TOK])
    A["rot_ci"] = k.inp("rot_ci", [128, SDH + TOK])
    A["rot_si"] = k.inp("rot_si", [128, SDH + TOK])
    A["f_wqkv"] = k.inp("f_wqkv", [DEPTH1, D, 3072])
    A["f_wo"] = k.inp("f_wo", [DEPTH1, D, D])
    A["f_w1p"] = k.inp("f_w1p", [DEPTH1, D, F1P])
    A["f_b1p"] = k.inp("f_b1p", [DEPTH1, 128, F1P // 128])
    A["f_w2p"] = k.inp("f_w2p", [DEPTH1, F2P, D])
    A["f_b2"] = k.inp("f_b2", [DEPTH1, 128, DC])
    A["f_lng"] = k.inp("f_lng", [DEPTH1, 128, DC])
    A["f_lnb"] = k.inp("f_lnb", [DEPTH1, 128, DC])
    A["f_qs"] = k.inp("f_qs", [DEPTH1, 1, 128])
    A["f_ks"] = k.inp("f_ks", [DEPTH1, 1, 128])
    A["f_qs_c"] = k.inp("f_qs_c", [DEPTH1, 128, 1])
    A["f_ks_c"] = k.inp("f_ks_c", [DEPTH1, 128, 1])
    A["y"] = k.outp("y", [TOK, D])
    A["sd"] = k.dram("sd", [D, SDH + TOK])
    A["hal_in"] = k.dram("hal_in", [D, SDH])
    A["hal_ag"] = k.dram("hal_ag", [4 * D, SDH])
    with ctx:
        emit_l1f(k, A)
        with nc.Block() as block:
            P.emit(nc, ctx, block)
    ins = ["xn1fm_d", "part_d", "x1_d", "ident", "ones_row", "ones_col", "perm",
           "hsel", "hbc", "msk_w0", "msk_int", "msk_ext", "halo_gate", "halo_rows",
           "rot_cs", "rot_ss", "rot_ci", "rot_si", "f_wqkv", "f_wo", "f_w1p",
           "f_b1p", "f_w2p", "f_b2", "f_lng", "f_lnb", "f_qs", "f_ks",
           "f_qs_c", "f_ks_c"]
    return nc, ins, ["y"]


GRP_BATCH = [[0, 1, 2, 3], [4, 5, 6, 7]]
GRP_ALL = [list(range(8))]

PHASE_MODE = "split"
_DEV_PHASES = None


def build_launch_l0():
    nc = bass.Bass("TRN2", target_bir_lowering=False)
    P = Plan()
    ctx = contextlib.ExitStack()
    k = K(nc, ctx, P)
    A = {}
    A["x_own"] = k.inp("x_own", [TOK, D])
    A["wqkvT0"] = k.inp("wqkvT0", [D, 3 * D])
    A["wg0"] = k.inp("wg0", [D, 16])
    A["ident"] = k.inp("ident", [128, 128])
    A["a0_d"] = k.outp("a0_d", [TOK, D])
    A["rt_d"] = k.outp("rt_d", [128, 32])
    A["kag_in"] = k.dram("kag_in", [D, TOK], BF16)
    A["kag"] = k.dram("kag", [4 * D, TOK], BF16)
    A["vag_in"] = k.dram("vag_in", [TOK, D], BF16)
    A["vag"] = k.dram("vag", [4 * TOK, D], BF16)
    with ctx:
        emit_l0(k, A, GRP_BATCH)
        with nc.Block() as block:
            P.emit(nc, ctx, block)
    return nc, ["x_own", "wqkvT0", "wg0", "ident"], ["a0_d", "rt_d"]


def build_launch_moe():
    nc = bass.Bass("TRN2", target_bir_lowering=False)
    P = Plan()
    ctx = contextlib.ExitStack()
    k = K(nc, ctx, P)
    A = {}
    A["rt_d"] = k.inp("rt_d", [128, 32])
    A["x_full"] = k.inp("x_full", [B * N, D])
    A["moe_w1"] = k.inp("moe_w1", [2, D, FF_MOE])
    A["moe_w2"] = k.inp("moe_w2", [2, FF_MOE, D])
    A["prefmask"] = k.inp("prefmask", [8, 1])
    A["gid1"] = k.inp("gid1", [128, NT])
    A["ident"] = k.inp("ident", [128, 128])
    A["ut"] = k.inp("ut", [128, 128])
    A["ones_col"] = k.inp("ones_col", [128, 1])
    A["ones_row"] = k.inp("ones_row", [1, 128])
    A["moe_d"] = k.outp("moe_d", [TOK, D])
    A["dbg_d"] = k.outp("dbg_d", [128, 3 * NT])
    A["eo_dbg"] = k.outp("eo_dbg", [TOK, D])
    A["cnt_in"] = k.dram("cnt_in", [1, 16])
    A["cnt_ag"] = k.dram("cnt_ag", [8, 16])
    A["tbl_d"] = k.dram("tbl_d", [B * N, 8])
    A["tbl_rs"] = k.dram("tbl_rs", [TOK, 8])
    A["eo_d"] = k.dram("eo_d", [TOK, D])
    A["eo_ag"] = k.dram("eo_ag", [B * N, D])
    with ctx:
        emit_moe(k, A)
        with nc.Block() as block:
            P.emit(nc, ctx, block)
    return nc, ["rt_d", "x_full", "moe_w1", "moe_w2", "prefmask", "gid1",
                "ident", "ut", "ones_col", "ones_row"], ["moe_d", "dbg_d", "eo_dbg"]


_BUILD_CACHE = {}


def _run_launch(name, builder, in_maps):
    if name not in _BUILD_CACHE:
        _BUILD_CACHE[name] = builder()
    nc, in_names, out_names = _BUILD_CACHE[name]
    maps = [{n: im[n] for n in in_names} for im in in_maps]
    res = run_bass_kernel_spmd(nc, maps, core_ids=list(range(CORES)))
    outs = []
    for c in range(CORES):
        outs.append({n: np.asarray(res.results[c][n]) for n in out_names})
    return outs


def kernel(**inputs):
    in_maps = prep_inputs(inputs)
    l0 = _run_launch("l0", build_launch_l0, in_maps)
    for c in range(CORES):
        in_maps[c]["rt_d"] = l0[c]["rt_d"]
        in_maps[c]["a0_d"] = l0[c]["a0_d"]
    mo = _run_launch("moe", build_launch_moe, in_maps)
    for c in range(CORES):
        in_maps[c]["moe_d"] = mo[c]["moe_d"]
    la = _run_launch("l1a", build_launch_l1a, in_maps)
    for c in range(CORES):
        for n in ["x1_d", "xn1fm_d", "part_d"]:
            in_maps[c][n] = la[c][n]
    lf = _run_launch("l1f", build_launch_l1f, in_maps)
    out = np.empty((B, N, D), np.float32)
    for c in range(CORES):
        b, j = divmod(c, 4)
        out[b, j * TOK:(j + 1) * TOK] = lf[c]["y"]
    return out


# version 04
# speedup vs baseline: 1.1378x; 1.1378x over previous
import contextlib
import math
import numpy as np
import ml_dtypes
import concourse.bass as bass
import concourse.mybir as mybir
from concourse.bass_utils import run_bass_kernel_spmd
from concourse.bass_core_rust import IndirectOffsetOnAxis

F32 = mybir.dt.float32
BF16 = mybir.dt.bfloat16
U32 = mybir.dt.uint32
I32 = mybir.dt.int32
AF = mybir.ActivationFunctionType
ALU = mybir.AluOpType
AX = mybir.AxisListType

BF = ml_dtypes.bfloat16

B, N, D = 2, 8192, 1024
CORES = 8
TOK = N // 4
NT = TOK // 128
DC = D // 128
E_MOE, CAP = 16, 1024
FF_MOE = 4096
HALO = 513
WIN = 256
NWIN = TOK // WIN
HEADS1, DH1 = 16, 64
F1P = 5632
F2P = 2816
DEPTH1 = 6
_L1F_DEPTH = 6
NEG = -3.38e38
MSKV = -30000.0
OOBV = float(1 << 24)

ENGINES = ("v", "s", "t", "g", "y")


class Buf:
    __slots__ = ("name", "writer", "readers")

    def __init__(self, name=""):
        self.name = name
        self.writer = None
        self.readers = []


class DmaSem:
    __slots__ = ("name", "count", "waited", "step", "sem")

    def __init__(self, name, step=16):
        self.name = name
        self.count = 0
        self.waited = 0
        self.step = step
        self.sem = None


class PlannedOp:
    __slots__ = ("ename", "fn", "waits", "inc", "lineno")

    def __init__(self, ename, fn, waits, inc):
        self.ename = ename
        self.fn = fn
        self.waits = waits
        self.inc = inc
        self.lineno = 0


class Plan:
    def __init__(self):
        self.ops = {e: [] for e in ENGINES}
        self.ticks = {e: 0 for e in ENGINES}
        self.marks = {}
        self.dma_sems = []
        self.n_ops = 0
        self.pending = {e: ([], []) for e in ENGINES}

    def dma_sem(self, name, step=16):
        d = DmaSem(name, step)
        self.dma_sems.append(d)
        return d

    def _need_wait(self, ename, marker):
        if marker is None:
            return None
        key, val = marker
        if isinstance(key, DmaSem):
            val = key.count
            if val > key.waited:
                key.waited = val
        wm = self.marks.get((ename, key), 0)
        if val <= wm:
            return None
        self.marks[(ename, key)] = val
        return (key, val)

    def _guard_pending(self, ename, reads, writes):
        for e2 in ENGINES:
            if e2 == ename:
                continue
            pr, pw = self.pending[e2]
            for b in reads:
                assert b is None or b not in pw, \
                    f"read of {b.name} while quiet-written on {e2}"
            for b in writes:
                assert b is None or (b not in pw and b not in pr), \
                    f"write of {b.name} while quiet-pending on {e2}"

    def op(self, ename, fn, reads=(), writes=(), dma=None, quiet=False):
        self._guard_pending(ename, reads, writes)
        waits = []
        for b in reads:
            if b is not None:
                w = self._need_wait(ename, b.writer)
                if w:
                    waits.append(w)
        for b in writes:
            if b is None:
                continue
            w = self._need_wait(ename, b.writer)
            if w:
                waits.append(w)
            for m in b.readers:
                w = self._need_wait(ename, m)
                if w:
                    waits.append(w)
        if quiet:
            assert dma is None
            self.ops[ename].append(PlannedOp(ename, fn, self._coalesce(waits), None))
            self.n_ops += 1
            pr, pw = self.pending[ename]
            pr.extend(b for b in reads if b is not None)
            pw.extend(b for b in writes if b is not None)
            return None
        if dma is not None and dma.waited > 0 and dma.count > 0:
            w = self._need_wait(ename, (dma, dma.count))
            if w:
                waits.append(w)
        if dma is not None:
            dma.count += dma.step
            inc = (dma, dma.step)
            marker = (dma, dma.count)
        else:
            self.ticks[ename] += 1
            inc = (ename, 1)
            marker = (ename, self.ticks[ename])
        po = PlannedOp(ename, fn, self._coalesce(waits), inc)
        import sys as _sys
        po.lineno = _sys._getframe(1).f_lineno
        self.ops[ename].append(po)
        self.n_ops += 1
        if dma is None:
            pr, pw = self.pending[ename]
            for b in pr:
                b.readers.append(marker)
            for b in pw:
                b.writer = marker
            self.pending[ename] = ([], [])
        for b in reads:
            if b is not None:
                b.readers.append(marker)
        for b in writes:
            if b is not None:
                b.writer = marker
                b.readers = []
        return marker

    @staticmethod
    def _coalesce(waits):
        best = {}
        for key, val in waits:
            if val > best.get(id(key), (None, -1))[1]:
                best[id(key)] = (key, val)
        return list(best.values())

    def raw_wait(self, ename, marker):
        w = self._need_wait(ename, marker)
        if w:
            self.ops[ename].append(PlannedOp(ename, None, [w], None))
            self.n_ops += 1

    def barrier(self, ename, bufs):
        for b in bufs:
            if b is None:
                continue
            for m in [b.writer] + b.readers:
                w = self._need_wait(ename, m)
                if w:
                    self.ops[ename].append(PlannedOp(ename, None, [w], None))
                    self.n_ops += 1

    def group_end(self, bufs, dma):
        m = (dma, dma.count)
        for b in bufs:
            b.writer = m
            b.readers = []

    def fence(self):
        for e in ENGINES:
            assert not self.pending[e][0] and not self.pending[e][1], \
                f"fence with pending quiet effects on {e}"
        for e in ENGINES:
            for e2 in ENGINES:
                if self.ticks[e2] > 0:
                    w = self._need_wait(e, (e2, self.ticks[e2]))
                    if w:
                        self.ops[e].append(PlannedOp(e, None, [w], None))
                        self.n_ops += 1
            for d in self.dma_sems:
                if d.count > 0:
                    w = self._need_wait(e, (d, d.count))
                    if w:
                        self.ops[e].append(PlannedOp(e, None, [w], None))
                        self.n_ops += 1

    def emit(self, nc, ctx, block):
        tick_sems = {}
        for e in ENGINES:
            if self.ops[e]:
                tick_sems[e] = ctx.enter_context(nc.semaphore(f"tick_{e}"))
        for d in self.dma_sems:
            if d.count > 0:
                d.sem = ctx.enter_context(nc.semaphore(f"dma_{d.name}"))

        def sem_of(key):
            return tick_sems[key] if isinstance(key, str) else key.sem

        def run(eng, ename):
            for po in self.ops[ename]:
                for key, val in po.waits:
                    eng.wait_ge(sem_of(key), val)
                if po.fn is None:
                    continue
                inst = po.fn(eng)
                if po.inc is not None:
                    key, step = po.inc
                    if step == 1 and not isinstance(key, str):
                        inst.then_inc(sem_of(key))
                    else:
                        inst.then_inc(sem_of(key), step)

        section = {"v": block.vector, "s": block.scalar, "t": block.tensor,
                   "g": block.gpsimd, "y": block.sync}
        for e in ENGINES:
            if self.ops[e]:
                section[e](lambda eng, _e=e: run(eng, _e))



def _rot_tables(lo, hi):
    n = N
    half = np.arange(0, DH1, 2, dtype=np.float64)
    inv_freq = 1.0 / 10000.0 ** (half / DH1)
    t = np.arange(n, dtype=np.float64)
    freqs = t[:, None] * inv_freq[None, :]
    base_scale = (half + 0.4 * DH1) / (1.4 * DH1)
    power = (t - n // 2) / (WIN // 2)
    scale = base_scale[None, :] ** power[:, None]
    cos, sin = np.cos(freqs), np.sin(freqs)
    pos = np.arange(lo, hi)
    posc = np.clip(pos, 0, n - 1)
    cs = (cos * scale)[posc].T
    ss = (sin * scale)[posc].T
    ci = (cos / scale)[posc].T
    si = (sin / scale)[posc].T

    def tile4(a):
        return np.ascontiguousarray(np.tile(a, (4, 1)).astype(np.float32))
    return tile4(cs), tile4(ss), tile4(ci), tile4(si)


def _win_masks(first):
    out = []
    for sub in range(2):
        p = np.arange(128) + 128 * sub
        c = np.arange(512)
        allow = (c[None, :] >= p[:, None]) & (c[None, :] <= p[:, None] + WIN)
        if first:
            allow &= c[None, :] >= WIN
        m = np.where(allow, 0.0, MSKV).astype(np.float32)
        out.append(m)
    return out


def _perm_mat():
    P = np.zeros((128, 128), np.float32)
    for pp in range(128):
        if pp % 64 < 32:
            P[pp + 32, pp] = -1.0
        else:
            P[pp - 32, pp] = 1.0
    return P


def prep_inputs(I):
    x = np.ascontiguousarray(np.asarray(I["L0_attn_x"], np.float32))
    x_full = np.ascontiguousarray(x.reshape(B * N, D))
    wqkvT0 = np.ascontiguousarray(np.asarray(I["L0_attn_W_qkv"]).T)
    wg0 = np.ascontiguousarray(np.asarray(I["L0_ffn_Wg"]))
    w1 = np.asarray(I["L0_ffn_w1"])
    w2 = np.asarray(I["L0_ffn_w2"])
    l1a_w = np.ascontiguousarray(np.asarray(I["L1_attn_W_qkv"]))
    l1a_wo = np.ascontiguousarray(np.asarray(I["L1_attn_W_out"]))
    l1a_bo = np.asarray(I["L1_attn_b_out"]).reshape(DC, 128).T.copy()
    f_wqkv = np.ascontiguousarray(np.asarray(I["L1_ffn_Wqkv"]))
    f_qs = np.asarray(I["L1_ffn_q_scale"])
    f_ks = np.asarray(I["L1_ffn_k_scale"])
    f_wo = np.ascontiguousarray(np.asarray(I["L1_ffn_Wo"]))
    f_lng = np.asarray(I["L1_ffn_ln_g"]).reshape(DEPTH1, DC, 128).transpose(0, 2, 1).copy()
    f_lnb = np.asarray(I["L1_ffn_ln_b"]).reshape(DEPTH1, DC, 128).transpose(0, 2, 1).copy()
    W1 = np.asarray(I["L1_ffn_W1"])
    b1 = np.asarray(I["L1_ffn_b1"])
    W2 = np.asarray(I["L1_ffn_W2"])
    b2 = np.asarray(I["L1_ffn_b2"]).reshape(DEPTH1, DC, 128).transpose(0, 2, 1).copy()

    DI = 2730
    nfull = DI // 128
    w1p = np.zeros((DEPTH1, D, F1P), np.float32)
    b1p = np.zeros((DEPTH1, F1P), np.float32)
    for t in range(nfull + 1):
        a0, a1 = t * 128, min((t + 1) * 128, DI)
        wdt = a1 - a0
        col = t * 256
        w1p[:, :, col:col + wdt] = W1[:, :, a0:a1]
        b1p[:, col:col + wdt] = b1[:, a0:a1]
        w1p[:, :, col + 128:col + 128 + wdt] = W1[:, :, DI + a0:DI + a1]
        b1p[:, col + 128:col + 128 + wdt] = b1[:, DI + a0:DI + a1]
    b1p_c = b1p.reshape(DEPTH1, F1P // 128, 128).transpose(0, 2, 1).copy()
    w2p = np.zeros((DEPTH1, F2P, D), np.float32)
    w2p[:, :DI, :] = W2
    b2_c = b2

    qs128 = np.tile(f_qs, (1, 2))
    ks128 = np.tile(f_ks, (1, 2))

    perm = _perm_mat()
    ident = np.eye(128, dtype=np.float32)
    hsel = np.zeros((DC, 128, 16), np.float32)
    for ch in range(DC):
        for r in range(2):
            hsel[ch, r * 64:(r + 1) * 64, 2 * ch + r] = 1.0
    hbc = np.zeros((DC, 16, 128), np.float32)
    for ch in range(DC):
        for r in range(2):
            hbc[ch, 2 * ch + r, r * 64:(r + 1) * 64] = 1.0
    ut = np.triu(np.ones((128, 128), np.float32))
    ones_col = np.ones((128, 1), np.float32)
    ones_row = np.ones((1, 128), np.float32)

    in_maps = []
    msk_int = _win_masks(False)
    msk_fst = _win_masks(True)
    for c in range(CORES):
        b, j = divmod(c, 4)
        t0 = j * TOK
        g0 = b * N + t0
        d = {}
        d["x_own"] = np.ascontiguousarray(x[b, t0:t0 + TOK])
        d["x_full"] = x_full
        d["wqkvT0"] = wqkvT0
        d["wg0"] = wg0
        d["moe_w1"] = np.ascontiguousarray(w1[2 * c:2 * c + 2])
        d["moe_w2"] = np.ascontiguousarray(w2[2 * c:2 * c + 2])
        d["prefmask"] = np.asarray([[1.0] if r < c else [0.0] for r in range(8)],
                                   np.float32)
        d["gid1"] = (np.arange(TOK, dtype=np.float32).reshape(NT, 128).T + g0 + 1.0).copy()
        d["l1a_w"] = l1a_w
        d["l1a_wo"] = l1a_wo
        d["l1a_bo"] = l1a_bo
        d["f_wqkv"] = f_wqkv
        d["f_wo"] = f_wo
        d["f_w1p"] = w1p
        d["f_b1p"] = b1p_c
        d["f_w2p"] = w2p
        d["f_b2"] = b2_c
        d["f_lng"] = f_lng
        d["f_lnb"] = f_lnb
        d["f_qs"] = qs128[:, None, :].copy()
        d["f_ks"] = ks128[:, None, :].copy()
        d["f_qs_c"] = qs128[:, :, None].copy()
        d["f_ks_c"] = ks128[:, :, None].copy()
        cs, ss, ci, si = _rot_tables(t0 - HALO, t0 + TOK)
        d["rot_cs"], d["rot_ss"], d["rot_ci"], d["rot_si"] = cs, ss, ci, si
        m0 = msk_fst if j == 0 else msk_int
        d["msk_w0"] = np.stack(m0)
        d["msk_int"] = np.stack(msk_int)
        d["msk_ext"] = (np.full((128, 512), MSKV, np.float32) if j == 0
                        else msk_int[1])
        d["halo_gate"] = np.full((128, 1), 0.0 if j == 0 else 1.0, np.float32)
        blk = (j - 1) % 4
        d["halo_rows"] = (np.arange(128, dtype=np.float32)[:, None]
                          + (blk * 1024 + 128 * np.arange(DC, dtype=np.float32))[None, :]).copy()
        d["perm"] = perm
        d["ident"] = ident
        d["hsel"] = np.ascontiguousarray(hsel.transpose(1, 0, 2).reshape(128, DC * 16))
        d["hbc"] = np.ascontiguousarray(hbc.transpose(1, 0, 2).reshape(16, DC * 128))
        d["ut"] = ut
        d["ones_col"] = ones_col
        d["ones_row"] = ones_row
        in_maps.append(d)
    return in_maps



class K:

    def __init__(self, nc, ctx, P):
        self.nc = nc
        self.ctx = ctx
        self.P = P
        self.scopes = [ctx]

    def push_scope(self):
        es = contextlib.ExitStack()
        self.scopes.append(es)

    def pop_scope(self, fence=True):
        es = self.scopes.pop()
        es.close()
        if fence:
            self.P.fence()

    def sb(self, name, shape, dt):
        self._uid = getattr(self, "_uid", 0) + 1
        return self.scopes[-1].enter_context(
            self.nc.sbuf_tensor(f"{name}_u{self._uid}", shape, dt))

    def breg(self, g, val):
        if not hasattr(self, "_regs"):
            self._regs = {}
        if val not in self._regs:
            self._regs[val] = g.to_reg(val)
        return self._regs[val]

    def ps(self, name, shape, dt=F32):
        self._uid = getattr(self, "_uid", 0) + 1
        return self.scopes[-1].enter_context(
            self.nc.psum_tensor(f"{name}_u{self._uid}", shape, dt))

    def inp(self, name, shape):
        return self.nc.dram_tensor(name, shape, F32, kind="ExternalInput").ap()

    def outp(self, name, shape, dt=F32):
        return self.nc.dram_tensor(name, shape, dt, kind="ExternalOutput").ap()

    def dram(self, name, shape, dt=F32, shared=False):
        return self.nc.dram_tensor(name, shape, dt, kind="Internal",
                                   addr_space="Shared" if shared else "Local").ap()



def emit_l0(k, A, grp):
    nc, P = k.nc, k.P
    ld = P.dma_sem("l0ld")
    lw = P.dma_sem("l0lw")
    lx = P.dma_sem("l0lx")
    stk = P.dma_sem("l0stk")
    stv = P.dma_sem("l0stv")
    cck = P.dma_sem("l0cck", step=1)
    ccv = P.dma_sem("l0ccv", step=1)
    lkb = P.dma_sem("l0lkb")
    lvb = P.dma_sem("l0lvb")
    sta = P.dma_sem("l0sta")
    strt = P.dma_sem("l0strt")

    ident = k.sb("id_bf", [128, 128], BF16); bI = Buf("id")
    P.op("g", lambda g: g.dma_start(out=ident[:, :], in_=A["ident"]), writes=[bI], dma=ld)
    q_fm = [k.sb(f"qfm_{i}", [128, TOK], BF16) for i in range(DC)]
    bq = [Buf(f"qfm_{i}") for i in range(DC)]

    k.push_scope()
    w0 = [k.sb(f"w0_{i}", [128, 3072], BF16) for i in range(DC)]
    bw0 = [Buf(f"w0_{i}") for i in range(DC)]
    for i in range(DC):
        P.op("g", lambda g, i=i: g.dma_start(out=w0[i][:, :],
                                             in_=A["wqkvT0"][128 * i:128 * (i + 1), :]),
             writes=[bw0[i]], dma=lw)
    wg = k.sb("wg", [128, DC, 16], BF16); bwg = Buf("wg")
    P.op("g", lambda g: g.dma_start(out=wg[:, :, :],
                                    in_=A["wg0"].rearrange("(c p) e -> p c e", p=128)),
         writes=[bwg], dma=lw)

    xn_fm = [k.sb(f"xnfm_{i}", [128, TOK], BF16) for i in range(DC)]
    bxn = [Buf(f"xnfm_{i}") for i in range(DC)]
    xt = [k.sb(f"xt_{i}", [128, D], F32) for i in range(2)]
    bxt = [Buf(f"xt_{i}") for i in range(2)]
    xnt = [k.sb(f"xnt_{i}", [128, D], BF16) for i in range(2)]
    bxnt = [Buf(f"xnt_{i}") for i in range(2)]
    sqs = k.sb("sqs", [128, D], F32); bsq = Buf("sq")
    st1 = k.sb("st1", [128, 8], F32); bst = Buf("st")
    tr_ps = k.ps("tr_ps", [128, 512], BF16); btr = Buf("tr")

    def ln_tile(xbuf, bxb, xnb, bxnb):
        P.op("v", lambda v: v.reduce_sum(st1[:, 0:1], xbuf[:, :], axis=AX.X),
             reads=[bxb], writes=[bst])
        P.op("s", lambda s: s.activation(sqs[:, :], xbuf[:, :], AF.Square,
                                         accum_out=st1[:, 1:2]),
             reads=[bxb], writes=[bsq, bst])
        P.op("v", lambda v: v.tensor_scalar_mul(st1[:, 2:3], st1[:, 0:1], 1.0 / D),
             reads=[bst], writes=[bst])
        P.op("v", lambda v: v.tensor_tensor(st1[:, 3:4], st1[:, 2:3], st1[:, 2:3], ALU.mult),
             reads=[bst], writes=[bst])
        P.op("v", lambda v: v.scalar_tensor_tensor(st1[:, 3:4], st1[:, 1:2], 1.0 / D,
                                                   st1[:, 3:4], ALU.mult, ALU.subtract),
             reads=[bst], writes=[bst])
        P.op("v", lambda v: v.tensor_scalar_add(st1[:, 3:4], st1[:, 3:4], 1e-5),
             reads=[bst], writes=[bst])
        P.op("s", lambda s: s.activation(st1[:, 4:5], st1[:, 3:4], AF.Sqrt),
             reads=[bst], writes=[bst])
        P.op("v", lambda v: v.reciprocal(st1[:, 4:5], st1[:, 4:5]), reads=[bst], writes=[bst])
        P.op("v", lambda v: v.scalar_tensor_tensor(st1[:, 5:6], st1[:, 2:3], -1.0,
                                                   st1[:, 4:5], ALU.mult, ALU.mult),
             reads=[bst], writes=[bst])
        P.op("v", lambda v: v.tensor_scalar(xnb[:, :], xbuf[:, :], st1[:, 4:5],
                                            st1[:, 5:6], ALU.mult, ALU.add),
             reads=[bxb, bst], writes=[bxnb])

    for t in range(NT):
        sl = t % 2
        P.op("y", lambda y, t=t, sl=sl: y.dma_start(
                out=xt[sl][:, :], in_=A["x_own"][128 * t:128 * (t + 1), :]),
             writes=[bxt[sl]], dma=lx)
        ln_tile(xt[sl], bxt[sl], xnt[sl], bxnt[sl])
        for ch in range(DC):
            q = ch % 4
            P.op("t", lambda te, ch=ch, sl=sl, q=q: te.transpose(
                    tr_ps[:, 128 * q:128 * (q + 1)],
                    xnt[sl][:, 128 * ch:128 * (ch + 1)], ident[:, :]),
                 reads=[bxnt[sl], bI], writes=[btr])
            P.op("v", lambda v, ch=ch, t=t, q=q: v.tensor_copy(
                    xn_fm[ch][:, 128 * t:128 * (t + 1)],
                    tr_ps[:, 128 * q:128 * (q + 1)]),
                 reads=[btr], writes=[bxn[ch]])

    gl_ps = k.ps("gl_ps", [128, 16], F32); bgl = Buf("gl")
    glx = k.sb("glx", [128, 16], F32); bglx = Buf("glx")
    mx8 = k.sb("mx8", [128, 8], F32); bmx = Buf("mx8")
    ix8 = k.sb("ix8", [128, 8], U32); bix = Buf("ix8")
    exs = k.sb("exs", [128, 16], F32); bex = Buf("exs")
    rts = k.sb("rts", [128, 32], F32); brt = Buf("rts")
    for t in range(NT):
        for ch in range(DC):
            P.op("t", lambda te, ch=ch, t=t: te.matmul(
                    gl_ps[:, :], xn_fm[ch][:, 128 * t:128 * (t + 1)], wg[:, ch, :],
                    start=(ch == 0), stop=(ch == DC - 1)),
                 reads=[bxn[ch], bwg], writes=[bgl], quiet=(ch != DC - 1))
        P.op("v", lambda v: v.tensor_copy(glx[:, :], gl_ps[:, :]), reads=[bgl], writes=[bglx])
        P.op("v", lambda v: v.max(mx8[:, :], glx[:, :]), reads=[bglx], writes=[bmx])
        P.op("v", lambda v: v.max_index(ix8[:, :], mx8[:, :], glx[:, :]),
             reads=[bmx, bglx], writes=[bix])
        P.op("v", lambda v: v.tensor_scalar_mul(st1[:, 6:7], mx8[:, 0:1], -1.0),
             reads=[bmx], writes=[bst])
        P.op("s", lambda s: s.activation(exs[:, :], glx[:, :], AF.Exp,
                                         bias=st1[:, 6:7], accum_out=st1[:, 7:8]),
             reads=[bglx, bst], writes=[bex, bst])
        P.op("v", lambda v, t=t: v.reciprocal(rts[:, t:t + 1], st1[:, 7:8]),
             reads=[bst], writes=[brt])
        P.op("v", lambda v, t=t: v.tensor_copy(rts[:, 16 + t:17 + t], ix8[:, 0:1]),
             reads=[bix], writes=[brt])
    P.op("y", lambda y: y.dma_start(out=A["rt_d"], in_=rts[:, :]), reads=[brt], dma=strt)

    pj_ps = [k.ps(f"pj_ps{i}", [128, 512], F32) for i in range(2)]
    bpj = [Buf(f"pj{i}") for i in range(2)]

    def proj_chunk(dst_sl, bdst, col0, m):
        for n in range(TOK // 512):
            sl = n % 2
            for kk in range(DC):
                P.op("t", lambda te, m=m, n=n, kk=kk, sl=sl: te.matmul(
                        pj_ps[sl][:, :],
                        w0[kk][:, col0 + 128 * m:col0 + 128 * (m + 1)],
                        xn_fm[kk][:, 512 * n:512 * (n + 1)],
                        start=(kk == 0), stop=(kk == DC - 1)),
                     reads=[bw0[kk], bxn[kk]], writes=[bpj[sl]], quiet=(kk != DC - 1))
            P.op("v", lambda v, n=n, sl=sl: v.tensor_copy(
                    dst_sl[:, 512 * n:512 * (n + 1)], pj_ps[sl][:, :]),
                 reads=[bpj[sl]], writes=[bdst])

    for m in range(DC):
        proj_chunk(q_fm[m], bq[m], 0, m)

    kch = [k.sb(f"kch_{i}", [128, TOK], BF16) for i in range(2)]
    bkch = [Buf(f"kch_{i}") for i in range(2)]
    for m in range(DC):
        sl = m % 2
        proj_chunk(kch[sl], bkch[sl], D, m)
        P.op("y", lambda y, m=m, sl=sl: y.dma_start(
                out=A["kag_in"][128 * m:128 * (m + 1), :], in_=kch[sl][:, :]),
             reads=[bkch[sl]], dma=stk)

    vt = [k.sb(f"vt_{i}", [128, D], BF16) for i in range(2)]
    bvt = [Buf(f"vt_{i}") for i in range(2)]
    vp_ps = [k.ps(f"vp_ps{i}", [128, 512], F32) for i in range(2)]
    bvp = [Buf(f"vp{i}") for i in range(2)]
    for t in range(NT):
        sl = t % 2
        for nn in range(2):
            for kk in range(DC):
                P.op("t", lambda te, t=t, nn=nn, kk=kk, sl=sl: te.matmul(
                        vp_ps[nn][:, :],
                        xn_fm[kk][:, 128 * t:128 * (t + 1)],
                        w0[kk][:, 2 * D + 512 * nn:2 * D + 512 * (nn + 1)],
                        start=(kk == 0), stop=(kk == DC - 1)),
                     reads=[bxn[kk], bw0[kk]], writes=[bvp[nn]], quiet=(kk != DC - 1))
            P.op("v", lambda v, nn=nn, sl=sl: v.tensor_copy(
                    vt[sl][:, 512 * nn:512 * (nn + 1)], vp_ps[nn][:, :]),
                 reads=[bvp[nn]], writes=[bvt[sl]])
        P.op("y", lambda y, t=t, sl=sl: y.dma_start(
                out=A["vag_in"][128 * t:128 * (t + 1), :], in_=vt[sl][:, :]),
             reads=[bvt[sl]], dma=stv)

    bkagi = Buf("kagi"); bvagi = Buf("vagi")
    P.group_end([bkagi], stk)
    P.group_end([bvagi], stv)
    bkag = Buf("kag"); bvag = Buf("vag")
    P.op("g", lambda g: g.collective_compute("AllGather", ALU.bypass,
                                             replica_groups=grp,
                                             ins=[A["kag_in"]], outs=[A["kag"]]),
         reads=[bkagi], writes=[bkag], dma=cck)
    P.op("g", lambda g: g.collective_compute("AllGather", ALU.bypass,
                                             replica_groups=grp,
                                             ins=[A["vag_in"]], outs=[A["vag"]]),
         reads=[bvagi], writes=[bvag], dma=ccv)

    k.pop_scope()

    Kb = [k.sb(f"Kb_{i}", [128, TOK], BF16) for i in range(DC)]
    bKb = [Buf(f"Kb_{i}") for i in range(DC)]
    Vb = [k.sb(f"Vb_{i}", [128, D], BF16) for i in range(NT)]
    bVb = [Buf(f"Vb_{i}") for i in range(NT)]
    acc = [k.sb(f"acc_{i}", [128, D], F32) for i in range(NT)]
    bacc = [Buf(f"acc_{i}") for i in range(NT)]
    ml = k.sb("ml", [128, 64], F32)
    bml = [Buf(f"ml_{i}") for i in range(NT)]
    fsc = k.sb("fsc", [128, 8], F32); bfsc = Buf("fsc")
    s_ps = k.ps("s_ps", [128, TOK], F32); bS = Buf("s_ps")
    av_ps = k.ps("av_ps", [128, D], F32); bAV = Buf("av")
    pt_ps = k.ps("pt_ps", [128, 512], BF16); bPT = Buf("ptp")
    Pt = k.sb("Pt", [128, TOK], BF16); bPt = Buf("Pt")
    PTs = k.sb("PTs", [128, TOK], BF16); bPTs = Buf("PTs")

    for qt in range(NT):
        P.op("v", lambda v, qt=qt: v.memset(acc[qt][:, :], 0.0), writes=[bacc[qt]])
        P.op("v", lambda v, qt=qt: v.memset(ml[:, qt:qt + 1], NEG), writes=[bml[qt]])
        P.op("v", lambda v, qt=qt: v.memset(ml[:, 16 + qt:17 + qt], 0.0), writes=[bml[qt]])

    for kc in range(4):
        for i in range(DC):
            P.op("y", lambda y, i=i, kc=kc: y.dma_start(
                    out=Kb[i][:, :], in_=A["kag"][1024 * kc + 128 * i:1024 * kc + 128 * (i + 1), :]),
                 reads=[bkag], writes=[bKb[i]], dma=lkb)
        for i in range(NT):
            P.op("g", lambda g, i=i, kc=kc: g.dma_start(
                    out=Vb[i][:, :], in_=A["vag"][TOK * kc + 128 * i:TOK * kc + 128 * (i + 1), :]),
                 reads=[bvag], writes=[bVb[i]], dma=lvb)
        for qt in range(NT):
            for g4 in range(4):
                for kk in range(DC):
                    P.op("t", lambda te, qt=qt, g4=g4, kk=kk: te.matmul(
                            s_ps[:, 512 * g4:512 * (g4 + 1)],
                            q_fm[kk][:, 128 * qt:128 * (qt + 1)],
                            Kb[kk][:, 512 * g4:512 * (g4 + 1)],
                            start=(kk == 0), stop=(kk == DC - 1)),
                         reads=[bq[kk], bKb[kk]], writes=[bS],
                         quiet=not (kk == DC - 1 and g4 == 3))
            P.op("v", lambda v: v.reduce_max(fsc[:, 0:1], s_ps[:, :], axis=AX.X),
                 reads=[bS], writes=[bfsc])
            P.op("v", lambda v, qt=qt: v.scalar_tensor_tensor(
                    fsc[:, 1:2], fsc[:, 0:1], 1.0 / 32.0, ml[:, qt:qt + 1],
                    ALU.mult, ALU.max),
                 reads=[bfsc, bml[qt]], writes=[bfsc])
            P.op("v", lambda v, qt=qt: v.tensor_tensor(
                    fsc[:, 2:3], ml[:, qt:qt + 1], fsc[:, 1:2], ALU.subtract),
                 reads=[bml[qt], bfsc], writes=[bfsc])
            P.op("s", lambda s: s.activation(fsc[:, 2:3], fsc[:, 2:3], AF.Exp),
                 reads=[bfsc], writes=[bfsc])
            P.op("v", lambda v: v.tensor_scalar_mul(fsc[:, 3:4], fsc[:, 1:2], -1.0),
                 reads=[bfsc], writes=[bfsc])
            P.op("v", lambda v, qt=qt: v.tensor_copy(ml[:, qt:qt + 1], fsc[:, 1:2]),
                 reads=[bfsc], writes=[bml[qt]])
            P.op("s", lambda s: s.activation(Pt[:, :], s_ps[:, :], AF.Exp,
                                             bias=fsc[:, 3:4], scale=1.0 / 32.0,
                                             accum_out=fsc[:, 4:5]),
                 reads=[bS, bfsc], writes=[bPt, bfsc])
            P.op("v", lambda v, qt=qt: v.tensor_tensor(
                    ml[:, 16 + qt:17 + qt], ml[:, 16 + qt:17 + qt], fsc[:, 2:3], ALU.mult),
                 reads=[bfsc], writes=[bml[qt]])
            P.op("v", lambda v, qt=qt: v.tensor_tensor(
                    ml[:, 16 + qt:17 + qt], ml[:, 16 + qt:17 + qt], fsc[:, 4:5], ALU.add),
                 reads=[bfsc], writes=[bml[qt]])
            for r in range(4):
                for u in range(4):
                    i = 4 * r + u
                    P.op("t", lambda te, i=i, u=u: te.transpose(
                            pt_ps[:, 128 * u:128 * (u + 1)],
                            Pt[:, 128 * i:128 * (i + 1)], ident[:, :]),
                         reads=[bPt, bI], writes=[bPT], quiet=(u != 3))
                P.op("v", lambda v, r=r: v.tensor_copy(
                        PTs[:, 512 * r:512 * (r + 1)], pt_ps[:, :]),
                     reads=[bPT], writes=[bPTs])
            for g2 in range(2):
                for i in range(NT):
                    P.op("t", lambda te, i=i, g2=g2: te.matmul(
                            av_ps[:, 512 * g2:512 * (g2 + 1)],
                            PTs[:, 128 * i:128 * (i + 1)],
                            Vb[i][:, 512 * g2:512 * (g2 + 1)],
                            start=(i == 0), stop=(i == NT - 1)),
                         reads=[bPTs, bVb[i]], writes=[bAV],
                         quiet=not (i == NT - 1 and g2 == 1))
            P.op("v", lambda v, qt=qt: v.scalar_tensor_tensor(
                    acc[qt][:, :], acc[qt][:, :], fsc[:, 2:3], av_ps[:, :],
                    ALU.mult, ALU.add),
                 reads=[bfsc, bAV, bacc[qt]], writes=[bacc[qt]])

    att = [k.sb(f"att_{i}", [128, D], F32) for i in range(2)]
    batt = [Buf(f"att_{i}") for i in range(2)]
    for qt in range(NT):
        sl = qt % 2
        P.op("v", lambda v, qt=qt: v.reciprocal(fsc[:, 5:6], ml[:, 16 + qt:17 + qt]),
             reads=[bml[qt]], writes=[bfsc])
        P.op("v", lambda v, qt=qt, sl=sl: v.tensor_scalar_mul(
                att[sl][:, :], acc[qt][:, :], fsc[:, 5:6]),
             reads=[bacc[qt], bfsc], writes=[batt[sl]])
        P.op("y", lambda y, qt=qt, sl=sl: y.dma_start(
                out=A["a0_d"][128 * qt:128 * (qt + 1), :], in_=att[sl][:, :]),
             reads=[batt[sl]], dma=sta)
    P.raw_wait("y", (sta, sta.count))
    P.raw_wait("y", (strt, strt.count))




def emit_moe(k, A):
    nc, P = k.nc, k.P
    ld = P.dma_sem("mld")
    lcn = P.dma_sem("mlcn")
    stc = P.dma_sem("mstc")
    ccc = P.dma_sem("mccc", step=1)
    stz = P.dma_sem("mstz")
    sct = P.dma_sem("msct")
    ccr = P.dma_sem("mccr", step=1)
    ltb = P.dma_sem("mltb")
    gx = P.dma_sem("mgx")
    lw1 = P.dma_sem("mlw1")
    lw2 = P.dma_sem("mlw2")
    seo = P.dma_sem("mseo")
    cce = P.dma_sem("mcce", step=1)
    geo = P.dma_sem("mgeo")
    stm = P.dma_sem("mstm")

    ldgi = P.dma_sem("mldgi")
    ident = k.sb("mid_bf", [128, 128], BF16); bI = Buf("mid")
    P.op("g", lambda g: g.dma_start(out=ident[:, :], in_=A["ident"]), writes=[bI], dma=ldgi)
    rts = k.sb("mrts", [128, 32], F32); brt = Buf("mrts")
    P.op("y", lambda y: y.dma_start(out=rts[:, :], in_=A["rt_d"]), writes=[brt], dma=ld)
    slot_u = k.sb("mslotu", [128, NT], U32); bslu = Buf("mslotu")
    gidu = k.sb("mgidu", [128, NT], U32); bgu = Buf("mgidu")
    xgT = [k.sb(f"mxgT_{i}", [128, TOK], BF16) for i in range(DC)]
    bxgT = [Buf(f"mxgT_{i}") for i in range(DC)]

    k.push_scope()
    utc = k.sb("mut", [128, 128], F32); but = Buf("mut")
    P.op("y", lambda y: y.dma_start(out=utc[:, :], in_=A["ut"]), writes=[but], dma=ld)
    onec = k.sb("monec", [128, 1], F32); bo1 = Buf("monec")
    P.op("y", lambda y: y.dma_start(out=onec[:, :], in_=A["ones_col"]), writes=[bo1], dma=ld)
    oner = k.sb("moner", [1, 128], F32); bor = Buf("moner")
    P.op("y", lambda y: y.dma_start(out=oner[:, :], in_=A["ones_row"]), writes=[bor], dma=ld)
    pfm = k.sb("mpfm", [8, 1], F32); bpf = Buf("mpfm")
    P.op("y", lambda y: y.dma_start(out=pfm[:, :], in_=A["prefmask"]), writes=[bpf], dma=ld)
    gid1 = k.sb("mgid", [128, NT], F32); bgi = Buf("mgid")
    P.op("y", lambda y: y.dma_start(out=gid1[:, :], in_=A["gid1"]), writes=[bgi], dma=ld)

    idxf = rts[:, 16:32]
    ohs = k.sb("mohs", [128, 16 * NT], F32); boh = Buf("mohs")
    lr = k.sb("mlr", [128, 16 * NT], F32); blr = Buf("mlr")
    cnt = k.sb("mcnt", [1, 16], F32); bcn = Buf("mcnt")
    scr = k.sb("mscr", [1, 64], F32); bsc = Buf("mscr")
    cum_ps = k.ps("mcum", [128, NT], F32); bcp = Buf("mcum")
    cs_ps = k.ps("mcs", [1, NT], F32); bcs = Buf("mcs")
    for e in range(16):
        P.op("v", lambda v, e=e: v.tensor_single_scalar(
                ohs[:, NT * e:NT * (e + 1)], idxf, float(e), ALU.is_equal),
             reads=[brt], writes=[boh])
        P.op("t", lambda t, e=e: t.matmul(cs_ps[:, :], onec[:, :],
                                          ohs[:, NT * e:NT * (e + 1)],
                                          start=True, stop=True),
             reads=[bo1, boh], writes=[bcs])
        P.op("v", lambda v: v.tensor_copy(scr[0:1, 32:32 + NT], cs_ps[0:1, :]),
             reads=[bcs], writes=[bsc])
        P.op("v", lambda v: v.tensor_tensor_scan(scr[0:1, 0:NT], oner[0:1, 0:NT],
                                                 scr[0:1, 32:32 + NT], 0.0,
                                                 ALU.mult, ALU.add),
             reads=[bsc, bor], writes=[bsc])
        P.op("v", lambda v: v.tensor_tensor(scr[0:1, 16:16 + NT], scr[0:1, 0:NT],
                                            scr[0:1, 32:32 + NT], ALU.subtract),
             reads=[bsc], writes=[bsc])
        P.op("v", lambda v, e=e: v.tensor_copy(cnt[0:1, e:e + 1], scr[0:1, NT - 1:NT]),
             reads=[bsc], writes=[bcn])
        P.op("t", lambda t, e=e: t.matmul(cum_ps[:, :], utc[:, :],
                                          ohs[:, NT * e:NT * (e + 1)],
                                          start=True, stop=False),
             reads=[but, boh], writes=[bcp], quiet=True)
        P.op("t", lambda t, e=e: t.matmul(cum_ps[:, :], oner[:, :], scr[0:1, 16:16 + NT],
                                          start=False, stop=True),
             reads=[bor, bsc], writes=[bcp])
        P.op("v", lambda v, e=e: v.tensor_copy(lr[:, NT * e:NT * (e + 1)], cum_ps[:, :]),
             reads=[bcp], writes=[blr])

    P.op("y", lambda y: y.dma_start(out=A["cnt_in"], in_=cnt[0:1, :]), reads=[bcn], dma=stc)
    bcnag = Buf("cntag")
    P.op("g", lambda g: g.collective_compute("AllGather", ALU.bypass,
                                             replica_groups=GRP_ALL,
                                             ins=[A["cnt_in"]], outs=[A["cnt_ag"]]),
         reads=[bcn], writes=[bcnag], dma=ccc)
    cnts = k.sb("mcnts", [8, 16], F32); bcns = Buf("mcnts")
    P.op("y", lambda y: y.dma_start(out=cnts[:, :], in_=A["cnt_ag"]),
         reads=[bcnag], writes=[bcns], dma=lcn)
    prf_ps = k.ps("mprf", [128, 16], F32); bpp = Buf("mprf")
    P.op("t", lambda t: t.matmul(prf_ps[0:1, :], pfm[:, :], cnts[:, :],
                                 start=True, stop=True),
         reads=[bpf, bcns], writes=[bpp])
    prow = k.sb("mprow", [1, 16], F32); bpr = Buf("mprow")
    P.op("v", lambda v: v.tensor_copy(prow[0:1, :], prf_ps[0:1, :]),
         reads=[bpp], writes=[bpr])
    P.op("t", lambda t: t.matmul(prf_ps[:, :], oner[:, :], prow[0:1, :],
                                 start=True, stop=True),
         reads=[bor, bpr], writes=[bpp])
    pb = k.sb("mpb", [128, 16], F32); bpb = Buf("mpb")
    P.op("v", lambda v: v.tensor_copy(pb[:, :], prf_ps[:, :]), reads=[bpp], writes=[bpb])

    sel = k.sb("msel", [128, NT], F32); bsl = Buf("msel")
    tmp = k.sb("mtmp", [128, NT], F32); btm = Buf("mtmp")
    P.op("v", lambda v: v.memset(sel[:, :], -1.0), writes=[bsl])
    for e in range(16):
        P.op("v", lambda v, e=e: v.tensor_scalar_add(tmp[:, :], lr[:, NT * e:NT * (e + 1)],
                                                     pb[:, e:e + 1]),
             reads=[blr, bpb], writes=[btm])
        P.op("v", lambda v, e=e: v.tensor_tensor(tmp[:, :], tmp[:, :],
                                                 ohs[:, NT * e:NT * (e + 1)], ALU.mult),
             reads=[btm, boh], writes=[btm])
        P.op("v", lambda v: v.tensor_tensor(sel[:, :], sel[:, :], tmp[:, :], ALU.add),
             reads=[btm, bsl], writes=[bsl])
    vm = k.sb("mvm", [128, NT], F32); bvm = Buf("mvm")
    P.op("v", lambda v: v.tensor_single_scalar(vm[:, :], sel[:, :], float(CAP), ALU.is_ge),
         reads=[bsl], writes=[bvm])
    slot = k.sb("mslot", [128, NT], F32); bslo = Buf("mslot")
    P.op("v", lambda v: v.scalar_tensor_tensor(slot[:, :], idxf, float(CAP), sel[:, :],
                                               ALU.mult, ALU.add),
         reads=[brt, bsl], writes=[bslo])
    P.op("v", lambda v: v.scalar_tensor_tensor(slot[:, :], vm[:, :], OOBV, slot[:, :],
                                               ALU.mult, ALU.add),
         reads=[bvm, bslo], writes=[bslo])
    P.op("v", lambda v: v.tensor_copy(slot_u[:, :], slot[:, :]), reads=[bslo], writes=[bslu])
    if "dbg_d" in A:
        P.op("y", lambda y: y.dma_start(out=A["dbg_d"][:, 0:NT], in_=sel[:, :]),
             reads=[bsl], dma=stm)
        P.op("y", lambda y: y.dma_start(out=A["dbg_d"][:, NT:2 * NT], in_=slot[:, :]),
             reads=[bslo], dma=stm)

    zero = k.sb("mzero", [128, 1024], F32); bz = Buf("mzero")
    P.op("v", lambda v: v.memset(zero[:, :], 0.0), writes=[bz])
    btbl = Buf("mtbl")
    P.op("y", lambda y: y.dma_start(out=A["tbl_d"].rearrange("(a p) w -> p a w", p=128),
                                    in_=zero[:, :].rearrange("p (a w) -> p a w", w=8)),
         reads=[bz], writes=[btbl], dma=stz)
    gval = k.sb("mgval", [128, NT, 8], F32); bgv = Buf("mgval")
    P.op("v", lambda v: v.tensor_copy(gval[:, :, :],
                                      gid1[:, :].unsqueeze(2).broadcast_to([128, NT, 8])),
         reads=[bgi], writes=[bgv])
    for t in range(NT):
        P.op("g", lambda g, t=t: g.indirect_dma_start(
                out=A["tbl_d"],
                out_offset=IndirectOffsetOnAxis(ap=slot_u[:, t:t + 1], axis=0),
                in_=gval[:, t, :], in_offset=None,
                bounds_check=k.breg(g, B * N - 1), oob_is_err=False),
             reads=[bslu, bgv, btbl], writes=[btbl], dma=sct)
    btrs = Buf("mtrs")
    P.op("g", lambda g: g.collective_compute("ReduceScatter", ALU.add,
                                             replica_groups=GRP_ALL,
                                             ins=[A["tbl_d"]], outs=[A["tbl_rs"]]),
         reads=[btbl], writes=[btrs], dma=ccr)

    sts = k.sb("msts", [128, NT, 8], F32); bsts = Buf("msts")
    P.op("y", lambda y: y.dma_start(out=sts[:, :, :],
                                    in_=A["tbl_rs"].rearrange("(a p) w -> p a w", p=128)),
         reads=[btrs], writes=[bsts], dma=ltb)
    gidx = k.sb("mgidx", [128, NT], F32); bgx2 = Buf("mgidx")
    for t in range(NT):
        P.op("v", lambda v, t=t: v.tensor_single_scalar(gidx[:, t:t + 1], sts[:, t, 0:1],
                                                        0.0, ALU.is_equal),
             reads=[bsts], writes=[bgx2])
        P.op("v", lambda v, t=t: v.scalar_tensor_tensor(gidx[:, t:t + 1], gidx[:, t:t + 1],
                                                        OOBV, sts[:, t, 0:1],
                                                        ALU.mult, ALU.add),
             reads=[bgx2, bsts], writes=[bgx2])
        P.op("v", lambda v, t=t: v.tensor_scalar_add(gidx[:, t:t + 1], gidx[:, t:t + 1], -1.0),
             reads=[bgx2], writes=[bgx2])
    P.op("v", lambda v: v.tensor_copy(gidu[:, :], gidx[:, :]), reads=[bgx2], writes=[bgu])
    if "dbg_d" in A:
        P.op("y", lambda y: y.dma_start(out=A["dbg_d"][:, 2 * NT:3 * NT], in_=gidx[:, :]),
             reads=[bgx2], dma=stm)
    k.pop_scope()

    k.push_scope()
    xg = [k.sb(f"mxg_{i}", [128, D], F32) for i in range(2)]
    bxg = [Buf(f"mxg_{i}") for i in range(2)]
    xng = [k.sb(f"mxng_{i}", [128, D], BF16) for i in range(2)]
    bxng = [Buf(f"mxng_{i}") for i in range(2)]
    sqs = k.sb("msqs", [128, D], F32); bsq = Buf("msq")
    st1 = k.sb("mst1", [128, 8], F32); bst = Buf("mst")
    tr_ps = k.ps("mtr", [128, 512], BF16); btr = Buf("mtr")

    def ln_tile(xbuf, bxb, xnb, bxnb):
        P.op("v", lambda v: v.reduce_sum(st1[:, 0:1], xbuf[:, :], axis=AX.X),
             reads=[bxb], writes=[bst])
        P.op("s", lambda s: s.activation(sqs[:, :], xbuf[:, :], AF.Square,
                                         accum_out=st1[:, 1:2]),
             reads=[bxb], writes=[bsq, bst])
        P.op("v", lambda v: v.tensor_scalar_mul(st1[:, 2:3], st1[:, 0:1], 1.0 / D),
             reads=[bst], writes=[bst])
        P.op("v", lambda v: v.tensor_tensor(st1[:, 3:4], st1[:, 2:3], st1[:, 2:3], ALU.mult),
             reads=[bst], writes=[bst])
        P.op("v", lambda v: v.scalar_tensor_tensor(st1[:, 3:4], st1[:, 1:2], 1.0 / D,
                                                   st1[:, 3:4], ALU.mult, ALU.subtract),
             reads=[bst], writes=[bst])
        P.op("v", lambda v: v.tensor_scalar_add(st1[:, 3:4], st1[:, 3:4], 1e-5),
             reads=[bst], writes=[bst])
        P.op("s", lambda s: s.activation(st1[:, 4:5], st1[:, 3:4], AF.Sqrt),
             reads=[bst], writes=[bst])
        P.op("v", lambda v: v.reciprocal(st1[:, 4:5], st1[:, 4:5]), reads=[bst], writes=[bst])
        P.op("v", lambda v: v.scalar_tensor_tensor(st1[:, 5:6], st1[:, 2:3], -1.0,
                                                   st1[:, 4:5], ALU.mult, ALU.mult),
             reads=[bst], writes=[bst])
        P.op("v", lambda v: v.tensor_scalar(xnb[:, :], xbuf[:, :], st1[:, 4:5],
                                            st1[:, 5:6], ALU.mult, ALU.add),
             reads=[bxb, bst], writes=[bxnb])

    for t in range(NT):
        sl = t % 2
        P.op("v", lambda v, sl=sl: v.memset(xg[sl][:, :], 0.0), writes=[bxg[sl]])
        P.op("g", lambda g, t=t, sl=sl: g.indirect_dma_start(
                out=xg[sl][:, :], out_offset=None,
                in_=A["x_full"],
                in_offset=IndirectOffsetOnAxis(ap=gidu[:, t:t + 1], axis=0),
                bounds_check=k.breg(g, B * N - 1), oob_is_err=False),
             reads=[bgu, bxg[sl]], writes=[bxg[sl]], dma=gx)
        ln_tile(xg[sl], bxg[sl], xng[sl], bxng[sl])
        for ch in range(DC):
            q = ch % 4
            P.op("t", lambda te, ch=ch, sl=sl, q=q: te.transpose(
                    tr_ps[:, 128 * q:128 * (q + 1)],
                    xng[sl][:, 128 * ch:128 * (ch + 1)], ident[:, :]),
                 reads=[bxng[sl], bI], writes=[btr])
            P.op("v", lambda v, ch=ch, t=t, q=q: v.tensor_copy(
                    xgT[ch][:, 128 * t:128 * (t + 1)],
                    tr_ps[:, 128 * q:128 * (q + 1)]),
                 reads=[btr], writes=[bxgT[ch]])

    k.pop_scope()

    k.push_scope()
    w1c = [k.sb(f"mw1_{i}", [128, FF_MOE], BF16) for i in range(DC)]
    bw1 = [Buf(f"mw1_{i}") for i in range(DC)]
    h_sl = k.sb("mh", [128, 32 * 512], BF16); bh = Buf("mh")
    w2c = [k.sb(f"mw2_{i}", [128, D], BF16) for i in range(2)]
    bw2 = [Buf(f"mw2_{i}") for i in range(2)]
    h_ps = [k.ps(f"mhp_{i}", [128, 512], F32) for i in range(2)]
    bhp = [Buf(f"mhp_{i}") for i in range(2)]
    o_ps = [k.ps(f"mop_{i}", [128, D], F32) for i in range(2)]
    bop = [Buf(f"mop_{i}") for i in range(2)]
    eo = [k.sb(f"meo_{i}", [128, D], F32) for i in range(2)]
    beo = [Buf(f"meo_{i}") for i in range(2)]

    for e01 in range(2):
        for i in range(DC):
            P.op("g", lambda g, e01=e01, i=i: g.dma_start(
                    out=w1c[i][:, :], in_=A["moe_w1"][e01, 128 * i:128 * (i + 1), :]),
                 writes=[bw1[i]], dma=lw1)
        for sg in range(2):
            soff = 1024 * e01 + 512 * sg
            for f in range(32):
                sl = f % 2
                for kk in range(DC):
                    P.op("t", lambda t, f=f, kk=kk, sl=sl, soff=soff: t.matmul(
                            h_ps[sl][:, :], w1c[kk][:, 128 * f:128 * (f + 1)],
                            xgT[kk][:, soff:soff + 512],
                            start=(kk == 0), stop=(kk == DC - 1)),
                         reads=[bw1[kk], bxgT[kk]], writes=[bhp[sl]],
                         quiet=(kk != DC - 1))
                P.op("s", lambda s, f=f, sl=sl: s.activation(
                        h_sl[:, 512 * f:512 * (f + 1)], h_ps[sl][:, :], AF.Relu),
                     reads=[bhp[sl]], writes=[bh])
            for sp in range(2):
                for f in range(32):
                    sl = f % 2
                    P.op("g", lambda g, e01=e01, f=f, sl=sl: g.dma_start(
                            out=w2c[sl][:, :],
                            in_=A["moe_w2"][e01, 128 * f:128 * (f + 1), :]),
                         writes=[bw2[sl]], dma=lw2)
                    for hp in range(2):
                        st4 = 2 * sp + hp
                        for nn in range(2):
                            P.op("t", lambda t, f=f, st4=st4, hp=hp, nn=nn, sl=sl: t.matmul(
                                    o_ps[hp][:, 512 * nn:512 * (nn + 1)],
                                    h_sl[:, 512 * f + 128 * st4:512 * f + 128 * (st4 + 1)],
                                    w2c[sl][:, 512 * nn:512 * (nn + 1)],
                                    start=(f == 0), stop=(f == 31)),
                                 reads=[bh, bw2[sl]], writes=[bop[hp]],
                                 quiet=(nn == 0))
                for hp in range(2):
                    st4 = 2 * sp + hp
                    P.op("v", lambda v, hp=hp: v.tensor_copy(
                            eo[hp][:, :], o_ps[hp][:, :]),
                         reads=[bop[hp]], writes=[beo[hp]])
                    row = 1024 * e01 + 512 * sg + 128 * st4
                    P.op("y", lambda y, row=row, hp=hp: y.dma_start(
                            out=A["eo_d"][row:row + 128, :], in_=eo[hp][:, :]),
                         reads=[beo[hp]], dma=seo)
                    if "eo_dbg" in A:
                        P.op("y", lambda y, row=row, hp=hp: y.dma_start(
                                out=A["eo_dbg"][row:row + 128, :], in_=eo[hp][:, :]),
                             reads=[beo[hp]], dma=seo)

    beod = Buf("meod"); beag = Buf("meag")
    P.group_end([beod], seo)
    P.op("g", lambda g: g.collective_compute("AllGather", ALU.bypass,
                                             replica_groups=GRP_ALL,
                                             ins=[A["eo_d"]], outs=[A["eo_ag"]]),
         reads=[beod], writes=[beag], dma=cce)

    k.pop_scope()

    cg = [k.sb(f"mcg_{i}", [128, D], F32) for i in range(2)]
    bcg = [Buf(f"mcg_{i}") for i in range(2)]
    mo = [k.sb(f"mmo_{i}", [128, D], F32) for i in range(2)]
    bmo = [Buf(f"mmo_{i}") for i in range(2)]
    for t in range(NT):
        sl = t % 2
        P.op("v", lambda v, sl=sl: v.memset(cg[sl][:, :], 0.0), writes=[bcg[sl]])
        P.op("g", lambda g, t=t, sl=sl: g.indirect_dma_start(
                out=cg[sl][:, :], out_offset=None,
                in_=A["eo_ag"],
                in_offset=IndirectOffsetOnAxis(ap=slot_u[:, t:t + 1], axis=0),
                bounds_check=k.breg(g, B * N - 1), oob_is_err=False),
             reads=[bslu, bcg[sl], beag], writes=[bcg[sl]], dma=geo)
        P.op("v", lambda v, t=t, sl=sl: v.tensor_scalar_mul(
                mo[sl][:, :], cg[sl][:, :], rts[:, t:t + 1]),
             reads=[bcg[sl], brt], writes=[bmo[sl]])
        P.op("y", lambda y, t=t, sl=sl: y.dma_start(
                out=A["moe_d"][128 * t:128 * (t + 1), :], in_=mo[sl][:, :]),
             reads=[bmo[sl]], dma=stm)
    P.raw_wait("y", (stm, stm.count))



def emit_l1a(k, A):
    nc, P = k.nc, k.P
    ld = P.dma_sem("ald")
    lx = P.dma_sem("alx")
    sx1 = P.dma_sem("asx1")
    sxn = P.dma_sem("asxn")
    sct = P.dma_sem("asct")
    ccx = P.dma_sem("accx", step=1)
    lct = P.dma_sem("alct")
    spt = P.dma_sem("aspt")

    ldy = P.dma_sem("aldy")
    identb = k.sb("aid_b", [128, 128], BF16); bIb = Buf("aidb")
    P.op("g", lambda g: g.dma_start(out=identb[:, :], in_=A["ident"]), writes=[bIb], dma=ld)
    identf = k.sb("aid_f", [128, 128], F32); bIf = Buf("aidf")
    P.op("y", lambda y: y.dma_start(out=identf[:, :], in_=A["ident"]), writes=[bIf], dma=ldy)
    wl = [k.sb(f"awl_{i}", [128, 1536], BF16) for i in range(DC)]
    bwl = [Buf(f"awl_{i}") for i in range(DC)]
    for i in range(DC):
        P.op("g", lambda g, i=i: g.dma_start(out=wl[i][:, :],
                                             in_=A["l1a_w"][128 * i:128 * (i + 1), :]),
             writes=[bwl[i]], dma=ld)
    wo = [k.sb(f"awo_{i}", [128, D], BF16) for i in range(4)]
    bwo = [Buf(f"awo_{i}") for i in range(4)]
    for i in range(4):
        P.op("g", lambda g, i=i: g.dma_start(out=wo[i][:, :],
                                             in_=A["l1a_wo"][128 * i:128 * (i + 1), :]),
             writes=[bwo[i]], dma=ld)
    bo = k.sb("abo", [128, DC], F32); bbo = Buf("abo")
    P.op("y", lambda y: y.dma_start(out=bo[:, :], in_=A["l1a_bo"]), writes=[bbo], dma=ldy)

    xn1b = [k.sb(f"axn_{i}", [128, TOK], BF16) for i in range(DC)]
    bxn = [Buf(f"axn_{i}") for i in range(DC)]
    qe = [k.sb(f"aqe_{i}", [64, TOK], BF16) for i in range(8)]
    bqe = [Buf(f"aqe_{i}") for i in range(8)]
    ke = [k.sb(f"ake_{i}", [128, 512], BF16) for i in range(NT)]
    bke = [Buf(f"ake_{i}") for i in range(NT)]
    ve = [k.sb(f"ave_{i}", [128, 512], BF16) for i in range(NT)]
    bve = [Buf(f"ave_{i}") for i in range(NT)]
    inn = [k.sb(f"ainn_{i}", [128, TOK], BF16) for i in range(4)]
    binn = [Buf(f"ainn_{i}") for i in range(4)]
    ctxb = k.sb("actxb", [64, 512], BF16); bcxb = Buf("acxb")

    k.push_scope()
    xa = [k.sb(f"axa_{i}", [128, D], F32) for i in range(2)]
    bxa = [Buf(f"axa_{i}") for i in range(2)]
    xb = k.sb("axb", [128, D], F32); bxb2 = Buf("axb")
    xc = k.sb("axc", [128, D], F32); bxc = Buf("axc")
    xnt_b = [k.sb(f"axnt_{i}", [128, D], BF16) for i in range(2)]
    bxnt = [Buf(f"axnt_{i}") for i in range(2)]
    xnt_f = [k.sb(f"axnf_{i}", [128, D], F32) for i in range(2)]
    bxnf = [Buf(f"axnf_{i}") for i in range(2)]
    sqs = k.sb("asqs", [128, D], F32); bsq = Buf("asq")
    st1 = k.sb("ast1", [128, 8], F32); bst = Buf("ast")
    tr_pb = k.ps("atrb", [128, 512], BF16); btrb = Buf("atrb")
    tr_pf = k.ps("atrf", [128, 512], F32); btrf = Buf("atrf")
    xnf_row = [k.sb(f"axnr_{i}", [128, D], F32) for i in range(2)]
    bxnr = [Buf(f"axnr_{i}") for i in range(2)]

    for t in range(NT):
        sl = t % 2
        P.op("y", lambda y, t=t, sl=sl: y.dma_start(
                out=xa[sl][:, :], in_=A["x_own"][128 * t:128 * (t + 1), :]),
             writes=[bxa[sl]], dma=lx)
        P.op("y", lambda y, t=t: y.dma_start(
                out=xb[:, :], in_=A["a0_d"][128 * t:128 * (t + 1), :]),
             writes=[bxb2], dma=lx)
        P.op("y", lambda y, t=t: y.dma_start(
                out=xc[:, :], in_=A["moe_d"][128 * t:128 * (t + 1), :]),
             writes=[bxc], dma=lx)
        P.op("v", lambda v, sl=sl: v.tensor_tensor(xa[sl][:, :], xa[sl][:, :],
                                                   xb[:, :], ALU.add),
             reads=[bxb2, bxa[sl]], writes=[bxa[sl]])
        P.op("v", lambda v, sl=sl: v.tensor_tensor(xa[sl][:, :], xa[sl][:, :],
                                                   xc[:, :], ALU.add),
             reads=[bxc, bxa[sl]], writes=[bxa[sl]])
        P.op("y", lambda y, t=t, sl=sl: y.dma_start(
                out=A["x1_d"][128 * t:128 * (t + 1), :], in_=xa[sl][:, :]),
             reads=[bxa[sl]], dma=sx1)
        P.op("v", lambda v, sl=sl: v.reduce_sum(st1[:, 0:1], xa[sl][:, :], axis=AX.X),
             reads=[bxa[sl]], writes=[bst])
        P.op("s", lambda s, sl=sl: s.activation(sqs[:, :], xa[sl][:, :], AF.Square,
                                                accum_out=st1[:, 1:2]),
             reads=[bxa[sl]], writes=[bsq, bst])
        P.op("v", lambda v: v.tensor_scalar_mul(st1[:, 2:3], st1[:, 0:1], 1.0 / D),
             reads=[bst], writes=[bst])
        P.op("v", lambda v: v.tensor_tensor(st1[:, 3:4], st1[:, 2:3], st1[:, 2:3], ALU.mult),
             reads=[bst], writes=[bst])
        P.op("v", lambda v: v.scalar_tensor_tensor(st1[:, 3:4], st1[:, 1:2], 1.0 / D,
                                                   st1[:, 3:4], ALU.mult, ALU.subtract),
             reads=[bst], writes=[bst])
        P.op("v", lambda v: v.tensor_scalar_add(st1[:, 3:4], st1[:, 3:4], 1e-5),
             reads=[bst], writes=[bst])
        P.op("s", lambda s: s.activation(st1[:, 4:5], st1[:, 3:4], AF.Sqrt),
             reads=[bst], writes=[bst])
        P.op("v", lambda v: v.reciprocal(st1[:, 4:5], st1[:, 4:5]), reads=[bst], writes=[bst])
        P.op("v", lambda v: v.scalar_tensor_tensor(st1[:, 5:6], st1[:, 2:3], -1.0,
                                                   st1[:, 4:5], ALU.mult, ALU.mult),
             reads=[bst], writes=[bst])
        P.op("v", lambda v, sl=sl: v.tensor_scalar(xnt_b[sl][:, :], xa[sl][:, :],
                                                   st1[:, 4:5], st1[:, 5:6],
                                                   ALU.mult, ALU.add),
             reads=[bxa[sl], bst], writes=[bxnt[sl]])
        P.op("v", lambda v, sl=sl: v.tensor_scalar(xnt_f[sl][:, :], xa[sl][:, :],
                                                   st1[:, 4:5], st1[:, 5:6],
                                                   ALU.mult, ALU.add),
             reads=[bxa[sl], bst], writes=[bxnf[sl]])
        for ch in range(DC):
            q = ch % 4
            P.op("t", lambda te, ch=ch, sl=sl, q=q: te.transpose(
                    tr_pb[:, 128 * q:128 * (q + 1)],
                    xnt_b[sl][:, 128 * ch:128 * (ch + 1)], identb[:, :]),
                 reads=[bxnt[sl], bIb], writes=[btrb])
            P.op("v", lambda v, ch=ch, t=t, q=q: v.tensor_copy(
                    xn1b[ch][:, 128 * t:128 * (t + 1)],
                    tr_pb[:, 128 * q:128 * (q + 1)]),
                 reads=[btrb], writes=[bxn[ch]])
            P.op("t", lambda te, ch=ch, sl=sl, q=q: te.transpose(
                    tr_pf[:, 128 * q:128 * (q + 1)],
                    xnt_f[sl][:, 128 * ch:128 * (ch + 1)], identf[:, :]),
                 reads=[bxnf[sl], bIf], writes=[btrf])
            P.op("v", lambda v, ch=ch, sl=sl, q=q: v.tensor_copy(
                    xnf_row[sl][:, 128 * ch:128 * (ch + 1)],
                    tr_pf[:, 128 * q:128 * (q + 1)]),
                 reads=[btrf], writes=[bxnr[sl]])
        P.op("y", lambda y, t=t, sl=sl: y.dma_start(
                out=A["xn1fm_d"].rearrange("(c p) n -> p c n", p=128)[:, :, 128 * t:128 * (t + 1)],
                in_=xnf_row[sl][:, :].rearrange("p (c n) -> p c n", n=128)),
             reads=[bxnr[sl]], dma=sxn)
    k.pop_scope()

    k.push_scope()
    pj = [k.ps(f"apj_{i}", [128, 512], F32) for i in range(2)]
    bpj = [Buf(f"apj_{i}") for i in range(2)]
    kv_ps = [k.ps(f"akv_{i}", [128, D], F32) for i in range(2)]
    bkv = [Buf(f"akv_{i}") for i in range(2)]
    ea = k.sb("aea", [128, 512], F32); bea = Buf("aea")
    eb = k.sb("aeb", [128, 512], F32); beb = Buf("aeb")
    ec = k.sb("aec", [128, 512], F32); bec = Buf("aec")
    for m in range(4):
        for n in range(4):
            sl = n % 2
            for kk in range(DC):
                P.op("t", lambda te, m=m, n=n, kk=kk, sl=sl: te.matmul(
                        pj[sl][:, :], wl[kk][:, 128 * m:128 * (m + 1)],
                        xn1b[kk][:, 512 * n:512 * (n + 1)],
                        start=(kk == 0), stop=(kk == DC - 1)),
                     reads=[bwl[kk], bxn[kk]], writes=[bpj[sl]], quiet=(kk != DC - 1))
            P.op("v", lambda v, sl=sl: v.tensor_scalar_min(ea[:, :], pj[sl][:, :], 0.0),
                 reads=[bpj[sl]], writes=[bea])
            P.op("v", lambda v, sl=sl: v.tensor_scalar_max(eb[:, :], pj[sl][:, :], 0.0),
                 reads=[bpj[sl]], writes=[beb])
            P.op("s", lambda s: s.activation(ea[:, :], ea[:, :], AF.Exp),
                 reads=[bea], writes=[bea])
            for hh in range(2):
                P.op("v", lambda v, m=m, n=n, hh=hh: v.tensor_tensor(
                        qe[2 * m + hh][:, 512 * n:512 * (n + 1)],
                        ea[64 * hh:64 * (hh + 1), :], eb[64 * hh:64 * (hh + 1), :],
                        ALU.add),
                     reads=[bea, beb], writes=[bqe[2 * m + hh]])
    for t in range(NT):
        sl = t % 2
        for nn in range(2):
            for kk in range(DC):
                P.op("t", lambda te, t=t, nn=nn, kk=kk, sl=sl: te.matmul(
                        kv_ps[sl][:, 512 * nn:512 * (nn + 1)],
                        xn1b[kk][:, 128 * t:128 * (t + 1)],
                        wl[kk][:, 512 + 512 * nn:512 + 512 * (nn + 1)],
                        start=(kk == 0), stop=(kk == DC - 1)),
                     reads=[bxn[kk], bwl[kk]], writes=[bkv[sl]],
                     quiet=not (kk == DC - 1 and nn == 1))
        P.op("v", lambda v, t=t, sl=sl: v.tensor_scalar_min(ea[:, :], kv_ps[sl][:, 0:512], 0.0),
             reads=[bkv[sl]], writes=[bea])
        P.op("v", lambda v, t=t, sl=sl: v.tensor_scalar_max(eb[:, :], kv_ps[sl][:, 0:512], 0.0),
             reads=[bkv[sl]], writes=[beb])
        P.op("s", lambda s: s.activation(ea[:, :], ea[:, :], AF.Exp),
             reads=[bea], writes=[bea])
        P.op("v", lambda v, t=t: v.tensor_tensor(ke[t][:, :], ea[:, :], eb[:, :], ALU.add),
             reads=[bea, beb], writes=[bke[t]])
        P.op("v", lambda v, t=t, sl=sl: v.tensor_copy(ve[t][:, :], kv_ps[sl][:, 512:1024]),
             reads=[bkv[sl]], writes=[bve[t]])

    ctx_ps = k.ps("actx", [64, 512], F32); bctx = Buf("actx")
    for t in range(NT):
        for h in range(8):
            P.op("t", lambda te, t=t, h=h: te.matmul(
                    ctx_ps[0:64, 64 * h:64 * (h + 1)],
                    ke[t][:, 64 * h:64 * (h + 1)], ve[t][:, 64 * h:64 * (h + 1)],
                    start=(t == 0 and h == 0), stop=(t == NT - 1 and h == 7)),
                 reads=[bke[t], bve[t]], writes=[bctx],
                 quiet=not (t == NT - 1 and h == 7))
    ctxs = k.sb("actxs", [64, 512], F32); bcxs = Buf("acxs")
    P.op("v", lambda v: v.tensor_copy(ctxs[:, :], ctx_ps[0:64, :]), reads=[bctx], writes=[bcxs])
    P.op("y", lambda y: y.dma_start(out=A["ctx_in"], in_=ctxs[:, :]), reads=[bcxs], dma=sct)
    bcin = Buf("acin"); bcar = Buf("acar")
    P.group_end([bcin], sct)
    P.op("g", lambda g: g.collective_compute("AllReduce", ALU.add,
                                             replica_groups=GRP_BATCH,
                                             ins=[A["ctx_in"]], outs=[A["ctx_ar"]]),
         reads=[bcin], writes=[bcar], dma=ccx)
    P.op("g", lambda g: g.dma_start(out=ctxb[:, :], in_=A["ctx_ar"]),
         reads=[bcar], writes=[bcxb], dma=lct)
    k.pop_scope()

    k.push_scope()
    oc_ps = [k.ps(f"aoc_{i}", [64, 512], F32) for i in range(2)]
    boc = [Buf(f"aoc_{i}") for i in range(2)]
    for h in range(8):
        for n in range(4):
            sl = n % 2
            P.op("t", lambda te, h=h, n=n, sl=sl: te.matmul(
                    oc_ps[sl][0:64, :], ctxb[0:64, 64 * h:64 * (h + 1)],
                    qe[h][:, 512 * n:512 * (n + 1)],
                    start=True, stop=True),
                 reads=[bcxb, bqe[h]], writes=[boc[sl]])
            P.op("v", lambda v, h=h, n=n, sl=sl: v.tensor_copy(
                    inn[h // 2][64 * (h % 2):64 * (h % 2) + 64, 512 * n:512 * (n + 1)],
                    oc_ps[sl][0:64, :]),
                 reads=[boc[sl]], writes=[binn[h // 2]])
    po = [k.ps(f"apo_{i}", [128, 512], F32) for i in range(2)]
    bpo = [Buf(f"apo_{i}") for i in range(2)]
    pst = [k.sb(f"apst_{i}", [128, 512], F32) for i in range(2)]
    bpst = [Buf(f"apst_{i}") for i in range(2)]
    for m in range(DC):
        for n in range(4):
            sl = n % 2
            for kc in range(4):
                P.op("t", lambda te, m=m, n=n, kc=kc, sl=sl: te.matmul(
                        po[sl][:, :], wo[kc][:, 128 * m:128 * (m + 1)],
                        inn[kc][:, 512 * n:512 * (n + 1)],
                        start=(kc == 0), stop=(kc == 3)),
                     reads=[bwo[kc], binn[kc]], writes=[bpo[sl]], quiet=(kc != 3))
            P.op("v", lambda v, m=m, sl=sl: v.tensor_scalar_add(
                    pst[sl][:, :], po[sl][:, :], bo[:, m:m + 1]),
                 reads=[bpo[sl], bbo], writes=[bpst[sl]])
            P.op("y", lambda y, m=m, n=n, sl=sl: y.dma_start(
                    out=A["part_d"][128 * m:128 * (m + 1), 512 * n:512 * (n + 1)],
                    in_=pst[sl][:, :]),
                 reads=[bpst[sl]], dma=spt)
    k.pop_scope()
    P.raw_wait("y", (spt, spt.count))
    P.raw_wait("y", (sx1, sx1.count))
    P.raw_wait("y", (sxn, sxn.count))


def build_launch_l1a():
    nc = bass.Bass("TRN2", target_bir_lowering=False)
    P = Plan()
    ctx = contextlib.ExitStack()
    k = K(nc, ctx, P)
    A = {}
    A["x_own"] = k.inp("x_own", [TOK, D])
    A["a0_d"] = k.inp("a0_d", [TOK, D])
    A["moe_d"] = k.inp("moe_d", [TOK, D])
    A["l1a_w"] = k.inp("l1a_w", [D, 1536])
    A["l1a_wo"] = k.inp("l1a_wo", [512, D])
    A["l1a_bo"] = k.inp("l1a_bo", [128, DC])
    A["ident"] = k.inp("ident", [128, 128])
    A["x1_d"] = k.outp("x1_d", [TOK, D])
    A["xn1fm_d"] = k.outp("xn1fm_d", [D, TOK])
    A["part_d"] = k.outp("part_d", [D, TOK])
    A["ctx_in"] = k.dram("ctx_in", [64, 512])
    A["ctx_ar"] = k.dram("ctx_ar", [64, 512])
    with ctx:
        emit_l1a(k, A)
        with nc.Block() as block:
            P.emit(nc, ctx, block)
    return nc, ["x_own", "a0_d", "moe_d", "l1a_w", "l1a_wo", "l1a_bo", "ident"], \
        ["x1_d", "xn1fm_d", "part_d"]




SDH = HALO


def emit_l1f(k, A):
    nc, P = k.nc, k.P
    ld = P.dma_sem("fld")
    ldg2 = P.dma_sem("fldg2")
    lhb = P.dma_sem("flhb")
    lwq = P.dma_sem("flwq")
    lwo = P.dma_sem("flwo")
    lw1 = P.dma_sem("flw1")
    lw2 = P.dma_sem("flw2")
    ltb = P.dma_sem("fltb")
    shal = P.dma_sem("fshal")
    cch = P.dma_sem("fcch", step=1)
    ghal = P.dma_sem("fghal")
    swr = P.dma_sem("fswr")
    lsd = P.dma_sem("flsd")
    lres = P.dma_sem("flres")
    sy = P.dma_sem("fsy")
    lfin = P.dma_sem("flfin")

    identb = k.sb("fid_b", [128, 128], BF16); bIb = Buf("fidb")
    P.op("g", lambda g: g.dma_start(out=identb[:, :], in_=A["ident"]), writes=[bIb], dma=ldg2)
    identf = k.sb("fid_f", [128, 128], F32); bIf = Buf("fidf")
    P.op("y", lambda y: y.dma_start(out=identf[:, :], in_=A["ident"]), writes=[bIf], dma=ld)
    onesr = k.sb("fonesr", [1, 128], F32); bor = Buf("fonesr")
    P.op("y", lambda y: y.dma_start(out=onesr[:, :], in_=A["ones_row"]), writes=[bor], dma=ld)
    onesc = k.sb("fonesc", [128, 1], F32); boc = Buf("fonesc")
    P.op("y", lambda y: y.dma_start(out=onesc[:, :], in_=A["ones_col"]), writes=[boc], dma=ld)
    permf = k.sb("fperm", [128, 128], F32); bpm = Buf("fperm")
    P.op("y", lambda y: y.dma_start(out=permf[:, :], in_=A["perm"]), writes=[bpm], dma=ld)
    hself = k.sb("fhsel", [128, DC * 16], F32); bhs = Buf("fhsel")
    P.op("y", lambda y: y.dma_start(out=hself[:, :], in_=A["hsel"]), writes=[bhs], dma=ld)
    hbcf = k.sb("fhbc", [16, DC * 128], F32); bhb = Buf("fhbc")
    P.op("y", lambda y: y.dma_start(out=hbcf[:, :], in_=A["hbc"]), writes=[bhb], dma=ld)
    msk = k.sb("fmsk", [128, 5, 512], BF16); bmk = Buf("fmsk")
    P.op("g", lambda g: g.dma_start(out=msk[:, 0:2, :],
                                    in_=A["msk_w0"].rearrange("a p w -> p a w")),
         writes=[bmk], dma=ldg2)
    P.op("g", lambda g: g.dma_start(out=msk[:, 2:4, :],
                                    in_=A["msk_int"].rearrange("a p w -> p a w")),
         writes=[bmk], dma=ldg2)
    P.op("g", lambda g: g.dma_start(out=msk[:, 4:5, :], in_=A["msk_ext"].unsqueeze(1)),
         writes=[bmk], dma=ldg2)
    hgate = k.sb("fhg", [128, 1], F32); bhg = Buf("fhg")
    P.op("y", lambda y: y.dma_start(out=hgate[:, :], in_=A["halo_gate"]), writes=[bhg], dma=ld)
    hrows = k.sb("fhrows", [128, DC], F32); bhr = Buf("fhrows")
    P.op("y", lambda y: y.dma_start(out=hrows[:, :], in_=A["halo_rows"]), writes=[bhr], dma=ld)
    hrowu = k.sb("fhrowu", [128, DC], U32); bhru = Buf("fhrowu")
    P.op("v", lambda v: v.tensor_copy(hrowu[:, :], hrows[:, :]), reads=[bhr], writes=[bhru])

    sd = A["sd"]
    bsd = Buf("fsd")
    bhal = Buf("fhal")
    bhag = Buf("fhag")

    pq = k.sb("fpq", [128, 128], BF16); bpq = Buf("fpq")
    pk = k.sb("fpk", [128, 128], BF16); bpk = Buf("fpk")
    hbw = k.sb("fhbw", [16, DC * 128], F32); bhw = Buf("fhbw")
    b1c = k.sb("fb1c", [128, F1P // 128], F32); bb1 = Buf("fb1c")
    b2c = k.sb("fb2c", [128, DC], F32); bb2 = Buf("fb2c")
    lngc = k.sb("flng", [128, DC], F32); blg = Buf("flng")
    lnbc = k.sb("flnb", [128, DC], F32); blb = Buf("flnb")
    qsc = k.sb("fqsc", [128, 1], F32); bqsc = Buf("fqsc")
    ksc = k.sb("fksc", [128, 1], F32); bksc = Buf("fksc")
    qsr = k.sb("fqsr", [1, 128], F32); bqsr = Buf("fqsr")

    k.push_scope()
    tmpc = [k.sb(f"ftmp_{i}", [128, TOK], F32) for i in range(2)]
    btmp = [Buf(f"ftmp_{i}") for i in range(2)]
    for ch in range(DC):
        sl = ch % 2
        P.op("y", lambda y, ch=ch, sl=sl: y.dma_start(
                out=tmpc[sl][:, :], in_=A["xn1fm_d"][128 * ch:128 * (ch + 1), :]),
             writes=[btmp[sl]], dma=lfin)
        P.op("y", lambda y, ch=ch, sl=sl: y.dma_start(
                out=sd[128 * ch:128 * (ch + 1), SDH:SDH + TOK], in_=tmpc[sl][:, :]),
             reads=[btmp[sl]], writes=[bsd], dma=swr)
    k.pop_scope(fence=False)

    for li in range(_L1F_DEPTH):
        k.push_scope()
        qs_bc = k.ps("fqsbc", [128, 128], F32); bqsb = Buf("fqsbc")
        P.op("y", lambda y, li=li: y.dma_start(out=qsc[:, :], in_=A["f_qs_c"][li]),
             writes=[bqsc], dma=ld)
        P.op("y", lambda y, li=li: y.dma_start(out=ksc[:, :], in_=A["f_ks_c"][li]),
             writes=[bksc], dma=ld)
        P.op("v", lambda v: v.tensor_scalar_mul(pq[:, :], permf[:, :], qsc[:, 0:1]),
             reads=[bpm, bqsc], writes=[bpq])
        P.op("v", lambda v: v.tensor_scalar_mul(pk[:, :], permf[:, :], ksc[:, 0:1]),
             reads=[bpm, bksc], writes=[bpk])
        P.op("y", lambda y, li=li: y.dma_start(out=qsr[:, :], in_=A["f_ks"][li]),
             writes=[bqsr], dma=ld)
        P.op("t", lambda t: t.matmul(qs_bc[0:16, :], onesr[0:1, 0:16], qsr[:, :],
                                     start=True, stop=True),
             reads=[bor, bqsr], writes=[bqsb])
        P.op("v", lambda v: v.tensor_tensor(
                hbw[:, :].rearrange("h (c n) -> h c n", n=128),
                hbcf[:, :].rearrange("h (c n) -> h c n", n=128),
                qs_bc[0:16, :].unsqueeze(1).broadcast_to([16, DC, 128]), ALU.mult),
             reads=[bhb, bqsb], writes=[bhw])
        P.op("y", lambda y, li=li: y.dma_start(out=b1c[:, :], in_=A["f_b1p"][li]),
             writes=[bb1], dma=ld)
        P.op("y", lambda y, li=li: y.dma_start(out=b2c[:, :], in_=A["f_b2"][li]),
             writes=[bb2], dma=ld)
        P.op("y", lambda y, li=li: y.dma_start(out=lngc[:, :], in_=A["f_lng"][li]),
             writes=[blg], dma=ld)
        P.op("y", lambda y, li=li: y.dma_start(out=lnbc[:, :], in_=A["f_lnb"][li]),
             writes=[blb], dma=ld)
        k.pop_scope()

        k.push_scope()
        hb = [k.sb(f"fhb_{i}", [128, SDH], F32) for i in range(2)]
        bhbb = [Buf(f"fhb_{i}") for i in range(2)]
        for ch in range(DC):
            sl = ch % 2
            P.op("y", lambda y, ch=ch, sl=sl: y.dma_start(
                    out=hb[sl][:, :], in_=sd[128 * ch:128 * (ch + 1), TOK:TOK + SDH]),
                 reads=[bsd], writes=[bhbb[sl]], dma=lhb)
            P.op("y", lambda y, ch=ch, sl=sl: y.dma_start(
                    out=A["hal_in"][128 * ch:128 * (ch + 1), :], in_=hb[sl][:, :]),
                 reads=[bhbb[sl]], writes=[bhal], dma=shal)
        P.op("g", lambda g: g.collective_compute("AllGather", ALU.bypass,
                                                 replica_groups=GRP_BATCH,
                                                 ins=[A["hal_in"]], outs=[A["hal_ag"]]),
             reads=[bhal], writes=[bhag], dma=cch)
        for ch in range(DC):
            sl = ch % 2
            P.op("v", lambda v, sl=sl: v.memset(hb[sl][:, :], 0.0), writes=[bhbb[sl]])
            P.op("g", lambda g, ch=ch, sl=sl: g.indirect_dma_start(
                    out=hb[sl][:, :], out_offset=None,
                    in_=A["hal_ag"],
                    in_offset=IndirectOffsetOnAxis(ap=hrowu[:, ch:ch + 1], axis=0),
                    bounds_check=k.breg(g, 4 * D - 1), oob_is_err=False),
                 reads=[bhru, bhag, bhbb[sl]], writes=[bhbb[sl]], dma=ghal)
            P.op("v", lambda v, sl=sl: v.tensor_scalar_mul(hb[sl][:, :], hb[sl][:, :],
                                                           hgate[:, 0:1]),
                 reads=[bhg, bhbb[sl]], writes=[bhbb[sl]])
            P.op("y", lambda y, ch=ch, sl=sl: y.dma_start(
                    out=sd[128 * ch:128 * (ch + 1), 0:SDH], in_=hb[sl][:, :]),
                 reads=[bhbb[sl]], writes=[bsd], dma=swr)
        k.pop_scope()

        k.push_scope()
        wq = [k.sb(f"fwq_{i}", [128, 3072], BF16) for i in range(DC)]
        bwq = [Buf(f"fwq_{i}") for i in range(DC)]
        for i in range(DC):
            P.op("g", lambda g, i=i, li=li: g.dma_start(
                    out=wq[i][:, :], in_=A["f_wqkv"][li, 128 * i:128 * (i + 1), :]),
                 writes=[bwq[i]], dma=lwq)
        wo = [k.sb(f"fwo_{i}", [128, D], BF16) for i in range(DC)]
        bwo = [Buf(f"fwo_{i}") for i in range(DC)]
        for i in range(DC):
            P.op("g", lambda g, i=i, li=li: g.dma_start(
                    out=wo[i][:, :], in_=A["f_wo"][li, 128 * i:128 * (i + 1), :]),
                 writes=[bwo[i]], dma=lwo)

        xshw = [k.sb(f"fxsh_{i}", [128, WIN], BF16) for i in range(DC)]
        bxw = [Buf(f"fxsh_{i}") for i in range(DC)]
        qsb = [[k.sb(f"fqsb_{p}_{i}", [128, WIN], BF16) for i in range(DC)]
               for p in range(2)]
        bqs = [[Buf(f"fqsb_{p}_{i}") for i in range(DC)] for p in range(2)]
        ksl = [[k.sb(f"fksl_{p}_{i}", [128, WIN], BF16) for i in range(DC)]
               for p in range(3)]
        bks = [[Buf(f"fksl_{p}_{i}") for i in range(DC)] for p in range(3)]
        vsl = [[k.sb(f"fvsl_{p}_{i}", [128, D], BF16) for i in range(2)]
               for p in range(3)]
        bvs = [[Buf(f"fvsl_{p}_{i}") for i in range(2)] for p in range(3)]
        tabs = k.sb("ftabs", [128, 4, WIN], F32); btab = Buf("ftabs")
        rot1 = k.sb("frot1", [128, WIN], F32); br1 = Buf("frot1")
        rot3 = k.sb("frot3", [128, WIN], F32); br3 = Buf("frot3")
        sqsc = k.sb("fsqsc", [128, WIN], F32); bsqc = Buf("fsqsc")
        nrm = k.sb("fnrm", [16, WIN], F32); bnr = Buf("fnrm")
        rkf = k.sb("frkf", [16, WIN], F32); brk = Buf("frkf")
        rqt = [k.sb(f"frqt_{p}", [128, 2, 16], F32) for p in range(2)]
        brq = [Buf(f"frqt_{p}") for p in range(2)]
        stt = k.sb("fstt", [128, 8], F32); bstt = Buf("fstt")
        Pw = k.sb("fPw", [128, 512], BF16); bPw = Buf("fPw")
        PTs = k.sb("fPTs", [128, 512], BF16); bPT = Buf("fPTs")
        aosb = k.sb("faosb", [128, D], BF16); bao = Buf("faosb")
        aoT = [k.sb(f"faoT_{i}", [128, 128], BF16) for i in range(DC)]
        baoT = [Buf(f"faoT_{i}") for i in range(DC)]
        res = [k.sb(f"fres_{i}", [128, WIN], F32) for i in range(DC)]
        bres = [Buf(f"fres_{i}") for i in range(DC)]

        pj_ps = [k.ps(f"fpj_{i}", [128, 512], F32) for i in range(2)]
        bpj = [Buf(f"fpj_{i}") for i in range(2)]
        rh_ps = k.ps("frh", [128, WIN], F32); brh = Buf("frh")
        n2_ps = rh_ps; bn2 = brh
        rb_ps = rh_ps; brb = brh
        s_ps = k.ps("fs", [128, 512], F32); bS = Buf("fs")
        pt_ps = k.ps("fpt", [128, 512], BF16); bpt = Buf("fpt")
        av_ps = k.ps("fav", [128, D], F32); bav = Buf("fav")

        def load_xshw(w):
            for ch in range(DC):
                c0 = SDH + WIN * w - (1 if ch >= 4 else 0)
                P.op("g", lambda g, ch=ch, c0=c0: g.dma_start(
                        out=xshw[ch][:, :], in_=sd[128 * ch:128 * (ch + 1), c0:c0 + WIN]),
                     reads=[bsd], writes=[bxw[ch]], dma=lsd)

        def load_tabs(w):
            c0 = SDH + WIN * w
            for a, nm in enumerate(["rot_cs", "rot_ss", "rot_ci", "rot_si"]):
                P.op("y", lambda y, c0=c0, a=a, nm=nm: y.dma_start(
                        out=tabs[:, a, :], in_=A[nm][:, c0:c0 + WIN]),
                     writes=[btab], dma=ltb)

        def qkv_part(dst, bdst, col0):
            for m in range(DC):
                sl = m % 2
                for kk in range(DC):
                    P.op("t", lambda t, m=m, kk=kk, sl=sl, col0=col0: t.matmul(
                            pj_ps[sl][:, 0:WIN],
                            wq[kk][:, col0 + 128 * m:col0 + 128 * (m + 1)],
                            xshw[kk][:, :], start=(kk == 0), stop=(kk == DC - 1)),
                         reads=[bwq[kk], bxw[kk]], writes=[bpj[sl]], quiet=(kk != DC - 1))
                P.op("v", lambda v, m=m, sl=sl: v.tensor_copy(dst[m][:, :],
                                                              pj_ps[sl][:, 0:WIN]),
                     reads=[bpj[sl]], writes=[bdst[m]])

        def rotary(dst, bdst, permb, bpermb, scol, bscol, ic, isn, is_k, rqslot):
            for ch in range(DC):
                P.op("s", lambda s, ch=ch: s.activation(sqsc[:, :], dst[ch][:, :], AF.Square),
                     reads=[bdst[ch]], writes=[bsqc])
                P.op("t", lambda t, ch=ch: t.matmul(
                        n2_ps[0:16, :], hself[:, 16 * ch:16 * (ch + 1)], sqsc[:, :],
                        start=(ch == 0), stop=(ch == DC - 1)),
                     reads=[bhs, bsqc], writes=[bn2])
            P.op("s", lambda s: s.activation(nrm[:, :], n2_ps[0:16, :], AF.Sqrt),
                 reads=[bn2], writes=[bnr])
            P.op("v", lambda v: v.tensor_scalar_max(nrm[:, :], nrm[:, :], 1e-12),
                 reads=[bnr], writes=[bnr])
            if is_k:
                P.op("v", lambda v: v.reciprocal(rkf[:, :], nrm[:, :]),
                     reads=[bnr], writes=[brk])
            else:
                for q2 in range(2):
                    P.op("t", lambda t, q2=q2: t.transpose(
                            rb_ps[:, 0:16], nrm[0:16, 128 * q2:128 * (q2 + 1)],
                            identf[0:16, 0:16]),
                         reads=[bnr, bIf], writes=[brb])
                    P.op("v", lambda v, q2=q2, rqslot=rqslot: v.tensor_copy(
                            rqt[rqslot][:, q2, :], rb_ps[:, 0:16]),
                         reads=[brb], writes=[brq[rqslot]])
                P.op("v", lambda v, rqslot=rqslot: v.reciprocal(
                        rqt[rqslot][:, :, :], rqt[rqslot][:, :, :]),
                     reads=[brq[rqslot]], writes=[brq[rqslot]])
                P.op("v", lambda v, rqslot=rqslot: v.tensor_scalar_mul(
                        rqt[rqslot][:, :, :], rqt[rqslot][:, :, :], 8.0),
                     reads=[brq[rqslot]], writes=[brq[rqslot]])
                P.op("v", lambda v, rqslot=rqslot: v.tensor_scalar_min(
                        rqt[rqslot][:, :, :], rqt[rqslot][:, :, :], 1e6),
                     reads=[brq[rqslot]], writes=[brq[rqslot]])
            for ch in range(DC):
                P.op("t", lambda t, ch=ch: t.matmul(rh_ps[:, :], permb[:, :], dst[ch][:, :],
                                                    start=True, stop=True),
                     reads=[bpermb, bdst[ch]], writes=[brh])
                P.op("v", lambda v, ch=ch, ic=ic: v.scalar_tensor_tensor(
                        rot1[:, :], dst[ch][:, :], scol[:, 0:1], tabs[:, ic, :],
                        ALU.mult, ALU.mult),
                     reads=[bdst[ch], bscol, btab], writes=[br1])
                P.op("v", lambda v, ch=ch, isn=isn: v.tensor_tensor(
                        rot3[:, :], rh_ps[:, :], tabs[:, isn, :], ALU.mult),
                     reads=[brh, btab], writes=[br3])
                if is_k:
                    P.op("v", lambda v: v.tensor_tensor(rot1[:, :], rot1[:, :],
                                                        rot3[:, :], ALU.add),
                         reads=[br3, br1], writes=[br1])
                    P.op("t", lambda t, ch=ch: t.matmul(
                            rb_ps[:, :], hbw[0:16, 128 * ch:128 * (ch + 1)], rkf[:, :],
                            start=True, stop=True),
                         reads=[bhw, brk], writes=[brb])
                    P.op("v", lambda v, ch=ch: v.tensor_tensor(
                            dst[ch][:, :], rot1[:, :], rb_ps[:, :], ALU.mult),
                         reads=[br1, brb, bdst[ch]], writes=[bdst[ch]])
                else:
                    P.op("v", lambda v, ch=ch: v.tensor_tensor(
                            dst[ch][:, :], rot1[:, :], rot3[:, :], ALU.add),
                         reads=[br3, br1, bdst[ch]], writes=[bdst[ch]])

        def stage(w):
            load_xshw(w)
            load_tabs(w)
            p3, p2 = w % 3, w % 2
            qkv_part(ksl[p3], bks[p3], D)
            rotary(ksl[p3], bks[p3], pk, bpk, ksc, bksc, 2, 3, True, None)
            if w >= -1:
                qkv_part(qsb[p2], bqs[p2], 0)
                rotary(qsb[p2], bqs[p2], pq, bpq, qsc, bqsc, 0, 1, False, p2)
            for t2 in range(2):
                sl = t2 % 2
                for nn in range(2):
                    for kk in range(DC):
                        P.op("t", lambda t, t2=t2, nn=nn, kk=kk, sl=sl: t.matmul(
                                pj_ps[sl][:, :],
                                xshw[kk][:, 128 * t2:128 * (t2 + 1)],
                                wq[kk][:, 2 * D + 512 * nn:2 * D + 512 * (nn + 1)],
                                start=(kk == 0), stop=(kk == DC - 1)),
                             reads=[bxw[kk], bwq[kk]], writes=[bpj[sl]],
                             quiet=(kk != DC - 1))
                    P.op("v", lambda v, t2=t2, nn=nn, sl=sl, p3=p3: v.tensor_copy(
                            vsl[p3][t2][:, 512 * nn:512 * (nn + 1)], pj_ps[sl][:, :]),
                         reads=[bpj[sl]], writes=[bvs[p3][t2]])

        def attn_tile(w, qt2, mski, extra):
            parp, parc = (w - 1) % 3, w % 3
            p2 = w % 2
            for h in range(HEADS1):
                ch, hh = h // 2, h % 2
                r0 = 64 * hh
                P.op("t", lambda t, ch=ch, r0=r0, qt2=qt2, p2=p2, parp=parp: t.matmul(
                        s_ps[:, 0:WIN],
                        qsb[p2][ch][r0:r0 + 64, 128 * qt2:128 * (qt2 + 1)],
                        ksl[parp][ch][r0:r0 + 64, :], start=True, stop=False),
                     reads=[bqs[p2][ch], bks[parp][ch]], writes=[bS])
                P.op("t", lambda t, ch=ch, r0=r0, qt2=qt2, p2=p2, parc=parc: t.matmul(
                        s_ps[:, WIN:2 * WIN],
                        qsb[p2][ch][r0:r0 + 64, 128 * qt2:128 * (qt2 + 1)],
                        ksl[parc][ch][r0:r0 + 64, :], start=False, stop=False),
                     reads=[bqs[p2][ch], bks[parc][ch]], writes=[bS])
                P.op("t", lambda t, mski=mski: t.matmul(
                        s_ps[:, :], identb[:, :], msk[:, mski, :],
                        start=False, stop=True),
                     reads=[bIb, bmk], writes=[bS])
                P.op("v", lambda v: v.reduce_max(stt[:, 0:1], s_ps[:, :], axis=AX.X),
                     reads=[bS], writes=[bstt])
                P.op("v", lambda v, h=h, qt2=qt2, p2=p2: v.scalar_tensor_tensor(
                        stt[:, 1:2], rqt[p2][:, qt2, h:h + 1], -1.0, stt[:, 0:1],
                        ALU.mult, ALU.mult),
                     reads=[brq[p2], bstt], writes=[bstt])
                P.op("s", lambda s, h=h, qt2=qt2, p2=p2: s.activation(
                        Pw[:, :], s_ps[:, :], AF.Exp, bias=stt[:, 1:2],
                        scale=rqt[p2][:, qt2, h:h + 1], accum_out=stt[:, 2:3]),
                     reads=[bS, bstt, brq[p2]], writes=[bPw, bstt])
                P.op("v", lambda v: v.reciprocal(stt[:, 3:4], stt[:, 2:3]),
                     reads=[bstt], writes=[bstt])
                P.op("v", lambda v: v.tensor_scalar_mul(Pw[:, :], Pw[:, :], stt[:, 3:4]),
                     reads=[bstt, bPw], writes=[bPw])
                for u in range(4):
                    P.op("t", lambda t, u=u: t.transpose(
                            pt_ps[:, 128 * u:128 * (u + 1)],
                            Pw[:, 128 * u:128 * (u + 1)], identb[:, :]),
                         reads=[bPw, bIb], writes=[bpt], quiet=(u != 3))
                P.op("v", lambda v: v.tensor_copy(PTs[:, :], pt_ps[:, :]),
                     reads=[bpt], writes=[bPT])
                for u in range(4):
                    par = parp if u < 2 else parc
                    t2 = u % 2
                    P.op("t", lambda t, u=u, h=h, par=par, t2=t2: t.matmul(
                            av_ps[:, 64 * h:64 * (h + 1)], PTs[:, 128 * u:128 * (u + 1)],
                            vsl[par][t2][:, 64 * h:64 * (h + 1)],
                            start=(h % 8 == 0 and u == 0), stop=(h % 8 == 7 and u == 3)),
                         reads=[bPT, bvs[par][t2]], writes=[bav],
                         quiet=not (h == HEADS1 - 1 and u == 3))
            P.op("v", lambda v: v.tensor_copy(aosb[:, :], av_ps[:, :]),
                 reads=[bav], writes=[bao])
            for ch in range(DC):
                P.op("t", lambda t, ch=ch: t.transpose(
                        pt_ps[:, 0:128], aosb[:, 128 * ch:128 * (ch + 1)], identb[:, :]),
                     reads=[bao, bIb], writes=[bpt])
                P.op("v", lambda v, ch=ch: v.tensor_copy(aoT[ch][:, :], pt_ps[:, 0:128]),
                     reads=[bpt], writes=[baoT[ch]])
            for m in range(DC):
                for kk in range(DC):
                    P.op("t", lambda t, m=m, kk=kk: t.matmul(
                            pj_ps[0][:, 0:128], wo[kk][:, 128 * m:128 * (m + 1)],
                            aoT[kk][:, :], start=(kk == 0), stop=(kk == DC - 1)),
                         reads=[bwo[kk], baoT[kk]], writes=[bpj[0]], quiet=(kk != DC - 1))
                if not extra:
                    c0 = SDH + WIN * w + 128 * qt2
                    P.op("y", lambda y, m=m, c0=c0: y.dma_start(
                            out=res[m][:, 0:128], in_=sd[128 * m:128 * (m + 1), c0:c0 + 128]),
                         reads=[bsd], writes=[bres[m]], dma=lres)
                    P.op("v", lambda v, m=m: v.tensor_tensor(
                            res[m][:, 0:128], res[m][:, 0:128], pj_ps[0][:, 0:128], ALU.add),
                         reads=[bpj[0], bres[m]], writes=[bres[m]])
                    P.op("y", lambda y, m=m, c0=c0: y.dma_start(
                            out=sd[128 * m:128 * (m + 1), c0:c0 + 128], in_=res[m][:, 0:128]),
                         reads=[bres[m]], writes=[bsd], dma=swr)
                else:
                    P.op("y", lambda y, m=m: y.dma_start(
                            out=res[m][:, 0:1], in_=sd[128 * m:128 * (m + 1), SDH - 1:SDH],
                            allow_slow_non_contiguous=True),
                         reads=[bsd], writes=[bres[m]], dma=lres)
                    P.op("v", lambda v, m=m: v.tensor_tensor(
                            res[m][:, 0:1], res[m][:, 0:1], pj_ps[0][:, 127:128], ALU.add),
                         reads=[bpj[0], bres[m]], writes=[bres[m]])
                    P.op("y", lambda y, m=m: y.dma_start(
                            out=sd[128 * m:128 * (m + 1), SDH - 1:SDH], in_=res[m][:, 0:1],
                            allow_slow_non_contiguous=True),
                         reads=[bres[m]], writes=[bsd], dma=swr)

        stage(-2)
        stage(-1)
        stage(0)
        attn_tile(-1, 1, 4, True)
        for w in range(NWIN):
            if w + 1 < NWIN:
                stage(w + 1)
            mski0 = 0 if w == 0 else 2
            attn_tile(w, 0, mski0, False)
            attn_tile(w, 1, mski0 + 1, False)
        k.pop_scope()

        k.push_scope()
        xnb = [k.sb(f"fxnb_{i}", [128, TOK], BF16) for i in range(DC)]
        bxnb = [Buf(f"fxnb_{i}") for i in range(DC)]
        aG = k.sb("faG", [128, (F2P // 128) * TOK], BF16); baG = Buf("faG")
        xsh = [k.sb(f"fxshf_{i}", [128, 512], BF16) for i in range(2)]
        bxsh = [Buf(f"fxshf_{i}") for i in range(2)]
        x2t = k.sb("fx2t", [128, 512], BF16); bx2 = Buf("fx2t")
        srow = k.sb("fsrow", [1, 4, TOK], F32); bsr = Buf("fsrow")
        ab_row = k.sb("fabrow", [1, 2, TOK], F32); babr = Buf("fabrow")
        w1b = [k.sb(f"fw1b_{i}", [128, DC, 256], BF16) for i in range(2)]
        bw1b = [Buf(f"fw1b_{i}") for i in range(2)]
        w2b = [k.sb(f"fw2b_{i}", [128, D], BF16) for i in range(2)]
        bw2b = [Buf(f"fw2b_{i}") for i in range(2)]
        glb = k.sb("fglb", [128, 512], BF16); bgl = Buf("fglb")
        ffr = [k.sb(f"ffr_{i}", [128, 512], F32) for i in range(2)]
        bffr = [Buf(f"ffr_{i}") for i in range(2)]

        onescb = k.sb("fonescb", [128, 1], BF16); bocb = Buf("fonescb")
        P.op("v", lambda v: v.tensor_copy(onescb[:, :], onesc[:, :]),
             reads=[boc], writes=[bocb])
        k.push_scope()
        st_ps = k.ps("fst", [1, 512], F32); bst2 = Buf("fst")
        sq_ps = k.ps("fsq", [1, 512], F32); bsq2 = Buf("fsq")
        for t4 in range(4):
            c0 = SDH + 512 * t4
            for ch in range(DC):
                sl = ch % 2
                cs = c0 - (1 if ch >= 4 else 0)
                P.op("g", lambda g, ch=ch, cs=cs, sl=sl: g.dma_start(
                        out=xsh[sl][:, :], in_=sd[128 * ch:128 * (ch + 1), cs:cs + 512]),
                     reads=[bsd], writes=[bxsh[sl]], dma=lsd)
                P.op("v", lambda v, sl=sl: v.tensor_tensor(
                        x2t[:, :], xsh[sl][:, :], xsh[sl][:, :], ALU.mult),
                     reads=[bxsh[sl]], writes=[bx2])
                P.op("t", lambda t, ch=ch, sl=sl: t.matmul(
                        st_ps[0:1, :], onescb[:, :], xsh[sl][:, :],
                        start=(ch == 0), stop=(ch == DC - 1)),
                     reads=[bocb, bxsh[sl]], writes=[bst2])
                P.op("t", lambda t, ch=ch: t.matmul(
                        sq_ps[0:1, :], onescb[:, :], x2t[:, :],
                        start=(ch == 0), stop=(ch == DC - 1)),
                     reads=[bocb, bx2], writes=[bsq2])
                P.op("v", lambda v, ch=ch, t4=t4, sl=sl: v.tensor_copy(
                        xnb[ch][:, 512 * t4:512 * (t4 + 1)], xsh[sl][:, :]),
                     reads=[bxsh[sl]], writes=[bxnb[ch]])
            P.op("v", lambda v, t4=t4: v.tensor_scalar_mul(
                    srow[0:1, 0, 512 * t4:512 * (t4 + 1)], st_ps[0:1, :], 1.0 / D),
                 reads=[bst2], writes=[bsr])
            P.op("v", lambda v, t4=t4: v.tensor_copy(
                    srow[0:1, 1, 512 * t4:512 * (t4 + 1)], sq_ps[0:1, :]),
                 reads=[bsq2], writes=[bsr])
        P.op("v", lambda v: v.tensor_tensor(srow[0:1, 2, :], srow[0:1, 0, :],
                                            srow[0:1, 0, :], ALU.mult),
             reads=[bsr], writes=[bsr])
        P.op("v", lambda v: v.scalar_tensor_tensor(srow[0:1, 2, :], srow[0:1, 1, :],
                                                   1.0 / D, srow[0:1, 2, :],
                                                   ALU.mult, ALU.subtract),
             reads=[bsr], writes=[bsr])
        P.op("v", lambda v: v.tensor_scalar_add(srow[0:1, 2, :], srow[0:1, 2, :], 1e-5),
             reads=[bsr], writes=[bsr])
        P.op("s", lambda s: s.activation(srow[0:1, 2, :], srow[0:1, 2, :], AF.Sqrt),
             reads=[bsr], writes=[bsr])
        P.op("v", lambda v: v.reciprocal(ab_row[0:1, 0, :], srow[0:1, 2, :]),
             reads=[bsr], writes=[babr])
        P.op("v", lambda v: v.scalar_tensor_tensor(ab_row[0:1, 1, :], srow[0:1, 0, :],
                                                   -1.0, ab_row[0:1, 0, :],
                                                   ALU.mult, ALU.mult),
             reads=[babr, bsr], writes=[babr])
        abc_ps = k.ps("fabc", [128, 512], F32); babc = Buf("fabc")
        bbc_ps = k.ps("fbbc", [128, 512], F32); bbbc = Buf("fbbc")
        for t4 in range(4):
            P.op("t", lambda t, t4=t4: t.matmul(
                    abc_ps[:, :], onesr[:, :], ab_row[0:1, 0, 512 * t4:512 * (t4 + 1)],
                    start=True, stop=True),
                 reads=[bor, babr], writes=[babc])
            P.op("t", lambda t, t4=t4: t.matmul(
                    bbc_ps[:, :], onesr[:, :], ab_row[0:1, 1, 512 * t4:512 * (t4 + 1)],
                    start=True, stop=True),
                 reads=[bor, babr], writes=[bbbc])
            for ch in range(DC):
                P.op("v", lambda v, ch=ch, t4=t4: v.tensor_tensor(
                        xnb[ch][:, 512 * t4:512 * (t4 + 1)],
                        xnb[ch][:, 512 * t4:512 * (t4 + 1)], abc_ps[:, :], ALU.mult),
                     reads=[babc, bxnb[ch]], writes=[bxnb[ch]])
                P.op("v", lambda v, ch=ch, t4=t4: v.tensor_tensor(
                        xnb[ch][:, 512 * t4:512 * (t4 + 1)],
                        xnb[ch][:, 512 * t4:512 * (t4 + 1)], bbc_ps[:, :], ALU.add),
                     reads=[bbbc, bxnb[ch]], writes=[bxnb[ch]])
                P.op("v", lambda v, ch=ch, t4=t4: v.tensor_scalar(
                        xnb[ch][:, 512 * t4:512 * (t4 + 1)],
                        xnb[ch][:, 512 * t4:512 * (t4 + 1)],
                        lngc[:, ch:ch + 1], lnbc[:, ch:ch + 1], ALU.mult, ALU.add),
                     reads=[blg, blb, bxnb[ch]], writes=[bxnb[ch]])

        k.pop_scope()

        k.push_scope()
        h_ps = [k.ps(f"fh_{i}", [128, 512], F32) for i in range(2)]
        bh2 = [Buf(f"fh_{i}") for i in range(2)]
        for fp in range(F1P // 256):
            sl = fp % 2
            P.op("g", lambda g, fp=fp, sl=sl, li=li: g.dma_start(
                    out=w1b[sl][:, :, :],
                    in_=A["f_w1p"][li][:, 256 * fp:256 * (fp + 1)].rearrange(
                        "(c p) f -> p c f", p=128)),
                 writes=[bw1b[sl]], dma=lw1)
            for nn in range(4):
                for kk in range(DC):
                    P.op("t", lambda t, nn=nn, kk=kk, sl=sl: t.matmul(
                            h_ps[0][:, :], w1b[sl][:, kk, 0:128],
                            xnb[kk][:, 512 * nn:512 * (nn + 1)],
                            start=(kk == 0), stop=(kk == DC - 1)),
                         reads=[bw1b[sl], bxnb[kk]], writes=[bh2[0]],
                         quiet=(kk != DC - 1))
                    P.op("t", lambda t, nn=nn, kk=kk, sl=sl: t.matmul(
                            h_ps[1][:, :], w1b[sl][:, kk, 128:256],
                            xnb[kk][:, 512 * nn:512 * (nn + 1)],
                            start=(kk == 0), stop=(kk == DC - 1)),
                         reads=[bw1b[sl], bxnb[kk]], writes=[bh2[1]],
                         quiet=(kk != DC - 1))
                P.op("s", lambda s, fp=fp: s.activation(
                        glb[:, :], h_ps[1][:, :], AF.Gelu,
                        bias=b1c[:, 2 * fp + 1:2 * fp + 2]),
                     reads=[bh2[1], bb1], writes=[bgl])
                P.op("v", lambda v, fp=fp, nn=nn: v.scalar_tensor_tensor(
                        aG[:, TOK * fp + 512 * nn:TOK * fp + 512 * (nn + 1)],
                        h_ps[0][:, :], b1c[:, 2 * fp:2 * fp + 1], glb[:, :],
                        ALU.add, ALU.mult),
                     reads=[bh2[0], bb1, bgl], writes=[baG])

        k.pop_scope()

        k.push_scope()
        o_ps = [k.ps(f"fo_{i}", [128, 512], F32) for i in range(DC)]
        bo2 = [Buf(f"fo_{i}") for i in range(DC)]
        NKF = F2P // 128
        for nn in range(4):
            for kf in range(NKF):
                sl = kf % 2
                P.op("g", lambda g, kf=kf, sl=sl, li=li: g.dma_start(
                        out=w2b[sl][:, :], in_=A["f_w2p"][li, 128 * kf:128 * (kf + 1), :]),
                     writes=[bw2b[sl]], dma=lw2)
                for m in range(DC):
                    P.op("t", lambda t, kf=kf, m=m, nn=nn, sl=sl: t.matmul(
                            o_ps[m][:, :], w2b[sl][:, 128 * m:128 * (m + 1)],
                            aG[:, TOK * kf + 512 * nn:TOK * kf + 512 * (nn + 1)],
                            start=(kf == 0), stop=(kf == NKF - 1)),
                         reads=[bw2b[sl], baG], writes=[bo2[m]],
                         quiet=(m != DC - 1))
            for m in range(DC):
                sl = m % 2
                c0 = SDH + 512 * nn
                P.op("y", lambda y, m=m, c0=c0, sl=sl: y.dma_start(
                        out=ffr[sl][:, :], in_=sd[128 * m:128 * (m + 1), c0:c0 + 512]),
                     reads=[bsd], writes=[bffr[sl]], dma=lres)
                P.op("v", lambda v, m=m, sl=sl: v.scalar_tensor_tensor(
                        ffr[sl][:, :], o_ps[m][:, :], 1.0, ffr[sl][:, :],
                        ALU.mult, ALU.add),
                     reads=[bo2[m], bffr[sl]], writes=[bffr[sl]])
                P.op("v", lambda v, m=m, sl=sl: v.tensor_scalar_add(
                        ffr[sl][:, :], ffr[sl][:, :], b2c[:, m:m + 1]),
                     reads=[bb2, bffr[sl]], writes=[bffr[sl]])
                P.op("y", lambda y, m=m, c0=c0, sl=sl: y.dma_start(
                        out=sd[128 * m:128 * (m + 1), c0:c0 + 512], in_=ffr[sl][:, :]),
                     reads=[bffr[sl]], writes=[bsd], dma=swr)
        k.pop_scope()
        k.pop_scope()

    k.push_scope()
    dch = [k.sb(f"fdch_{i}", [128, TOK], F32) for i in range(DC)]
    bdch = [Buf(f"fdch_{i}") for i in range(DC)]
    pch = k.sb("fpch", [128, TOK], F32); bpch = Buf("fpch")
    ytl = [k.sb(f"fytl_{i}", [128, D], F32) for i in range(2)]
    byt = [Buf(f"fytl_{i}") for i in range(2)]
    x1t = k.sb("fx1t", [128, D], F32); bx1t = Buf("fx1t")
    tr_pf = k.ps("ftrf", [128, 512], F32); btrf = Buf("ftrf")
    for ch in range(DC):
        P.op("y", lambda y, ch=ch: y.dma_start(
                out=dch[ch][:, :], in_=sd[128 * ch:128 * (ch + 1), SDH:SDH + TOK]),
             reads=[bsd], writes=[bdch[ch]], dma=lfin)
        P.op("y", lambda y, ch=ch: y.dma_start(
                out=pch[:, :], in_=A["part_d"][128 * ch:128 * (ch + 1), :]),
             writes=[bpch], dma=lfin)
        P.op("v", lambda v, ch=ch: v.tensor_tensor(dch[ch][:, :], dch[ch][:, :],
                                                   pch[:, :], ALU.add),
             reads=[bpch, bdch[ch]], writes=[bdch[ch]])
    for t in range(NT):
        sl = t % 2
        P.op("y", lambda y, t=t: y.dma_start(
                out=x1t[:, :], in_=A["x1_d"][128 * t:128 * (t + 1), :]),
             writes=[bx1t], dma=lfin)
        for ch in range(DC):
            q = ch % 4
            P.op("t", lambda te, ch=ch, t=t, q=q: te.transpose(
                    tr_pf[:, 128 * q:128 * (q + 1)],
                    dch[ch][:, 128 * t:128 * (t + 1)], identf[:, :]),
                 reads=[bdch[ch], bIf], writes=[btrf])
            P.op("v", lambda v, ch=ch, q=q, sl=sl: v.tensor_copy(
                    ytl[sl][:, 128 * ch:128 * (ch + 1)], tr_pf[:, 128 * q:128 * (q + 1)]),
                 reads=[btrf], writes=[byt[sl]])
        P.op("v", lambda v, sl=sl: v.tensor_tensor(ytl[sl][:, :], ytl[sl][:, :],
                                                   x1t[:, :], ALU.add),
             reads=[bx1t, byt[sl]], writes=[byt[sl]])
        P.op("y", lambda y, t=t, sl=sl: y.dma_start(
                out=A["y"][128 * t:128 * (t + 1), :], in_=ytl[sl][:, :]),
             reads=[byt[sl]], dma=sy)
    k.pop_scope(fence=False)
    P.raw_wait("y", (sy, sy.count))


def build_fused():
    global _LAST_PLAN
    nc = bass.Bass("TRN2", target_bir_lowering=False)
    P = Plan()
    _LAST_PLAN = P
    ctx = contextlib.ExitStack()
    k = K(nc, ctx, P)
    A = {}
    ins = ["x_own", "x_full", "wqkvT0", "wg0", "ident", "moe_w1", "moe_w2",
           "prefmask", "gid1", "ut", "ones_col", "ones_row", "l1a_w", "l1a_wo",
           "l1a_bo", "perm", "hsel", "hbc", "msk_w0", "msk_int", "msk_ext",
           "halo_gate", "halo_rows", "rot_cs", "rot_ss", "rot_ci", "rot_si",
           "f_wqkv", "f_wo", "f_w1p", "f_b1p", "f_w2p", "f_b2", "f_lng",
           "f_lnb", "f_qs", "f_ks", "f_qs_c", "f_ks_c"]
    A["x_own"] = k.inp("x_own", [TOK, D])
    A["x_full"] = k.inp("x_full", [B * N, D])
    A["wqkvT0"] = k.inp("wqkvT0", [D, 3 * D])
    A["wg0"] = k.inp("wg0", [D, 16])
    A["ident"] = k.inp("ident", [128, 128])
    A["moe_w1"] = k.inp("moe_w1", [2, D, FF_MOE])
    A["moe_w2"] = k.inp("moe_w2", [2, FF_MOE, D])
    A["prefmask"] = k.inp("prefmask", [8, 1])
    A["gid1"] = k.inp("gid1", [128, NT])
    A["ut"] = k.inp("ut", [128, 128])
    A["ones_col"] = k.inp("ones_col", [128, 1])
    A["ones_row"] = k.inp("ones_row", [1, 128])
    A["l1a_w"] = k.inp("l1a_w", [D, 1536])
    A["l1a_wo"] = k.inp("l1a_wo", [512, D])
    A["l1a_bo"] = k.inp("l1a_bo", [128, DC])
    A["perm"] = k.inp("perm", [128, 128])
    A["hsel"] = k.inp("hsel", [128, DC * 16])
    A["hbc"] = k.inp("hbc", [16, DC * 128])
    A["msk_w0"] = k.inp("msk_w0", [2, 128, 512])
    A["msk_int"] = k.inp("msk_int", [2, 128, 512])
    A["msk_ext"] = k.inp("msk_ext", [128, 512])
    A["halo_gate"] = k.inp("halo_gate", [128, 1])
    A["halo_rows"] = k.inp("halo_rows", [128, DC])
    A["rot_cs"] = k.inp("rot_cs", [128, SDH + TOK])
    A["rot_ss"] = k.inp("rot_ss", [128, SDH + TOK])
    A["rot_ci"] = k.inp("rot_ci", [128, SDH + TOK])
    A["rot_si"] = k.inp("rot_si", [128, SDH + TOK])
    A["f_wqkv"] = k.inp("f_wqkv", [DEPTH1, D, 3072])
    A["f_wo"] = k.inp("f_wo", [DEPTH1, D, D])
    A["f_w1p"] = k.inp("f_w1p", [DEPTH1, D, F1P])
    A["f_b1p"] = k.inp("f_b1p", [DEPTH1, 128, F1P // 128])
    A["f_w2p"] = k.inp("f_w2p", [DEPTH1, F2P, D])
    A["f_b2"] = k.inp("f_b2", [DEPTH1, 128, DC])
    A["f_lng"] = k.inp("f_lng", [DEPTH1, 128, DC])
    A["f_lnb"] = k.inp("f_lnb", [DEPTH1, 128, DC])
    A["f_qs"] = k.inp("f_qs", [DEPTH1, 1, 128])
    A["f_ks"] = k.inp("f_ks", [DEPTH1, 1, 128])
    A["f_qs_c"] = k.inp("f_qs_c", [DEPTH1, 128, 1])
    A["f_ks_c"] = k.inp("f_ks_c", [DEPTH1, 128, 1])
    A["y"] = k.outp("y", [TOK, D])
    A["a0_d"] = k.dram("a0_d", [TOK, D])
    A["rt_d"] = k.dram("rt_d", [128, 32])
    A["moe_d"] = k.dram("moe_d", [TOK, D])
    A["x1_d"] = k.dram("x1_d", [TOK, D])
    A["xn1fm_d"] = k.dram("xn1fm_d", [D, TOK])
    A["part_d"] = k.dram("part_d", [D, TOK])
    A["kag_in"] = k.dram("kag_in", [D, TOK], BF16)
    A["kag"] = k.dram("kag", [4 * D, TOK], BF16)
    A["vag_in"] = k.dram("vag_in", [TOK, D], BF16)
    A["vag"] = k.dram("vag", [4 * TOK, D], BF16)
    A["cnt_in"] = k.dram("cnt_in", [1, 16])
    A["cnt_ag"] = k.dram("cnt_ag", [8, 16], shared=True)
    A["tbl_d"] = k.dram("tbl_d", [B * N, 8])
    A["tbl_rs"] = k.dram("tbl_rs", [TOK, 8])
    A["eo_d"] = k.dram("eo_d", [TOK, D])
    A["eo_ag"] = k.dram("eo_ag", [B * N, D], shared=True)
    A["ctx_in"] = k.dram("ctx_in", [64, 512])
    A["ctx_ar"] = k.dram("ctx_ar", [64, 512])
    A["sd"] = k.dram("sd", [D, SDH + TOK])
    A["hal_in"] = k.dram("hal_in", [D, SDH])
    A["hal_ag"] = k.dram("hal_ag", [4 * D, SDH])
    with ctx:
        k.push_scope()
        emit_l0(k, A, GRP_BATCH)
        k.pop_scope()
        k.push_scope()
        emit_moe(k, A)
        k.pop_scope()
        k.push_scope()
        emit_l1a(k, A)
        k.pop_scope()
        k.push_scope()
        emit_l1f(k, A)
        k.pop_scope(fence=False)
        with nc.Block() as block:
            P.emit(nc, ctx, block)
    return nc, ins, ["y"]


_LAST_PLAN = None


def build_launch_l1f():
    global _LAST_PLAN
    nc = bass.Bass("TRN2", target_bir_lowering=False)
    P = Plan()
    _LAST_PLAN = P
    ctx = contextlib.ExitStack()
    k = K(nc, ctx, P)
    A = {}
    A["xn1fm_d"] = k.inp("xn1fm_d", [D, TOK])
    A["part_d"] = k.inp("part_d", [D, TOK])
    A["x1_d"] = k.inp("x1_d", [TOK, D])
    A["ident"] = k.inp("ident", [128, 128])
    A["ones_row"] = k.inp("ones_row", [1, 128])
    A["ones_col"] = k.inp("ones_col", [128, 1])
    A["perm"] = k.inp("perm", [128, 128])
    A["hsel"] = k.inp("hsel", [128, DC * 16])
    A["hbc"] = k.inp("hbc", [16, DC * 128])
    A["msk_w0"] = k.inp("msk_w0", [2, 128, 512])
    A["msk_int"] = k.inp("msk_int", [2, 128, 512])
    A["msk_ext"] = k.inp("msk_ext", [128, 512])
    A["halo_gate"] = k.inp("halo_gate", [128, 1])
    A["halo_rows"] = k.inp("halo_rows", [128, DC])
    A["rot_cs"] = k.inp("rot_cs", [128, SDH + TOK])
    A["rot_ss"] = k.inp("rot_ss", [128, SDH + TOK])
    A["rot_ci"] = k.inp("rot_ci", [128, SDH + TOK])
    A["rot_si"] = k.inp("rot_si", [128, SDH + TOK])
    A["f_wqkv"] = k.inp("f_wqkv", [DEPTH1, D, 3072])
    A["f_wo"] = k.inp("f_wo", [DEPTH1, D, D])
    A["f_w1p"] = k.inp("f_w1p", [DEPTH1, D, F1P])
    A["f_b1p"] = k.inp("f_b1p", [DEPTH1, 128, F1P // 128])
    A["f_w2p"] = k.inp("f_w2p", [DEPTH1, F2P, D])
    A["f_b2"] = k.inp("f_b2", [DEPTH1, 128, DC])
    A["f_lng"] = k.inp("f_lng", [DEPTH1, 128, DC])
    A["f_lnb"] = k.inp("f_lnb", [DEPTH1, 128, DC])
    A["f_qs"] = k.inp("f_qs", [DEPTH1, 1, 128])
    A["f_ks"] = k.inp("f_ks", [DEPTH1, 1, 128])
    A["f_qs_c"] = k.inp("f_qs_c", [DEPTH1, 128, 1])
    A["f_ks_c"] = k.inp("f_ks_c", [DEPTH1, 128, 1])
    A["y"] = k.outp("y", [TOK, D])
    A["sd"] = k.dram("sd", [D, SDH + TOK])
    A["hal_in"] = k.dram("hal_in", [D, SDH])
    A["hal_ag"] = k.dram("hal_ag", [4 * D, SDH])
    with ctx:
        emit_l1f(k, A)
        with nc.Block() as block:
            P.emit(nc, ctx, block)
    ins = ["xn1fm_d", "part_d", "x1_d", "ident", "ones_row", "ones_col", "perm",
           "hsel", "hbc", "msk_w0", "msk_int", "msk_ext", "halo_gate", "halo_rows",
           "rot_cs", "rot_ss", "rot_ci", "rot_si", "f_wqkv", "f_wo", "f_w1p",
           "f_b1p", "f_w2p", "f_b2", "f_lng", "f_lnb", "f_qs", "f_ks",
           "f_qs_c", "f_ks_c"]
    return nc, ins, ["y"]


GRP_BATCH = [[0, 1, 2, 3], [4, 5, 6, 7]]
GRP_ALL = [list(range(8))]

PHASE_MODE = "split"
_DEV_PHASES = None


def build_launch_l0():
    nc = bass.Bass("TRN2", target_bir_lowering=False)
    P = Plan()
    ctx = contextlib.ExitStack()
    k = K(nc, ctx, P)
    A = {}
    A["x_own"] = k.inp("x_own", [TOK, D])
    A["wqkvT0"] = k.inp("wqkvT0", [D, 3 * D])
    A["wg0"] = k.inp("wg0", [D, 16])
    A["ident"] = k.inp("ident", [128, 128])
    A["a0_d"] = k.outp("a0_d", [TOK, D])
    A["rt_d"] = k.outp("rt_d", [128, 32])
    A["kag_in"] = k.dram("kag_in", [D, TOK], BF16)
    A["kag"] = k.dram("kag", [4 * D, TOK], BF16)
    A["vag_in"] = k.dram("vag_in", [TOK, D], BF16)
    A["vag"] = k.dram("vag", [4 * TOK, D], BF16)
    with ctx:
        emit_l0(k, A, GRP_BATCH)
        with nc.Block() as block:
            P.emit(nc, ctx, block)
    return nc, ["x_own", "wqkvT0", "wg0", "ident"], ["a0_d", "rt_d"]


def build_launch_moe():
    nc = bass.Bass("TRN2", target_bir_lowering=False)
    P = Plan()
    ctx = contextlib.ExitStack()
    k = K(nc, ctx, P)
    A = {}
    A["rt_d"] = k.inp("rt_d", [128, 32])
    A["x_full"] = k.inp("x_full", [B * N, D])
    A["moe_w1"] = k.inp("moe_w1", [2, D, FF_MOE])
    A["moe_w2"] = k.inp("moe_w2", [2, FF_MOE, D])
    A["prefmask"] = k.inp("prefmask", [8, 1])
    A["gid1"] = k.inp("gid1", [128, NT])
    A["ident"] = k.inp("ident", [128, 128])
    A["ut"] = k.inp("ut", [128, 128])
    A["ones_col"] = k.inp("ones_col", [128, 1])
    A["ones_row"] = k.inp("ones_row", [1, 128])
    A["moe_d"] = k.outp("moe_d", [TOK, D])
    A["dbg_d"] = k.outp("dbg_d", [128, 3 * NT])
    A["eo_dbg"] = k.outp("eo_dbg", [TOK, D])
    A["cnt_in"] = k.dram("cnt_in", [1, 16])
    A["cnt_ag"] = k.dram("cnt_ag", [8, 16], shared=True)
    A["tbl_d"] = k.dram("tbl_d", [B * N, 8])
    A["tbl_rs"] = k.dram("tbl_rs", [TOK, 8])
    A["eo_d"] = k.dram("eo_d", [TOK, D])
    A["eo_ag"] = k.dram("eo_ag", [B * N, D], shared=True)
    with ctx:
        emit_moe(k, A)
        with nc.Block() as block:
            P.emit(nc, ctx, block)
    return nc, ["rt_d", "x_full", "moe_w1", "moe_w2", "prefmask", "gid1",
                "ident", "ut", "ones_col", "ones_row"], ["moe_d", "dbg_d", "eo_dbg"]


_BUILD_CACHE = {}


def _run_launch(name, builder, in_maps):
    if name not in _BUILD_CACHE:
        _BUILD_CACHE[name] = builder()
    nc, in_names, out_names = _BUILD_CACHE[name]
    maps = [{n: im[n] for n in in_names} for im in in_maps]
    res = run_bass_kernel_spmd(nc, maps, core_ids=list(range(CORES)))
    outs = []
    for c in range(CORES):
        outs.append({n: np.asarray(res.results[c][n]) for n in out_names})
    return outs


FUSED = True


def kernel(**inputs):
    in_maps = prep_inputs(inputs)
    if FUSED:
        lf = _run_launch("fused", build_fused, in_maps)
        out = np.empty((B, N, D), np.float32)
        for c in range(CORES):
            b, j = divmod(c, 4)
            out[b, j * TOK:(j + 1) * TOK] = lf[c]["y"]
        return out
    l0 = _run_launch("l0", build_launch_l0, in_maps)
    for c in range(CORES):
        in_maps[c]["rt_d"] = l0[c]["rt_d"]
        in_maps[c]["a0_d"] = l0[c]["a0_d"]
    mo = _run_launch("moe", build_launch_moe, in_maps)
    for c in range(CORES):
        in_maps[c]["moe_d"] = mo[c]["moe_d"]
    la = _run_launch("l1a", build_launch_l1a, in_maps)
    for c in range(CORES):
        for n in ["x1_d", "xn1fm_d", "part_d"]:
            in_maps[c][n] = la[c][n]
    lf = _run_launch("l1f", build_launch_l1f, in_maps)
    out = np.empty((B, N, D), np.float32)
    for c in range(CORES):
        b, j = divmod(c, 4)
        out[b, j * TOK:(j + 1) * TOK] = lf[c]["y"]
    return out


# version 05
# speedup vs baseline: 1.1421x; 1.0038x over previous
import contextlib
import math
import numpy as np
import ml_dtypes
import concourse.bass as bass
import concourse.mybir as mybir
from concourse.bass_utils import run_bass_kernel_spmd
from concourse.bass_core_rust import IndirectOffsetOnAxis

F32 = mybir.dt.float32
BF16 = mybir.dt.bfloat16
U32 = mybir.dt.uint32
I32 = mybir.dt.int32
AF = mybir.ActivationFunctionType
ALU = mybir.AluOpType
AX = mybir.AxisListType

BF = ml_dtypes.bfloat16

B, N, D = 2, 8192, 1024
CORES = 8
TOK = N // 4
NT = TOK // 128
DC = D // 128
E_MOE, CAP = 16, 1024
FF_MOE = 4096
HALO = 513
WIN = 256
NWIN = TOK // WIN
HEADS1, DH1 = 16, 64
F1P = 5632
F2P = 2816
DEPTH1 = 6
_L1F_DEPTH = 6
NEG = -3.38e38
MSKV = -30000.0
OOBV = float(1 << 24)

ENGINES = ("v", "s", "t", "g", "y")


class Buf:
    __slots__ = ("name", "writer", "readers")

    def __init__(self, name=""):
        self.name = name
        self.writer = None
        self.readers = []


class DmaSem:
    __slots__ = ("name", "count", "waited", "step", "sem")

    def __init__(self, name, step=16):
        self.name = name
        self.count = 0
        self.waited = 0
        self.step = step
        self.sem = None


class PlannedOp:
    __slots__ = ("ename", "fn", "waits", "inc", "lineno")

    def __init__(self, ename, fn, waits, inc):
        self.ename = ename
        self.fn = fn
        self.waits = waits
        self.inc = inc
        self.lineno = 0


class Plan:
    def __init__(self):
        self.ops = {e: [] for e in ENGINES}
        self.ticks = {e: 0 for e in ENGINES}
        self.marks = {}
        self.dma_sems = []
        self.n_ops = 0
        self.pending = {e: ([], []) for e in ENGINES}

    def dma_sem(self, name, step=16):
        d = DmaSem(name, step)
        self.dma_sems.append(d)
        return d

    def _need_wait(self, ename, marker):
        if marker is None:
            return None
        key, val = marker
        if isinstance(key, DmaSem):
            val = key.count
            if val > key.waited:
                key.waited = val
        wm = self.marks.get((ename, key), 0)
        if val <= wm:
            return None
        self.marks[(ename, key)] = val
        return (key, val)

    def _guard_pending(self, ename, reads, writes):
        for e2 in ENGINES:
            if e2 == ename:
                continue
            pr, pw = self.pending[e2]
            for b in reads:
                assert b is None or b not in pw, \
                    f"read of {b.name} while quiet-written on {e2}"
            for b in writes:
                assert b is None or (b not in pw and b not in pr), \
                    f"write of {b.name} while quiet-pending on {e2}"

    def op(self, ename, fn, reads=(), writes=(), dma=None, quiet=False):
        self._guard_pending(ename, reads, writes)
        waits = []
        for b in reads:
            if b is not None:
                w = self._need_wait(ename, b.writer)
                if w:
                    waits.append(w)
        for b in writes:
            if b is None:
                continue
            w = self._need_wait(ename, b.writer)
            if w:
                waits.append(w)
            for m in b.readers:
                w = self._need_wait(ename, m)
                if w:
                    waits.append(w)
        if quiet:
            assert dma is None
            self.ops[ename].append(PlannedOp(ename, fn, self._coalesce(waits), None))
            self.n_ops += 1
            pr, pw = self.pending[ename]
            pr.extend(b for b in reads if b is not None)
            pw.extend(b for b in writes if b is not None)
            return None
        if dma is not None and dma.waited > 0 and dma.count > 0:
            w = self._need_wait(ename, (dma, dma.count))
            if w:
                waits.append(w)
        if dma is not None:
            dma.count += dma.step
            inc = (dma, dma.step)
            marker = (dma, dma.count)
        else:
            self.ticks[ename] += 1
            inc = (ename, 1)
            marker = (ename, self.ticks[ename])
        po = PlannedOp(ename, fn, self._coalesce(waits), inc)
        import sys as _sys
        po.lineno = _sys._getframe(1).f_lineno
        self.ops[ename].append(po)
        self.n_ops += 1
        if dma is None:
            pr, pw = self.pending[ename]
            for b in pr:
                b.readers.append(marker)
            for b in pw:
                b.writer = marker
            self.pending[ename] = ([], [])
        for b in reads:
            if b is not None:
                b.readers.append(marker)
        for b in writes:
            if b is not None:
                b.writer = marker
                b.readers = []
        return marker

    @staticmethod
    def _coalesce(waits):
        best = {}
        for key, val in waits:
            if val > best.get(id(key), (None, -1))[1]:
                best[id(key)] = (key, val)
        return list(best.values())

    def raw_wait(self, ename, marker):
        w = self._need_wait(ename, marker)
        if w:
            self.ops[ename].append(PlannedOp(ename, None, [w], None))
            self.n_ops += 1

    def barrier(self, ename, bufs):
        for b in bufs:
            if b is None:
                continue
            for m in [b.writer] + b.readers:
                w = self._need_wait(ename, m)
                if w:
                    self.ops[ename].append(PlannedOp(ename, None, [w], None))
                    self.n_ops += 1

    def group_end(self, bufs, dma):
        m = (dma, dma.count)
        for b in bufs:
            b.writer = m
            b.readers = []

    def fence(self):
        for e in ENGINES:
            assert not self.pending[e][0] and not self.pending[e][1], \
                f"fence with pending quiet effects on {e}"
        for e in ENGINES:
            for e2 in ENGINES:
                if self.ticks[e2] > 0:
                    w = self._need_wait(e, (e2, self.ticks[e2]))
                    if w:
                        self.ops[e].append(PlannedOp(e, None, [w], None))
                        self.n_ops += 1
            for d in self.dma_sems:
                if d.count > 0:
                    w = self._need_wait(e, (d, d.count))
                    if w:
                        self.ops[e].append(PlannedOp(e, None, [w], None))
                        self.n_ops += 1

    def emit(self, nc, ctx, block):
        tick_sems = {}
        for e in ENGINES:
            if self.ops[e]:
                tick_sems[e] = ctx.enter_context(nc.semaphore(f"tick_{e}"))
        for d in self.dma_sems:
            if d.count > 0:
                d.sem = ctx.enter_context(nc.semaphore(f"dma_{d.name}"))

        def sem_of(key):
            return tick_sems[key] if isinstance(key, str) else key.sem

        def run(eng, ename):
            for po in self.ops[ename]:
                for key, val in po.waits:
                    eng.wait_ge(sem_of(key), val)
                if po.fn is None:
                    continue
                inst = po.fn(eng)
                if po.inc is not None:
                    key, step = po.inc
                    if step == 1 and not isinstance(key, str):
                        inst.then_inc(sem_of(key))
                    else:
                        inst.then_inc(sem_of(key), step)

        section = {"v": block.vector, "s": block.scalar, "t": block.tensor,
                   "g": block.gpsimd, "y": block.sync}
        for e in ENGINES:
            if self.ops[e]:
                section[e](lambda eng, _e=e: run(eng, _e))



def _rot_tables(lo, hi):
    n = N
    half = np.arange(0, DH1, 2, dtype=np.float64)
    inv_freq = 1.0 / 10000.0 ** (half / DH1)
    t = np.arange(n, dtype=np.float64)
    freqs = t[:, None] * inv_freq[None, :]
    base_scale = (half + 0.4 * DH1) / (1.4 * DH1)
    power = (t - n // 2) / (WIN // 2)
    scale = base_scale[None, :] ** power[:, None]
    cos, sin = np.cos(freqs), np.sin(freqs)
    pos = np.arange(lo, hi)
    posc = np.clip(pos, 0, n - 1)
    cs = (cos * scale)[posc].T
    ss = (sin * scale)[posc].T
    ci = (cos / scale)[posc].T
    si = (sin / scale)[posc].T

    def tile4(a):
        return np.ascontiguousarray(np.tile(a, (4, 1)).astype(np.float32))
    return tile4(cs), tile4(ss), tile4(ci), tile4(si)


def _win_masks(first):
    out = []
    for sub in range(2):
        p = np.arange(128) + 128 * sub
        c = np.arange(512)
        allow = (c[None, :] >= p[:, None]) & (c[None, :] <= p[:, None] + WIN)
        if first:
            allow &= c[None, :] >= WIN
        m = np.where(allow, 0.0, MSKV).astype(np.float32)
        out.append(m)
    return out


def _perm_mat():
    P = np.zeros((128, 128), np.float32)
    for pp in range(128):
        if pp % 64 < 32:
            P[pp + 32, pp] = -1.0
        else:
            P[pp - 32, pp] = 1.0
    return P


def prep_inputs(I):
    x = np.ascontiguousarray(np.asarray(I["L0_attn_x"], np.float32))
    x_full = np.ascontiguousarray(x.reshape(B * N, D))
    wqkvT0 = np.ascontiguousarray(np.asarray(I["L0_attn_W_qkv"]).T)
    wg0 = np.ascontiguousarray(np.asarray(I["L0_ffn_Wg"]))
    w1 = np.asarray(I["L0_ffn_w1"])
    w2 = np.asarray(I["L0_ffn_w2"])
    l1a_w = np.ascontiguousarray(np.asarray(I["L1_attn_W_qkv"]))
    l1a_wo = np.ascontiguousarray(np.asarray(I["L1_attn_W_out"]))
    l1a_bo = np.asarray(I["L1_attn_b_out"]).reshape(DC, 128).T.copy()
    f_wqkv = np.ascontiguousarray(np.asarray(I["L1_ffn_Wqkv"]))
    f_qs = np.asarray(I["L1_ffn_q_scale"])
    f_ks = np.asarray(I["L1_ffn_k_scale"])
    f_wo = np.ascontiguousarray(np.asarray(I["L1_ffn_Wo"]))
    f_lng = np.asarray(I["L1_ffn_ln_g"]).reshape(DEPTH1, DC, 128).transpose(0, 2, 1).copy()
    f_lnb = np.asarray(I["L1_ffn_ln_b"]).reshape(DEPTH1, DC, 128).transpose(0, 2, 1).copy()
    W1 = np.asarray(I["L1_ffn_W1"])
    b1 = np.asarray(I["L1_ffn_b1"])
    W2 = np.asarray(I["L1_ffn_W2"])
    b2 = np.asarray(I["L1_ffn_b2"]).reshape(DEPTH1, DC, 128).transpose(0, 2, 1).copy()

    DI = 2730
    nfull = DI // 128
    w1p = np.zeros((DEPTH1, D, F1P), np.float32)
    b1p = np.zeros((DEPTH1, F1P), np.float32)
    for t in range(nfull + 1):
        a0, a1 = t * 128, min((t + 1) * 128, DI)
        wdt = a1 - a0
        col = t * 256
        w1p[:, :, col:col + wdt] = W1[:, :, a0:a1]
        b1p[:, col:col + wdt] = b1[:, a0:a1]
        w1p[:, :, col + 128:col + 128 + wdt] = W1[:, :, DI + a0:DI + a1]
        b1p[:, col + 128:col + 128 + wdt] = b1[:, DI + a0:DI + a1]
    b1p_c = b1p.reshape(DEPTH1, F1P // 128, 128).transpose(0, 2, 1).copy()
    w2p = np.zeros((DEPTH1, F2P, D), np.float32)
    w2p[:, :DI, :] = W2
    b2_c = b2

    qs128 = np.tile(f_qs, (1, 2))
    ks128 = np.tile(f_ks, (1, 2))

    perm = _perm_mat()
    ident = np.eye(128, dtype=np.float32)
    hsel = np.zeros((DC, 128, 16), np.float32)
    for ch in range(DC):
        for r in range(2):
            hsel[ch, r * 64:(r + 1) * 64, 2 * ch + r] = 1.0
    hbc = np.zeros((DC, 16, 128), np.float32)
    for ch in range(DC):
        for r in range(2):
            hbc[ch, 2 * ch + r, r * 64:(r + 1) * 64] = 1.0
    ut = np.triu(np.ones((128, 128), np.float32))
    ones_col = np.ones((128, 1), np.float32)
    ones_row = np.ones((1, 128), np.float32)

    in_maps = []
    msk_int = _win_masks(False)
    msk_fst = _win_masks(True)
    for c in range(CORES):
        b, j = divmod(c, 4)
        t0 = j * TOK
        g0 = b * N + t0
        d = {}
        d["x_own"] = np.ascontiguousarray(x[b, t0:t0 + TOK])
        d["x_full"] = x_full
        d["wqkvT0"] = wqkvT0
        d["wg0"] = wg0
        d["moe_w1"] = np.ascontiguousarray(w1[2 * c:2 * c + 2])
        d["moe_w2"] = np.ascontiguousarray(w2[2 * c:2 * c + 2])
        d["prefmask"] = np.asarray([[1.0] if r < c else [0.0] for r in range(8)],
                                   np.float32)
        d["gid1"] = (np.arange(TOK, dtype=np.float32).reshape(NT, 128).T + g0 + 1.0).copy()
        d["l1a_w"] = l1a_w
        d["l1a_wo"] = l1a_wo
        d["l1a_bo"] = l1a_bo
        d["f_wqkv"] = f_wqkv
        d["f_wo"] = f_wo
        d["f_w1p"] = w1p
        d["f_b1p"] = b1p_c
        d["f_w2p"] = w2p
        d["f_b2"] = b2_c
        d["f_lng"] = f_lng
        d["f_lnb"] = f_lnb
        d["f_qs"] = qs128[:, None, :].copy()
        d["f_ks"] = ks128[:, None, :].copy()
        d["f_qs_c"] = qs128[:, :, None].copy()
        d["f_ks_c"] = ks128[:, :, None].copy()
        cs, ss, ci, si = _rot_tables(t0 - HALO, t0 + TOK)
        d["rot_cs"], d["rot_ss"], d["rot_ci"], d["rot_si"] = cs, ss, ci, si
        m0 = msk_fst if j == 0 else msk_int
        d["msk_w0"] = np.stack(m0)
        d["msk_int"] = np.stack(msk_int)
        d["msk_ext"] = (np.full((128, 512), MSKV, np.float32) if j == 0
                        else msk_int[1])
        d["halo_gate"] = np.full((128, 1), 0.0 if j == 0 else 1.0, np.float32)
        blk = (j - 1) % 4
        d["halo_rows"] = (np.arange(128, dtype=np.float32)[:, None]
                          + (blk * 1024 + 128 * np.arange(DC, dtype=np.float32))[None, :]).copy()
        d["perm"] = perm
        d["ident"] = ident
        d["hsel"] = np.ascontiguousarray(hsel.transpose(1, 0, 2).reshape(128, DC * 16))
        d["hbc"] = np.ascontiguousarray(hbc.transpose(1, 0, 2).reshape(16, DC * 128))
        d["ut"] = ut
        d["ones_col"] = ones_col
        d["ones_row"] = ones_row
        in_maps.append(d)
    return in_maps



class K:

    def __init__(self, nc, ctx, P):
        self.nc = nc
        self.ctx = ctx
        self.P = P
        self.scopes = [ctx]

    def push_scope(self):
        es = contextlib.ExitStack()
        self.scopes.append(es)

    def pop_scope(self, fence=True):
        es = self.scopes.pop()
        es.close()
        if fence:
            self.P.fence()

    def sb(self, name, shape, dt):
        self._uid = getattr(self, "_uid", 0) + 1
        return self.scopes[-1].enter_context(
            self.nc.sbuf_tensor(f"{name}_u{self._uid}", shape, dt))

    def breg(self, g, val):
        if not hasattr(self, "_regs"):
            self._regs = {}
        if val not in self._regs:
            self._regs[val] = g.to_reg(val)
        return self._regs[val]

    def ps(self, name, shape, dt=F32):
        self._uid = getattr(self, "_uid", 0) + 1
        return self.scopes[-1].enter_context(
            self.nc.psum_tensor(f"{name}_u{self._uid}", shape, dt))

    def inp(self, name, shape):
        return self.nc.dram_tensor(name, shape, F32, kind="ExternalInput").ap()

    def outp(self, name, shape, dt=F32):
        return self.nc.dram_tensor(name, shape, dt, kind="ExternalOutput").ap()

    def dram(self, name, shape, dt=F32, shared=False):
        return self.nc.dram_tensor(name, shape, dt, kind="Internal",
                                   addr_space="Shared" if shared else "Local").ap()



def emit_l0(k, A, grp):
    nc, P = k.nc, k.P
    ld = P.dma_sem("l0ld")
    lw = P.dma_sem("l0lw")
    lx = P.dma_sem("l0lx")
    stk = P.dma_sem("l0stk")
    stv = P.dma_sem("l0stv")
    cck = P.dma_sem("l0cck", step=1)
    ccv = P.dma_sem("l0ccv", step=1)
    lkb = P.dma_sem("l0lkb")
    lvb = P.dma_sem("l0lvb")
    sta = P.dma_sem("l0sta")
    strt = P.dma_sem("l0strt")

    ident = k.sb("id_bf", [128, 128], BF16); bI = Buf("id")
    P.op("g", lambda g: g.dma_start(out=ident[:, :], in_=A["ident"]), writes=[bI], dma=ld)
    q_fm = [k.sb(f"qfm_{i}", [128, TOK], BF16) for i in range(DC)]
    bq = [Buf(f"qfm_{i}") for i in range(DC)]

    k.push_scope()
    w0 = [k.sb(f"w0_{i}", [128, 3072], BF16) for i in range(DC)]
    bw0 = [Buf(f"w0_{i}") for i in range(DC)]
    for i in range(DC):
        P.op("g", lambda g, i=i: g.dma_start(out=w0[i][:, :],
                                             in_=A["wqkvT0"][128 * i:128 * (i + 1), :]),
             writes=[bw0[i]], dma=lw)
    wg = k.sb("wg", [128, DC, 16], BF16); bwg = Buf("wg")
    P.op("g", lambda g: g.dma_start(out=wg[:, :, :],
                                    in_=A["wg0"].rearrange("(c p) e -> p c e", p=128)),
         writes=[bwg], dma=lw)

    xn_fm = [k.sb(f"xnfm_{i}", [128, TOK], BF16) for i in range(DC)]
    bxn = [Buf(f"xnfm_{i}") for i in range(DC)]
    xt = [k.sb(f"xt_{i}", [128, D], F32) for i in range(2)]
    bxt = [Buf(f"xt_{i}") for i in range(2)]
    xnt = [k.sb(f"xnt_{i}", [128, D], BF16) for i in range(2)]
    bxnt = [Buf(f"xnt_{i}") for i in range(2)]
    sqs = k.sb("sqs", [128, D], F32); bsq = Buf("sq")
    st1 = k.sb("st1", [128, 8], F32); bst = Buf("st")
    tr_ps = k.ps("tr_ps", [128, 512], BF16); btr = Buf("tr")

    def ln_tile(xbuf, bxb, xnb, bxnb):
        P.op("v", lambda v: v.reduce_sum(st1[:, 0:1], xbuf[:, :], axis=AX.X),
             reads=[bxb], writes=[bst])
        P.op("s", lambda s: s.activation(sqs[:, :], xbuf[:, :], AF.Square,
                                         accum_out=st1[:, 1:2]),
             reads=[bxb], writes=[bsq, bst])
        P.op("v", lambda v: v.tensor_scalar_mul(st1[:, 2:3], st1[:, 0:1], 1.0 / D),
             reads=[bst], writes=[bst])
        P.op("v", lambda v: v.tensor_tensor(st1[:, 3:4], st1[:, 2:3], st1[:, 2:3], ALU.mult),
             reads=[bst], writes=[bst])
        P.op("v", lambda v: v.scalar_tensor_tensor(st1[:, 3:4], st1[:, 1:2], 1.0 / D,
                                                   st1[:, 3:4], ALU.mult, ALU.subtract),
             reads=[bst], writes=[bst])
        P.op("v", lambda v: v.tensor_scalar_add(st1[:, 3:4], st1[:, 3:4], 1e-5),
             reads=[bst], writes=[bst])
        P.op("s", lambda s: s.activation(st1[:, 4:5], st1[:, 3:4], AF.Sqrt),
             reads=[bst], writes=[bst])
        P.op("v", lambda v: v.reciprocal(st1[:, 4:5], st1[:, 4:5]), reads=[bst], writes=[bst])
        P.op("v", lambda v: v.scalar_tensor_tensor(st1[:, 5:6], st1[:, 2:3], -1.0,
                                                   st1[:, 4:5], ALU.mult, ALU.mult),
             reads=[bst], writes=[bst])
        P.op("v", lambda v: v.tensor_scalar(xnb[:, :], xbuf[:, :], st1[:, 4:5],
                                            st1[:, 5:6], ALU.mult, ALU.add),
             reads=[bxb, bst], writes=[bxnb])

    for t in range(NT):
        sl = t % 2
        P.op("y", lambda y, t=t, sl=sl: y.dma_start(
                out=xt[sl][:, :], in_=A["x_own"][128 * t:128 * (t + 1), :]),
             writes=[bxt[sl]], dma=lx)
        ln_tile(xt[sl], bxt[sl], xnt[sl], bxnt[sl])
        for ch in range(DC):
            q = ch % 4
            P.op("t", lambda te, ch=ch, sl=sl, q=q: te.transpose(
                    tr_ps[:, 128 * q:128 * (q + 1)],
                    xnt[sl][:, 128 * ch:128 * (ch + 1)], ident[:, :]),
                 reads=[bxnt[sl], bI], writes=[btr])
            P.op("v", lambda v, ch=ch, t=t, q=q: v.tensor_copy(
                    xn_fm[ch][:, 128 * t:128 * (t + 1)],
                    tr_ps[:, 128 * q:128 * (q + 1)]),
                 reads=[btr], writes=[bxn[ch]])

    gl_ps = k.ps("gl_ps", [128, 16], F32); bgl = Buf("gl")
    glx = k.sb("glx", [128, 16], F32); bglx = Buf("glx")
    mx8 = k.sb("mx8", [128, 8], F32); bmx = Buf("mx8")
    ix8 = k.sb("ix8", [128, 8], U32); bix = Buf("ix8")
    exs = k.sb("exs", [128, 16], F32); bex = Buf("exs")
    rts = k.sb("rts", [128, 32], F32); brt = Buf("rts")
    for t in range(NT):
        for ch in range(DC):
            P.op("t", lambda te, ch=ch, t=t: te.matmul(
                    gl_ps[:, :], xn_fm[ch][:, 128 * t:128 * (t + 1)], wg[:, ch, :],
                    start=(ch == 0), stop=(ch == DC - 1)),
                 reads=[bxn[ch], bwg], writes=[bgl], quiet=(ch != DC - 1))
        P.op("v", lambda v: v.tensor_copy(glx[:, :], gl_ps[:, :]), reads=[bgl], writes=[bglx])
        P.op("v", lambda v: v.max(mx8[:, :], glx[:, :]), reads=[bglx], writes=[bmx])
        P.op("v", lambda v: v.max_index(ix8[:, :], mx8[:, :], glx[:, :]),
             reads=[bmx, bglx], writes=[bix])
        P.op("v", lambda v: v.tensor_scalar_mul(st1[:, 6:7], mx8[:, 0:1], -1.0),
             reads=[bmx], writes=[bst])
        P.op("s", lambda s: s.activation(exs[:, :], glx[:, :], AF.Exp,
                                         bias=st1[:, 6:7], accum_out=st1[:, 7:8]),
             reads=[bglx, bst], writes=[bex, bst])
        P.op("v", lambda v, t=t: v.reciprocal(rts[:, t:t + 1], st1[:, 7:8]),
             reads=[bst], writes=[brt])
        P.op("v", lambda v, t=t: v.tensor_copy(rts[:, 16 + t:17 + t], ix8[:, 0:1]),
             reads=[bix], writes=[brt])
    P.op("y", lambda y: y.dma_start(out=A["rt_d"], in_=rts[:, :]), reads=[brt], dma=strt)

    pj_ps = [k.ps(f"pj_ps{i}", [128, 512], F32) for i in range(2)]
    bpj = [Buf(f"pj{i}") for i in range(2)]

    def proj_chunk(dst_sl, bdst, col0, m):
        for n in range(TOK // 512):
            sl = n % 2
            for kk in range(DC):
                P.op("t", lambda te, m=m, n=n, kk=kk, sl=sl: te.matmul(
                        pj_ps[sl][:, :],
                        w0[kk][:, col0 + 128 * m:col0 + 128 * (m + 1)],
                        xn_fm[kk][:, 512 * n:512 * (n + 1)],
                        start=(kk == 0), stop=(kk == DC - 1)),
                     reads=[bw0[kk], bxn[kk]], writes=[bpj[sl]], quiet=(kk != DC - 1))
            P.op("v", lambda v, n=n, sl=sl: v.tensor_copy(
                    dst_sl[:, 512 * n:512 * (n + 1)], pj_ps[sl][:, :]),
                 reads=[bpj[sl]], writes=[bdst])

    for m in range(DC):
        proj_chunk(q_fm[m], bq[m], 0, m)

    kch = [k.sb(f"kch_{i}", [128, TOK], BF16) for i in range(2)]
    bkch = [Buf(f"kch_{i}") for i in range(2)]
    for m in range(DC):
        sl = m % 2
        proj_chunk(kch[sl], bkch[sl], D, m)
        P.op("y", lambda y, m=m, sl=sl: y.dma_start(
                out=A["kag_in"][128 * m:128 * (m + 1), :], in_=kch[sl][:, :]),
             reads=[bkch[sl]], dma=stk)

    vt = [k.sb(f"vt_{i}", [128, D], BF16) for i in range(2)]
    bvt = [Buf(f"vt_{i}") for i in range(2)]
    vp_ps = [k.ps(f"vp_ps{i}", [128, 512], F32) for i in range(2)]
    bvp = [Buf(f"vp{i}") for i in range(2)]
    for t in range(NT):
        sl = t % 2
        for nn in range(2):
            for kk in range(DC):
                P.op("t", lambda te, t=t, nn=nn, kk=kk, sl=sl: te.matmul(
                        vp_ps[nn][:, :],
                        xn_fm[kk][:, 128 * t:128 * (t + 1)],
                        w0[kk][:, 2 * D + 512 * nn:2 * D + 512 * (nn + 1)],
                        start=(kk == 0), stop=(kk == DC - 1)),
                     reads=[bxn[kk], bw0[kk]], writes=[bvp[nn]], quiet=(kk != DC - 1))
            P.op("v", lambda v, nn=nn, sl=sl: v.tensor_copy(
                    vt[sl][:, 512 * nn:512 * (nn + 1)], vp_ps[nn][:, :]),
                 reads=[bvp[nn]], writes=[bvt[sl]])
        P.op("y", lambda y, t=t, sl=sl: y.dma_start(
                out=A["vag_in"][128 * t:128 * (t + 1), :], in_=vt[sl][:, :]),
             reads=[bvt[sl]], dma=stv)

    bkagi = Buf("kagi"); bvagi = Buf("vagi")
    P.group_end([bkagi], stk)
    P.group_end([bvagi], stv)
    bkag = Buf("kag"); bvag = Buf("vag")
    P.op("g", lambda g: g.collective_compute("AllGather", ALU.bypass,
                                             replica_groups=grp,
                                             ins=[A["kag_in"]], outs=[A["kag"]]),
         reads=[bkagi], writes=[bkag], dma=cck)
    P.op("g", lambda g: g.collective_compute("AllGather", ALU.bypass,
                                             replica_groups=grp,
                                             ins=[A["vag_in"]], outs=[A["vag"]]),
         reads=[bvagi], writes=[bvag], dma=ccv)

    k.pop_scope()

    Kb = [k.sb(f"Kb_{i}", [128, TOK], BF16) for i in range(DC)]
    bKb = [Buf(f"Kb_{i}") for i in range(DC)]
    Vb = [k.sb(f"Vb_{i}", [128, D], BF16) for i in range(NT)]
    bVb = [Buf(f"Vb_{i}") for i in range(NT)]
    acc = [k.sb(f"acc_{i}", [128, D], F32) for i in range(NT)]
    bacc = [Buf(f"acc_{i}") for i in range(NT)]
    ml = k.sb("ml", [128, 64], F32)
    bml = [Buf(f"ml_{i}") for i in range(NT)]
    fsc = k.sb("fsc", [128, 8], F32); bfsc = Buf("fsc")
    s_ps = k.ps("s_ps", [128, TOK], F32); bS = Buf("s_ps")
    av_ps = k.ps("av_ps", [128, D], F32); bAV = Buf("av")
    pt_ps = k.ps("pt_ps", [128, 512], BF16); bPT = Buf("ptp")
    Pt = k.sb("Pt", [128, TOK], BF16); bPt = Buf("Pt")
    PTs = k.sb("PTs", [128, TOK], BF16); bPTs = Buf("PTs")

    for qt in range(NT):
        P.op("v", lambda v, qt=qt: v.memset(acc[qt][:, :], 0.0), writes=[bacc[qt]])
        P.op("v", lambda v, qt=qt: v.memset(ml[:, qt:qt + 1], NEG), writes=[bml[qt]])
        P.op("v", lambda v, qt=qt: v.memset(ml[:, 16 + qt:17 + qt], 0.0), writes=[bml[qt]])

    for kc in range(4):
        for i in range(DC):
            P.op("y", lambda y, i=i, kc=kc: y.dma_start(
                    out=Kb[i][:, :], in_=A["kag"][1024 * kc + 128 * i:1024 * kc + 128 * (i + 1), :]),
                 reads=[bkag], writes=[bKb[i]], dma=lkb)
        for i in range(NT):
            P.op("g", lambda g, i=i, kc=kc: g.dma_start(
                    out=Vb[i][:, :], in_=A["vag"][TOK * kc + 128 * i:TOK * kc + 128 * (i + 1), :]),
                 reads=[bvag], writes=[bVb[i]], dma=lvb)
        for qt in range(NT):
            for g4 in range(4):
                for kk in range(DC):
                    P.op("t", lambda te, qt=qt, g4=g4, kk=kk: te.matmul(
                            s_ps[:, 512 * g4:512 * (g4 + 1)],
                            q_fm[kk][:, 128 * qt:128 * (qt + 1)],
                            Kb[kk][:, 512 * g4:512 * (g4 + 1)],
                            start=(kk == 0), stop=(kk == DC - 1)),
                         reads=[bq[kk], bKb[kk]], writes=[bS],
                         quiet=not (kk == DC - 1 and g4 == 3))
            P.op("v", lambda v: v.reduce_max(fsc[:, 0:1], s_ps[:, :], axis=AX.X),
                 reads=[bS], writes=[bfsc])
            P.op("v", lambda v, qt=qt: v.scalar_tensor_tensor(
                    fsc[:, 1:2], fsc[:, 0:1], 1.0 / 32.0, ml[:, qt:qt + 1],
                    ALU.mult, ALU.max),
                 reads=[bfsc, bml[qt]], writes=[bfsc])
            P.op("v", lambda v, qt=qt: v.tensor_tensor(
                    fsc[:, 2:3], ml[:, qt:qt + 1], fsc[:, 1:2], ALU.subtract),
                 reads=[bml[qt], bfsc], writes=[bfsc])
            P.op("s", lambda s: s.activation(fsc[:, 2:3], fsc[:, 2:3], AF.Exp),
                 reads=[bfsc], writes=[bfsc])
            P.op("v", lambda v: v.tensor_scalar_mul(fsc[:, 3:4], fsc[:, 1:2], -1.0),
                 reads=[bfsc], writes=[bfsc])
            P.op("v", lambda v, qt=qt: v.tensor_copy(ml[:, qt:qt + 1], fsc[:, 1:2]),
                 reads=[bfsc], writes=[bml[qt]])
            P.op("s", lambda s: s.activation(Pt[:, :], s_ps[:, :], AF.Exp,
                                             bias=fsc[:, 3:4], scale=1.0 / 32.0,
                                             accum_out=fsc[:, 4:5]),
                 reads=[bS, bfsc], writes=[bPt, bfsc])
            P.op("v", lambda v, qt=qt: v.tensor_tensor(
                    ml[:, 16 + qt:17 + qt], ml[:, 16 + qt:17 + qt], fsc[:, 2:3], ALU.mult),
                 reads=[bfsc], writes=[bml[qt]])
            P.op("v", lambda v, qt=qt: v.tensor_tensor(
                    ml[:, 16 + qt:17 + qt], ml[:, 16 + qt:17 + qt], fsc[:, 4:5], ALU.add),
                 reads=[bfsc], writes=[bml[qt]])
            for r in range(4):
                for u in range(4):
                    i = 4 * r + u
                    P.op("t", lambda te, i=i, u=u: te.transpose(
                            pt_ps[:, 128 * u:128 * (u + 1)],
                            Pt[:, 128 * i:128 * (i + 1)], ident[:, :]),
                         reads=[bPt, bI], writes=[bPT], quiet=(u != 3))
                P.op("v", lambda v, r=r: v.tensor_copy(
                        PTs[:, 512 * r:512 * (r + 1)], pt_ps[:, :]),
                     reads=[bPT], writes=[bPTs])
            for g2 in range(2):
                for i in range(NT):
                    P.op("t", lambda te, i=i, g2=g2: te.matmul(
                            av_ps[:, 512 * g2:512 * (g2 + 1)],
                            PTs[:, 128 * i:128 * (i + 1)],
                            Vb[i][:, 512 * g2:512 * (g2 + 1)],
                            start=(i == 0), stop=(i == NT - 1)),
                         reads=[bPTs, bVb[i]], writes=[bAV],
                         quiet=not (i == NT - 1 and g2 == 1))
            P.op("v", lambda v, qt=qt: v.scalar_tensor_tensor(
                    acc[qt][:, :], acc[qt][:, :], fsc[:, 2:3], av_ps[:, :],
                    ALU.mult, ALU.add),
                 reads=[bfsc, bAV, bacc[qt]], writes=[bacc[qt]])

    att = [k.sb(f"att_{i}", [128, D], F32) for i in range(2)]
    batt = [Buf(f"att_{i}") for i in range(2)]
    for qt in range(NT):
        sl = qt % 2
        P.op("v", lambda v, qt=qt: v.reciprocal(fsc[:, 5:6], ml[:, 16 + qt:17 + qt]),
             reads=[bml[qt]], writes=[bfsc])
        P.op("v", lambda v, qt=qt, sl=sl: v.tensor_scalar_mul(
                att[sl][:, :], acc[qt][:, :], fsc[:, 5:6]),
             reads=[bacc[qt], bfsc], writes=[batt[sl]])
        P.op("y", lambda y, qt=qt, sl=sl: y.dma_start(
                out=A["a0_d"][128 * qt:128 * (qt + 1), :], in_=att[sl][:, :]),
             reads=[batt[sl]], dma=sta)
    P.raw_wait("y", (sta, sta.count))
    P.raw_wait("y", (strt, strt.count))




def emit_moe(k, A):
    nc, P = k.nc, k.P
    ld = P.dma_sem("mld")
    lcn = P.dma_sem("mlcn")
    stc = P.dma_sem("mstc")
    ccc = P.dma_sem("mccc", step=1)
    stz = P.dma_sem("mstz")
    sct = P.dma_sem("msct")
    ccr = P.dma_sem("mccr", step=1)
    ltb = P.dma_sem("mltb")
    gx = P.dma_sem("mgx")
    lw1 = P.dma_sem("mlw1")
    lw2 = P.dma_sem("mlw2")
    seo = P.dma_sem("mseo")
    cce = P.dma_sem("mcce", step=1)
    geo = P.dma_sem("mgeo")
    stm = P.dma_sem("mstm")

    ldgi = P.dma_sem("mldgi")
    ident = k.sb("mid_bf", [128, 128], BF16); bI = Buf("mid")
    P.op("g", lambda g: g.dma_start(out=ident[:, :], in_=A["ident"]), writes=[bI], dma=ldgi)
    rts = k.sb("mrts", [128, 32], F32); brt = Buf("mrts")
    P.op("y", lambda y: y.dma_start(out=rts[:, :], in_=A["rt_d"]), writes=[brt], dma=ld)
    slot_u = k.sb("mslotu", [128, NT], U32); bslu = Buf("mslotu")
    gidu = k.sb("mgidu", [128, NT], U32); bgu = Buf("mgidu")
    xgT = [k.sb(f"mxgT_{i}", [128, TOK], BF16) for i in range(DC)]
    bxgT = [Buf(f"mxgT_{i}") for i in range(DC)]

    k.push_scope()
    utc = k.sb("mut", [128, 128], F32); but = Buf("mut")
    P.op("y", lambda y: y.dma_start(out=utc[:, :], in_=A["ut"]), writes=[but], dma=ld)
    onec = k.sb("monec", [128, 1], F32); bo1 = Buf("monec")
    P.op("y", lambda y: y.dma_start(out=onec[:, :], in_=A["ones_col"]), writes=[bo1], dma=ld)
    oner = k.sb("moner", [1, 128], F32); bor = Buf("moner")
    P.op("y", lambda y: y.dma_start(out=oner[:, :], in_=A["ones_row"]), writes=[bor], dma=ld)
    pfm = k.sb("mpfm", [8, 1], F32); bpf = Buf("mpfm")
    P.op("y", lambda y: y.dma_start(out=pfm[:, :], in_=A["prefmask"]), writes=[bpf], dma=ld)
    gid1 = k.sb("mgid", [128, NT], F32); bgi = Buf("mgid")
    P.op("y", lambda y: y.dma_start(out=gid1[:, :], in_=A["gid1"]), writes=[bgi], dma=ld)

    idxf = rts[:, 16:32]
    ohs = k.sb("mohs", [128, 16 * NT], F32); boh = Buf("mohs")
    lr = k.sb("mlr", [128, 16 * NT], F32); blr = Buf("mlr")
    cnt = k.sb("mcnt", [1, 16], F32); bcn = Buf("mcnt")
    scr = k.sb("mscr", [1, 64], F32); bsc = Buf("mscr")
    cum_ps = k.ps("mcum", [128, NT], F32); bcp = Buf("mcum")
    cs_ps = k.ps("mcs", [1, NT], F32); bcs = Buf("mcs")
    for e in range(16):
        P.op("v", lambda v, e=e: v.tensor_single_scalar(
                ohs[:, NT * e:NT * (e + 1)], idxf, float(e), ALU.is_equal),
             reads=[brt], writes=[boh])
        P.op("t", lambda t, e=e: t.matmul(cs_ps[:, :], onec[:, :],
                                          ohs[:, NT * e:NT * (e + 1)],
                                          start=True, stop=True),
             reads=[bo1, boh], writes=[bcs])
        P.op("v", lambda v: v.tensor_copy(scr[0:1, 32:32 + NT], cs_ps[0:1, :]),
             reads=[bcs], writes=[bsc])
        P.op("v", lambda v: v.tensor_tensor_scan(scr[0:1, 0:NT], oner[0:1, 0:NT],
                                                 scr[0:1, 32:32 + NT], 0.0,
                                                 ALU.mult, ALU.add),
             reads=[bsc, bor], writes=[bsc])
        P.op("v", lambda v: v.tensor_tensor(scr[0:1, 16:16 + NT], scr[0:1, 0:NT],
                                            scr[0:1, 32:32 + NT], ALU.subtract),
             reads=[bsc], writes=[bsc])
        P.op("v", lambda v, e=e: v.tensor_copy(cnt[0:1, e:e + 1], scr[0:1, NT - 1:NT]),
             reads=[bsc], writes=[bcn])
        P.op("t", lambda t, e=e: t.matmul(cum_ps[:, :], utc[:, :],
                                          ohs[:, NT * e:NT * (e + 1)],
                                          start=True, stop=False),
             reads=[but, boh], writes=[bcp], quiet=True)
        P.op("t", lambda t, e=e: t.matmul(cum_ps[:, :], oner[:, :], scr[0:1, 16:16 + NT],
                                          start=False, stop=True),
             reads=[bor, bsc], writes=[bcp])
        P.op("v", lambda v, e=e: v.tensor_copy(lr[:, NT * e:NT * (e + 1)], cum_ps[:, :]),
             reads=[bcp], writes=[blr])

    P.op("y", lambda y: y.dma_start(out=A["cnt_in"], in_=cnt[0:1, :]), reads=[bcn], dma=stc)
    bcnag = Buf("cntag")
    P.op("g", lambda g: g.collective_compute("AllGather", ALU.bypass,
                                             replica_groups=GRP_ALL,
                                             ins=[A["cnt_in"]], outs=[A["cnt_ag"]]),
         reads=[bcn], writes=[bcnag], dma=ccc)
    cnts = k.sb("mcnts", [8, 16], F32); bcns = Buf("mcnts")
    P.op("y", lambda y: y.dma_start(out=cnts[:, :], in_=A["cnt_ag"]),
         reads=[bcnag], writes=[bcns], dma=lcn)
    prf_ps = k.ps("mprf", [128, 16], F32); bpp = Buf("mprf")
    P.op("t", lambda t: t.matmul(prf_ps[0:1, :], pfm[:, :], cnts[:, :],
                                 start=True, stop=True),
         reads=[bpf, bcns], writes=[bpp])
    prow = k.sb("mprow", [1, 16], F32); bpr = Buf("mprow")
    P.op("v", lambda v: v.tensor_copy(prow[0:1, :], prf_ps[0:1, :]),
         reads=[bpp], writes=[bpr])
    P.op("t", lambda t: t.matmul(prf_ps[:, :], oner[:, :], prow[0:1, :],
                                 start=True, stop=True),
         reads=[bor, bpr], writes=[bpp])
    pb = k.sb("mpb", [128, 16], F32); bpb = Buf("mpb")
    P.op("v", lambda v: v.tensor_copy(pb[:, :], prf_ps[:, :]), reads=[bpp], writes=[bpb])

    sel = k.sb("msel", [128, NT], F32); bsl = Buf("msel")
    tmp = k.sb("mtmp", [128, NT], F32); btm = Buf("mtmp")
    P.op("v", lambda v: v.memset(sel[:, :], -1.0), writes=[bsl])
    for e in range(16):
        P.op("v", lambda v, e=e: v.tensor_scalar_add(tmp[:, :], lr[:, NT * e:NT * (e + 1)],
                                                     pb[:, e:e + 1]),
             reads=[blr, bpb], writes=[btm])
        P.op("v", lambda v, e=e: v.tensor_tensor(tmp[:, :], tmp[:, :],
                                                 ohs[:, NT * e:NT * (e + 1)], ALU.mult),
             reads=[btm, boh], writes=[btm])
        P.op("v", lambda v: v.tensor_tensor(sel[:, :], sel[:, :], tmp[:, :], ALU.add),
             reads=[btm, bsl], writes=[bsl])
    vm = k.sb("mvm", [128, NT], F32); bvm = Buf("mvm")
    P.op("v", lambda v: v.tensor_single_scalar(vm[:, :], sel[:, :], float(CAP), ALU.is_ge),
         reads=[bsl], writes=[bvm])
    slot = k.sb("mslot", [128, NT], F32); bslo = Buf("mslot")
    P.op("v", lambda v: v.scalar_tensor_tensor(slot[:, :], idxf, float(CAP), sel[:, :],
                                               ALU.mult, ALU.add),
         reads=[brt, bsl], writes=[bslo])
    P.op("v", lambda v: v.scalar_tensor_tensor(slot[:, :], vm[:, :], OOBV, slot[:, :],
                                               ALU.mult, ALU.add),
         reads=[bvm, bslo], writes=[bslo])
    P.op("v", lambda v: v.tensor_copy(slot_u[:, :], slot[:, :]), reads=[bslo], writes=[bslu])
    if "dbg_d" in A:
        P.op("y", lambda y: y.dma_start(out=A["dbg_d"][:, 0:NT], in_=sel[:, :]),
             reads=[bsl], dma=stm)
        P.op("y", lambda y: y.dma_start(out=A["dbg_d"][:, NT:2 * NT], in_=slot[:, :]),
             reads=[bslo], dma=stm)

    zero = k.sb("mzero", [128, 1024], F32); bz = Buf("mzero")
    P.op("v", lambda v: v.memset(zero[:, :], 0.0), writes=[bz])
    btbl = Buf("mtbl")
    P.op("y", lambda y: y.dma_start(out=A["tbl_d"].rearrange("(a p) w -> p a w", p=128),
                                    in_=zero[:, :].rearrange("p (a w) -> p a w", w=8)),
         reads=[bz], writes=[btbl], dma=stz)
    gval = k.sb("mgval", [128, NT, 8], F32); bgv = Buf("mgval")
    P.op("v", lambda v: v.tensor_copy(gval[:, :, :],
                                      gid1[:, :].unsqueeze(2).broadcast_to([128, NT, 8])),
         reads=[bgi], writes=[bgv])
    for t in range(NT):
        P.op("g", lambda g, t=t: g.indirect_dma_start(
                out=A["tbl_d"],
                out_offset=IndirectOffsetOnAxis(ap=slot_u[:, t:t + 1], axis=0),
                in_=gval[:, t, :], in_offset=None,
                bounds_check=k.breg(g, B * N - 1), oob_is_err=False),
             reads=[bslu, bgv, btbl], writes=[btbl], dma=sct)
    btrs = Buf("mtrs")
    P.op("g", lambda g: g.collective_compute("ReduceScatter", ALU.add,
                                             replica_groups=GRP_ALL,
                                             ins=[A["tbl_d"]], outs=[A["tbl_rs"]]),
         reads=[btbl], writes=[btrs], dma=ccr)

    sts = k.sb("msts", [128, NT, 8], F32); bsts = Buf("msts")
    P.op("y", lambda y: y.dma_start(out=sts[:, :, :],
                                    in_=A["tbl_rs"].rearrange("(a p) w -> p a w", p=128)),
         reads=[btrs], writes=[bsts], dma=ltb)
    gidx = k.sb("mgidx", [128, NT], F32); bgx2 = Buf("mgidx")
    for t in range(NT):
        P.op("v", lambda v, t=t: v.tensor_single_scalar(gidx[:, t:t + 1], sts[:, t, 0:1],
                                                        0.0, ALU.is_equal),
             reads=[bsts], writes=[bgx2])
        P.op("v", lambda v, t=t: v.scalar_tensor_tensor(gidx[:, t:t + 1], gidx[:, t:t + 1],
                                                        OOBV, sts[:, t, 0:1],
                                                        ALU.mult, ALU.add),
             reads=[bgx2, bsts], writes=[bgx2])
        P.op("v", lambda v, t=t: v.tensor_scalar_add(gidx[:, t:t + 1], gidx[:, t:t + 1], -1.0),
             reads=[bgx2], writes=[bgx2])
    P.op("v", lambda v: v.tensor_copy(gidu[:, :], gidx[:, :]), reads=[bgx2], writes=[bgu])
    if "dbg_d" in A:
        P.op("y", lambda y: y.dma_start(out=A["dbg_d"][:, 2 * NT:3 * NT], in_=gidx[:, :]),
             reads=[bgx2], dma=stm)
    k.pop_scope()

    k.push_scope()
    xg = [k.sb(f"mxg_{i}", [128, D], F32) for i in range(2)]
    bxg = [Buf(f"mxg_{i}") for i in range(2)]
    xng = [k.sb(f"mxng_{i}", [128, D], BF16) for i in range(2)]
    bxng = [Buf(f"mxng_{i}") for i in range(2)]
    sqs = k.sb("msqs", [128, D], F32); bsq = Buf("msq")
    st1 = k.sb("mst1", [128, 8], F32); bst = Buf("mst")
    tr_ps = k.ps("mtr", [128, 512], BF16); btr = Buf("mtr")

    def ln_tile(xbuf, bxb, xnb, bxnb):
        P.op("v", lambda v: v.reduce_sum(st1[:, 0:1], xbuf[:, :], axis=AX.X),
             reads=[bxb], writes=[bst])
        P.op("s", lambda s: s.activation(sqs[:, :], xbuf[:, :], AF.Square,
                                         accum_out=st1[:, 1:2]),
             reads=[bxb], writes=[bsq, bst])
        P.op("v", lambda v: v.tensor_scalar_mul(st1[:, 2:3], st1[:, 0:1], 1.0 / D),
             reads=[bst], writes=[bst])
        P.op("v", lambda v: v.tensor_tensor(st1[:, 3:4], st1[:, 2:3], st1[:, 2:3], ALU.mult),
             reads=[bst], writes=[bst])
        P.op("v", lambda v: v.scalar_tensor_tensor(st1[:, 3:4], st1[:, 1:2], 1.0 / D,
                                                   st1[:, 3:4], ALU.mult, ALU.subtract),
             reads=[bst], writes=[bst])
        P.op("v", lambda v: v.tensor_scalar_add(st1[:, 3:4], st1[:, 3:4], 1e-5),
             reads=[bst], writes=[bst])
        P.op("s", lambda s: s.activation(st1[:, 4:5], st1[:, 3:4], AF.Sqrt),
             reads=[bst], writes=[bst])
        P.op("v", lambda v: v.reciprocal(st1[:, 4:5], st1[:, 4:5]), reads=[bst], writes=[bst])
        P.op("v", lambda v: v.scalar_tensor_tensor(st1[:, 5:6], st1[:, 2:3], -1.0,
                                                   st1[:, 4:5], ALU.mult, ALU.mult),
             reads=[bst], writes=[bst])
        P.op("v", lambda v: v.tensor_scalar(xnb[:, :], xbuf[:, :], st1[:, 4:5],
                                            st1[:, 5:6], ALU.mult, ALU.add),
             reads=[bxb, bst], writes=[bxnb])

    for t in range(NT):
        sl = t % 2
        P.op("v", lambda v, sl=sl: v.memset(xg[sl][:, :], 0.0), writes=[bxg[sl]])
        P.op("g", lambda g, t=t, sl=sl: g.indirect_dma_start(
                out=xg[sl][:, :], out_offset=None,
                in_=A["x_full"],
                in_offset=IndirectOffsetOnAxis(ap=gidu[:, t:t + 1], axis=0),
                bounds_check=k.breg(g, B * N - 1), oob_is_err=False),
             reads=[bgu, bxg[sl]], writes=[bxg[sl]], dma=gx)
        ln_tile(xg[sl], bxg[sl], xng[sl], bxng[sl])
        for ch in range(DC):
            q = ch % 4
            P.op("t", lambda te, ch=ch, sl=sl, q=q: te.transpose(
                    tr_ps[:, 128 * q:128 * (q + 1)],
                    xng[sl][:, 128 * ch:128 * (ch + 1)], ident[:, :]),
                 reads=[bxng[sl], bI], writes=[btr])
            P.op("v", lambda v, ch=ch, t=t, q=q: v.tensor_copy(
                    xgT[ch][:, 128 * t:128 * (t + 1)],
                    tr_ps[:, 128 * q:128 * (q + 1)]),
                 reads=[btr], writes=[bxgT[ch]])

    k.pop_scope()

    k.push_scope()
    w1c = [k.sb(f"mw1_{i}", [128, FF_MOE], BF16) for i in range(DC)]
    bw1 = [Buf(f"mw1_{i}") for i in range(DC)]
    h_sl = k.sb("mh", [128, 32 * 512], BF16); bh = Buf("mh")
    w2c = [k.sb(f"mw2_{i}", [128, D], BF16) for i in range(2)]
    bw2 = [Buf(f"mw2_{i}") for i in range(2)]
    h_ps = [k.ps(f"mhp_{i}", [128, 512], F32) for i in range(2)]
    bhp = [Buf(f"mhp_{i}") for i in range(2)]
    o_ps = [k.ps(f"mop_{i}", [128, D], F32) for i in range(2)]
    bop = [Buf(f"mop_{i}") for i in range(2)]
    eo = [k.sb(f"meo_{i}", [128, D], F32) for i in range(2)]
    beo = [Buf(f"meo_{i}") for i in range(2)]

    for e01 in range(2):
        for i in range(DC):
            P.op("g", lambda g, e01=e01, i=i: g.dma_start(
                    out=w1c[i][:, :], in_=A["moe_w1"][e01, 128 * i:128 * (i + 1), :]),
                 writes=[bw1[i]], dma=lw1)
        for sg in range(2):
            soff = 1024 * e01 + 512 * sg
            for f in range(32):
                sl = f % 2
                for kk in range(DC):
                    P.op("t", lambda t, f=f, kk=kk, sl=sl, soff=soff: t.matmul(
                            h_ps[sl][:, :], w1c[kk][:, 128 * f:128 * (f + 1)],
                            xgT[kk][:, soff:soff + 512],
                            start=(kk == 0), stop=(kk == DC - 1)),
                         reads=[bw1[kk], bxgT[kk]], writes=[bhp[sl]],
                         quiet=(kk != DC - 1))
                P.op("s", lambda s, f=f, sl=sl: s.activation(
                        h_sl[:, 512 * f:512 * (f + 1)], h_ps[sl][:, :], AF.Relu),
                     reads=[bhp[sl]], writes=[bh])
            for sp in range(2):
                for f in range(32):
                    sl = f % 2
                    P.op("g", lambda g, e01=e01, f=f, sl=sl: g.dma_start(
                            out=w2c[sl][:, :],
                            in_=A["moe_w2"][e01, 128 * f:128 * (f + 1), :]),
                         writes=[bw2[sl]], dma=lw2)
                    for hp in range(2):
                        st4 = 2 * sp + hp
                        for nn in range(2):
                            P.op("t", lambda t, f=f, st4=st4, hp=hp, nn=nn, sl=sl: t.matmul(
                                    o_ps[hp][:, 512 * nn:512 * (nn + 1)],
                                    h_sl[:, 512 * f + 128 * st4:512 * f + 128 * (st4 + 1)],
                                    w2c[sl][:, 512 * nn:512 * (nn + 1)],
                                    start=(f == 0), stop=(f == 31)),
                                 reads=[bh, bw2[sl]], writes=[bop[hp]],
                                 quiet=(nn == 0))
                for hp in range(2):
                    st4 = 2 * sp + hp
                    P.op("v", lambda v, hp=hp: v.tensor_copy(
                            eo[hp][:, :], o_ps[hp][:, :]),
                         reads=[bop[hp]], writes=[beo[hp]])
                    row = 1024 * e01 + 512 * sg + 128 * st4
                    P.op("y", lambda y, row=row, hp=hp: y.dma_start(
                            out=A["eo_d"][row:row + 128, :], in_=eo[hp][:, :]),
                         reads=[beo[hp]], dma=seo)
                    if "eo_dbg" in A:
                        P.op("y", lambda y, row=row, hp=hp: y.dma_start(
                                out=A["eo_dbg"][row:row + 128, :], in_=eo[hp][:, :]),
                             reads=[beo[hp]], dma=seo)

    beod = Buf("meod"); beag = Buf("meag")
    P.group_end([beod], seo)
    P.op("g", lambda g: g.collective_compute("AllGather", ALU.bypass,
                                             replica_groups=GRP_ALL,
                                             ins=[A["eo_d"]], outs=[A["eo_ag"]]),
         reads=[beod], writes=[beag], dma=cce)

    k.pop_scope()

    cg = [k.sb(f"mcg_{i}", [128, D], F32) for i in range(2)]
    bcg = [Buf(f"mcg_{i}") for i in range(2)]
    mo = [k.sb(f"mmo_{i}", [128, D], F32) for i in range(2)]
    bmo = [Buf(f"mmo_{i}") for i in range(2)]
    for t in range(NT):
        sl = t % 2
        P.op("v", lambda v, sl=sl: v.memset(cg[sl][:, :], 0.0), writes=[bcg[sl]])
        P.op("g", lambda g, t=t, sl=sl: g.indirect_dma_start(
                out=cg[sl][:, :], out_offset=None,
                in_=A["eo_ag"],
                in_offset=IndirectOffsetOnAxis(ap=slot_u[:, t:t + 1], axis=0),
                bounds_check=k.breg(g, B * N - 1), oob_is_err=False),
             reads=[bslu, bcg[sl], beag], writes=[bcg[sl]], dma=geo)
        P.op("v", lambda v, t=t, sl=sl: v.tensor_scalar_mul(
                mo[sl][:, :], cg[sl][:, :], rts[:, t:t + 1]),
             reads=[bcg[sl], brt], writes=[bmo[sl]])
        P.op("y", lambda y, t=t, sl=sl: y.dma_start(
                out=A["moe_d"][128 * t:128 * (t + 1), :], in_=mo[sl][:, :]),
             reads=[bmo[sl]], dma=stm)
    P.raw_wait("y", (stm, stm.count))



def emit_l1a(k, A):
    nc, P = k.nc, k.P
    ld = P.dma_sem("ald")
    lx = P.dma_sem("alx")
    sx1 = P.dma_sem("asx1")
    sxn = P.dma_sem("asxn")
    sct = P.dma_sem("asct")
    ccx = P.dma_sem("accx", step=1)
    lct = P.dma_sem("alct")
    spt = P.dma_sem("aspt")

    ldy = P.dma_sem("aldy")
    identb = k.sb("aid_b", [128, 128], BF16); bIb = Buf("aidb")
    P.op("g", lambda g: g.dma_start(out=identb[:, :], in_=A["ident"]), writes=[bIb], dma=ld)
    identf = k.sb("aid_f", [128, 128], F32); bIf = Buf("aidf")
    P.op("y", lambda y: y.dma_start(out=identf[:, :], in_=A["ident"]), writes=[bIf], dma=ldy)
    wl = [k.sb(f"awl_{i}", [128, 1536], BF16) for i in range(DC)]
    bwl = [Buf(f"awl_{i}") for i in range(DC)]
    for i in range(DC):
        P.op("g", lambda g, i=i: g.dma_start(out=wl[i][:, :],
                                             in_=A["l1a_w"][128 * i:128 * (i + 1), :]),
             writes=[bwl[i]], dma=ld)
    wo = [k.sb(f"awo_{i}", [128, D], BF16) for i in range(4)]
    bwo = [Buf(f"awo_{i}") for i in range(4)]
    for i in range(4):
        P.op("g", lambda g, i=i: g.dma_start(out=wo[i][:, :],
                                             in_=A["l1a_wo"][128 * i:128 * (i + 1), :]),
             writes=[bwo[i]], dma=ld)
    bo = k.sb("abo", [128, DC], F32); bbo = Buf("abo")
    P.op("y", lambda y: y.dma_start(out=bo[:, :], in_=A["l1a_bo"]), writes=[bbo], dma=ldy)

    xn1b = [k.sb(f"axn_{i}", [128, TOK], BF16) for i in range(DC)]
    bxn = [Buf(f"axn_{i}") for i in range(DC)]
    qe = [k.sb(f"aqe_{i}", [64, TOK], BF16) for i in range(8)]
    bqe = [Buf(f"aqe_{i}") for i in range(8)]
    ke = [k.sb(f"ake_{i}", [128, 512], BF16) for i in range(NT)]
    bke = [Buf(f"ake_{i}") for i in range(NT)]
    ve = [k.sb(f"ave_{i}", [128, 512], BF16) for i in range(NT)]
    bve = [Buf(f"ave_{i}") for i in range(NT)]
    inn = [k.sb(f"ainn_{i}", [128, TOK], BF16) for i in range(4)]
    binn = [Buf(f"ainn_{i}") for i in range(4)]
    ctxb = k.sb("actxb", [64, 512], BF16); bcxb = Buf("acxb")

    k.push_scope()
    xa = [k.sb(f"axa_{i}", [128, D], F32) for i in range(2)]
    bxa = [Buf(f"axa_{i}") for i in range(2)]
    xb = k.sb("axb", [128, D], F32); bxb2 = Buf("axb")
    xc = k.sb("axc", [128, D], F32); bxc = Buf("axc")
    xnt_b = [k.sb(f"axnt_{i}", [128, D], BF16) for i in range(2)]
    bxnt = [Buf(f"axnt_{i}") for i in range(2)]
    xnt_f = [k.sb(f"axnf_{i}", [128, D], F32) for i in range(2)]
    bxnf = [Buf(f"axnf_{i}") for i in range(2)]
    sqs = k.sb("asqs", [128, D], F32); bsq = Buf("asq")
    st1 = k.sb("ast1", [128, 8], F32); bst = Buf("ast")
    tr_pb = k.ps("atrb", [128, 512], BF16); btrb = Buf("atrb")
    tr_pf = k.ps("atrf", [128, 512], F32); btrf = Buf("atrf")
    xnf_row = [k.sb(f"axnr_{i}", [128, D], F32) for i in range(2)]
    bxnr = [Buf(f"axnr_{i}") for i in range(2)]

    for t in range(NT):
        sl = t % 2
        P.op("y", lambda y, t=t, sl=sl: y.dma_start(
                out=xa[sl][:, :], in_=A["x_own"][128 * t:128 * (t + 1), :]),
             writes=[bxa[sl]], dma=lx)
        P.op("y", lambda y, t=t: y.dma_start(
                out=xb[:, :], in_=A["a0_d"][128 * t:128 * (t + 1), :]),
             writes=[bxb2], dma=lx)
        P.op("y", lambda y, t=t: y.dma_start(
                out=xc[:, :], in_=A["moe_d"][128 * t:128 * (t + 1), :]),
             writes=[bxc], dma=lx)
        P.op("v", lambda v, sl=sl: v.tensor_tensor(xa[sl][:, :], xa[sl][:, :],
                                                   xb[:, :], ALU.add),
             reads=[bxb2, bxa[sl]], writes=[bxa[sl]])
        P.op("v", lambda v, sl=sl: v.tensor_tensor(xa[sl][:, :], xa[sl][:, :],
                                                   xc[:, :], ALU.add),
             reads=[bxc, bxa[sl]], writes=[bxa[sl]])
        P.op("y", lambda y, t=t, sl=sl: y.dma_start(
                out=A["x1_d"][128 * t:128 * (t + 1), :], in_=xa[sl][:, :]),
             reads=[bxa[sl]], dma=sx1)
        P.op("v", lambda v, sl=sl: v.reduce_sum(st1[:, 0:1], xa[sl][:, :], axis=AX.X),
             reads=[bxa[sl]], writes=[bst])
        P.op("s", lambda s, sl=sl: s.activation(sqs[:, :], xa[sl][:, :], AF.Square,
                                                accum_out=st1[:, 1:2]),
             reads=[bxa[sl]], writes=[bsq, bst])
        P.op("v", lambda v: v.tensor_scalar_mul(st1[:, 2:3], st1[:, 0:1], 1.0 / D),
             reads=[bst], writes=[bst])
        P.op("v", lambda v: v.tensor_tensor(st1[:, 3:4], st1[:, 2:3], st1[:, 2:3], ALU.mult),
             reads=[bst], writes=[bst])
        P.op("v", lambda v: v.scalar_tensor_tensor(st1[:, 3:4], st1[:, 1:2], 1.0 / D,
                                                   st1[:, 3:4], ALU.mult, ALU.subtract),
             reads=[bst], writes=[bst])
        P.op("v", lambda v: v.tensor_scalar_add(st1[:, 3:4], st1[:, 3:4], 1e-5),
             reads=[bst], writes=[bst])
        P.op("s", lambda s: s.activation(st1[:, 4:5], st1[:, 3:4], AF.Sqrt),
             reads=[bst], writes=[bst])
        P.op("v", lambda v: v.reciprocal(st1[:, 4:5], st1[:, 4:5]), reads=[bst], writes=[bst])
        P.op("v", lambda v: v.scalar_tensor_tensor(st1[:, 5:6], st1[:, 2:3], -1.0,
                                                   st1[:, 4:5], ALU.mult, ALU.mult),
             reads=[bst], writes=[bst])
        P.op("v", lambda v, sl=sl: v.tensor_scalar(xnt_b[sl][:, :], xa[sl][:, :],
                                                   st1[:, 4:5], st1[:, 5:6],
                                                   ALU.mult, ALU.add),
             reads=[bxa[sl], bst], writes=[bxnt[sl]])
        P.op("v", lambda v, sl=sl: v.tensor_scalar(xnt_f[sl][:, :], xa[sl][:, :],
                                                   st1[:, 4:5], st1[:, 5:6],
                                                   ALU.mult, ALU.add),
             reads=[bxa[sl], bst], writes=[bxnf[sl]])
        for ch in range(DC):
            q = ch % 4
            P.op("t", lambda te, ch=ch, sl=sl, q=q: te.transpose(
                    tr_pb[:, 128 * q:128 * (q + 1)],
                    xnt_b[sl][:, 128 * ch:128 * (ch + 1)], identb[:, :]),
                 reads=[bxnt[sl], bIb], writes=[btrb])
            P.op("v", lambda v, ch=ch, t=t, q=q: v.tensor_copy(
                    xn1b[ch][:, 128 * t:128 * (t + 1)],
                    tr_pb[:, 128 * q:128 * (q + 1)]),
                 reads=[btrb], writes=[bxn[ch]])
            P.op("t", lambda te, ch=ch, sl=sl, q=q: te.transpose(
                    tr_pf[:, 128 * q:128 * (q + 1)],
                    xnt_f[sl][:, 128 * ch:128 * (ch + 1)], identf[:, :]),
                 reads=[bxnf[sl], bIf], writes=[btrf])
            P.op("v", lambda v, ch=ch, sl=sl, q=q: v.tensor_copy(
                    xnf_row[sl][:, 128 * ch:128 * (ch + 1)],
                    tr_pf[:, 128 * q:128 * (q + 1)]),
                 reads=[btrf], writes=[bxnr[sl]])
        P.op("y", lambda y, t=t, sl=sl: y.dma_start(
                out=A["xn1fm_d"].rearrange("(c p) n -> p c n", p=128)[:, :, 128 * t:128 * (t + 1)],
                in_=xnf_row[sl][:, :].rearrange("p (c n) -> p c n", n=128)),
             reads=[bxnr[sl]], dma=sxn)
    k.pop_scope()

    k.push_scope()
    pj = [k.ps(f"apj_{i}", [128, 512], F32) for i in range(2)]
    bpj = [Buf(f"apj_{i}") for i in range(2)]
    kv_ps = [k.ps(f"akv_{i}", [128, D], F32) for i in range(2)]
    bkv = [Buf(f"akv_{i}") for i in range(2)]
    ea = k.sb("aea", [128, 512], F32); bea = Buf("aea")
    eb = k.sb("aeb", [128, 512], F32); beb = Buf("aeb")
    ec = k.sb("aec", [128, 512], F32); bec = Buf("aec")
    for m in range(4):
        for n in range(4):
            sl = n % 2
            for kk in range(DC):
                P.op("t", lambda te, m=m, n=n, kk=kk, sl=sl: te.matmul(
                        pj[sl][:, :], wl[kk][:, 128 * m:128 * (m + 1)],
                        xn1b[kk][:, 512 * n:512 * (n + 1)],
                        start=(kk == 0), stop=(kk == DC - 1)),
                     reads=[bwl[kk], bxn[kk]], writes=[bpj[sl]], quiet=(kk != DC - 1))
            P.op("v", lambda v, sl=sl: v.tensor_scalar_min(ea[:, :], pj[sl][:, :], 0.0),
                 reads=[bpj[sl]], writes=[bea])
            P.op("v", lambda v, sl=sl: v.tensor_scalar_max(eb[:, :], pj[sl][:, :], 0.0),
                 reads=[bpj[sl]], writes=[beb])
            P.op("s", lambda s: s.activation(ea[:, :], ea[:, :], AF.Exp),
                 reads=[bea], writes=[bea])
            for hh in range(2):
                P.op("v", lambda v, m=m, n=n, hh=hh: v.tensor_tensor(
                        qe[2 * m + hh][:, 512 * n:512 * (n + 1)],
                        ea[64 * hh:64 * (hh + 1), :], eb[64 * hh:64 * (hh + 1), :],
                        ALU.add),
                     reads=[bea, beb], writes=[bqe[2 * m + hh]])
    for t in range(NT):
        sl = t % 2
        for nn in range(2):
            for kk in range(DC):
                P.op("t", lambda te, t=t, nn=nn, kk=kk, sl=sl: te.matmul(
                        kv_ps[sl][:, 512 * nn:512 * (nn + 1)],
                        xn1b[kk][:, 128 * t:128 * (t + 1)],
                        wl[kk][:, 512 + 512 * nn:512 + 512 * (nn + 1)],
                        start=(kk == 0), stop=(kk == DC - 1)),
                     reads=[bxn[kk], bwl[kk]], writes=[bkv[sl]],
                     quiet=not (kk == DC - 1 and nn == 1))
        P.op("v", lambda v, t=t, sl=sl: v.tensor_scalar_min(ea[:, :], kv_ps[sl][:, 0:512], 0.0),
             reads=[bkv[sl]], writes=[bea])
        P.op("v", lambda v, t=t, sl=sl: v.tensor_scalar_max(eb[:, :], kv_ps[sl][:, 0:512], 0.0),
             reads=[bkv[sl]], writes=[beb])
        P.op("s", lambda s: s.activation(ea[:, :], ea[:, :], AF.Exp),
             reads=[bea], writes=[bea])
        P.op("v", lambda v, t=t: v.tensor_tensor(ke[t][:, :], ea[:, :], eb[:, :], ALU.add),
             reads=[bea, beb], writes=[bke[t]])
        P.op("v", lambda v, t=t, sl=sl: v.tensor_copy(ve[t][:, :], kv_ps[sl][:, 512:1024]),
             reads=[bkv[sl]], writes=[bve[t]])

    ctx_ps = k.ps("actx", [64, 512], F32); bctx = Buf("actx")
    for t in range(NT):
        for h in range(8):
            P.op("t", lambda te, t=t, h=h: te.matmul(
                    ctx_ps[0:64, 64 * h:64 * (h + 1)],
                    ke[t][:, 64 * h:64 * (h + 1)], ve[t][:, 64 * h:64 * (h + 1)],
                    start=(t == 0 and h == 0), stop=(t == NT - 1 and h == 7)),
                 reads=[bke[t], bve[t]], writes=[bctx],
                 quiet=not (t == NT - 1 and h == 7))
    ctxs = k.sb("actxs", [64, 512], F32); bcxs = Buf("acxs")
    P.op("v", lambda v: v.tensor_copy(ctxs[:, :], ctx_ps[0:64, :]), reads=[bctx], writes=[bcxs])
    P.op("y", lambda y: y.dma_start(out=A["ctx_in"], in_=ctxs[:, :]), reads=[bcxs], dma=sct)
    bcin = Buf("acin"); bcar = Buf("acar")
    P.group_end([bcin], sct)
    P.op("g", lambda g: g.collective_compute("AllReduce", ALU.add,
                                             replica_groups=GRP_BATCH,
                                             ins=[A["ctx_in"]], outs=[A["ctx_ar"]]),
         reads=[bcin], writes=[bcar], dma=ccx)
    P.op("g", lambda g: g.dma_start(out=ctxb[:, :], in_=A["ctx_ar"]),
         reads=[bcar], writes=[bcxb], dma=lct)
    k.pop_scope()

    k.push_scope()
    oc_ps = [k.ps(f"aoc_{i}", [64, 512], F32) for i in range(2)]
    boc = [Buf(f"aoc_{i}") for i in range(2)]
    for h in range(8):
        for n in range(4):
            sl = n % 2
            P.op("t", lambda te, h=h, n=n, sl=sl: te.matmul(
                    oc_ps[sl][0:64, :], ctxb[0:64, 64 * h:64 * (h + 1)],
                    qe[h][:, 512 * n:512 * (n + 1)],
                    start=True, stop=True),
                 reads=[bcxb, bqe[h]], writes=[boc[sl]])
            P.op("v", lambda v, h=h, n=n, sl=sl: v.tensor_copy(
                    inn[h // 2][64 * (h % 2):64 * (h % 2) + 64, 512 * n:512 * (n + 1)],
                    oc_ps[sl][0:64, :]),
                 reads=[boc[sl]], writes=[binn[h // 2]])
    po = [k.ps(f"apo_{i}", [128, 512], F32) for i in range(2)]
    bpo = [Buf(f"apo_{i}") for i in range(2)]
    pst = [k.sb(f"apst_{i}", [128, 512], F32) for i in range(2)]
    bpst = [Buf(f"apst_{i}") for i in range(2)]
    for m in range(DC):
        for n in range(4):
            sl = n % 2
            for kc in range(4):
                P.op("t", lambda te, m=m, n=n, kc=kc, sl=sl: te.matmul(
                        po[sl][:, :], wo[kc][:, 128 * m:128 * (m + 1)],
                        inn[kc][:, 512 * n:512 * (n + 1)],
                        start=(kc == 0), stop=(kc == 3)),
                     reads=[bwo[kc], binn[kc]], writes=[bpo[sl]], quiet=(kc != 3))
            P.op("v", lambda v, m=m, sl=sl: v.tensor_scalar_add(
                    pst[sl][:, :], po[sl][:, :], bo[:, m:m + 1]),
                 reads=[bpo[sl], bbo], writes=[bpst[sl]])
            P.op("y", lambda y, m=m, n=n, sl=sl: y.dma_start(
                    out=A["part_d"][128 * m:128 * (m + 1), 512 * n:512 * (n + 1)],
                    in_=pst[sl][:, :]),
                 reads=[bpst[sl]], dma=spt)
    k.pop_scope()
    P.raw_wait("y", (spt, spt.count))
    P.raw_wait("y", (sx1, sx1.count))
    P.raw_wait("y", (sxn, sxn.count))


def build_launch_l1a():
    nc = bass.Bass("TRN2", target_bir_lowering=False)
    P = Plan()
    ctx = contextlib.ExitStack()
    k = K(nc, ctx, P)
    A = {}
    A["x_own"] = k.inp("x_own", [TOK, D])
    A["a0_d"] = k.inp("a0_d", [TOK, D])
    A["moe_d"] = k.inp("moe_d", [TOK, D])
    A["l1a_w"] = k.inp("l1a_w", [D, 1536])
    A["l1a_wo"] = k.inp("l1a_wo", [512, D])
    A["l1a_bo"] = k.inp("l1a_bo", [128, DC])
    A["ident"] = k.inp("ident", [128, 128])
    A["x1_d"] = k.outp("x1_d", [TOK, D])
    A["xn1fm_d"] = k.outp("xn1fm_d", [D, TOK])
    A["part_d"] = k.outp("part_d", [D, TOK])
    A["ctx_in"] = k.dram("ctx_in", [64, 512])
    A["ctx_ar"] = k.dram("ctx_ar", [64, 512])
    with ctx:
        emit_l1a(k, A)
        with nc.Block() as block:
            P.emit(nc, ctx, block)
    return nc, ["x_own", "a0_d", "moe_d", "l1a_w", "l1a_wo", "l1a_bo", "ident"], \
        ["x1_d", "xn1fm_d", "part_d"]




SDH = HALO


def emit_l1f(k, A):
    nc, P = k.nc, k.P
    ld = P.dma_sem("fld")
    ldg2 = P.dma_sem("fldg2")
    lhb = P.dma_sem("flhb")
    lwq = P.dma_sem("flwq")
    lwo = P.dma_sem("flwo")
    lw1 = P.dma_sem("flw1")
    lw2 = P.dma_sem("flw2")
    ltb = P.dma_sem("fltb")
    shal = P.dma_sem("fshal")
    cch = P.dma_sem("fcch", step=1)
    ghal = P.dma_sem("fghal")
    swr = P.dma_sem("fswr")
    lsd = P.dma_sem("flsd")
    lres = P.dma_sem("flres")
    sy = P.dma_sem("fsy")
    lfin = P.dma_sem("flfin")

    identb = k.sb("fid_b", [128, 128], BF16); bIb = Buf("fidb")
    P.op("g", lambda g: g.dma_start(out=identb[:, :], in_=A["ident"]), writes=[bIb], dma=ldg2)
    identf = k.sb("fid_f", [128, 128], F32); bIf = Buf("fidf")
    P.op("y", lambda y: y.dma_start(out=identf[:, :], in_=A["ident"]), writes=[bIf], dma=ld)
    onesr = k.sb("fonesr", [1, 128], F32); bor = Buf("fonesr")
    P.op("y", lambda y: y.dma_start(out=onesr[:, :], in_=A["ones_row"]), writes=[bor], dma=ld)
    onesc = k.sb("fonesc", [128, 1], F32); boc = Buf("fonesc")
    P.op("y", lambda y: y.dma_start(out=onesc[:, :], in_=A["ones_col"]), writes=[boc], dma=ld)
    permf = k.sb("fperm", [128, 128], F32); bpm = Buf("fperm")
    P.op("y", lambda y: y.dma_start(out=permf[:, :], in_=A["perm"]), writes=[bpm], dma=ld)
    hself = k.sb("fhsel", [128, DC * 16], F32); bhs = Buf("fhsel")
    P.op("y", lambda y: y.dma_start(out=hself[:, :], in_=A["hsel"]), writes=[bhs], dma=ld)
    hbcf = k.sb("fhbc", [16, DC * 128], F32); bhb = Buf("fhbc")
    P.op("y", lambda y: y.dma_start(out=hbcf[:, :], in_=A["hbc"]), writes=[bhb], dma=ld)
    msk = k.sb("fmsk", [128, 5, 512], BF16); bmk = Buf("fmsk")
    P.op("g", lambda g: g.dma_start(out=msk[:, 0:2, :],
                                    in_=A["msk_w0"].rearrange("a p w -> p a w")),
         writes=[bmk], dma=ldg2)
    P.op("g", lambda g: g.dma_start(out=msk[:, 2:4, :],
                                    in_=A["msk_int"].rearrange("a p w -> p a w")),
         writes=[bmk], dma=ldg2)
    P.op("g", lambda g: g.dma_start(out=msk[:, 4:5, :], in_=A["msk_ext"].unsqueeze(1)),
         writes=[bmk], dma=ldg2)
    hgate = k.sb("fhg", [128, 1], F32); bhg = Buf("fhg")
    P.op("y", lambda y: y.dma_start(out=hgate[:, :], in_=A["halo_gate"]), writes=[bhg], dma=ld)
    hrows = k.sb("fhrows", [128, DC], F32); bhr = Buf("fhrows")
    P.op("y", lambda y: y.dma_start(out=hrows[:, :], in_=A["halo_rows"]), writes=[bhr], dma=ld)
    hrowu = k.sb("fhrowu", [128, DC], U32); bhru = Buf("fhrowu")
    P.op("v", lambda v: v.tensor_copy(hrowu[:, :], hrows[:, :]), reads=[bhr], writes=[bhru])

    sd = A["sd"]
    bsd = Buf("fsd")
    bhal = Buf("fhal")
    bhag = Buf("fhag")

    pq = k.sb("fpq", [128, 128], BF16); bpq = Buf("fpq")
    pk = k.sb("fpk", [128, 128], BF16); bpk = Buf("fpk")
    hbw = k.sb("fhbw", [16, DC * 128], F32); bhw = Buf("fhbw")
    b1c = k.sb("fb1c", [128, F1P // 128], F32); bb1 = Buf("fb1c")
    b2c = k.sb("fb2c", [128, DC], F32); bb2 = Buf("fb2c")
    lngc = k.sb("flng", [128, DC], F32); blg = Buf("flng")
    lnbc = k.sb("flnb", [128, DC], F32); blb = Buf("flnb")
    qsc = k.sb("fqsc", [128, 1], F32); bqsc = Buf("fqsc")
    ksc = k.sb("fksc", [128, 1], F32); bksc = Buf("fksc")
    qsr = k.sb("fqsr", [1, 128], F32); bqsr = Buf("fqsr")

    k.push_scope()
    tmpc = [k.sb(f"ftmp_{i}", [128, TOK], F32) for i in range(2)]
    btmp = [Buf(f"ftmp_{i}") for i in range(2)]
    for ch in range(DC):
        sl = ch % 2
        P.op("y", lambda y, ch=ch, sl=sl: y.dma_start(
                out=tmpc[sl][:, :], in_=A["xn1fm_d"][128 * ch:128 * (ch + 1), :]),
             writes=[btmp[sl]], dma=lfin)
        P.op("y", lambda y, ch=ch, sl=sl: y.dma_start(
                out=sd[128 * ch:128 * (ch + 1), SDH:SDH + TOK], in_=tmpc[sl][:, :]),
             reads=[btmp[sl]], writes=[bsd], dma=swr)
    k.pop_scope(fence=False)

    for li in range(_L1F_DEPTH):
        k.push_scope()
        qs_bc = k.ps("fqsbc", [128, 128], F32); bqsb = Buf("fqsbc")
        P.op("y", lambda y, li=li: y.dma_start(out=qsc[:, :], in_=A["f_qs_c"][li]),
             writes=[bqsc], dma=ld)
        P.op("y", lambda y, li=li: y.dma_start(out=ksc[:, :], in_=A["f_ks_c"][li]),
             writes=[bksc], dma=ld)
        P.op("v", lambda v: v.tensor_scalar_mul(pq[:, :], permf[:, :], qsc[:, 0:1]),
             reads=[bpm, bqsc], writes=[bpq])
        P.op("v", lambda v: v.tensor_scalar_mul(pk[:, :], permf[:, :], ksc[:, 0:1]),
             reads=[bpm, bksc], writes=[bpk])
        P.op("y", lambda y, li=li: y.dma_start(out=qsr[:, :], in_=A["f_ks"][li]),
             writes=[bqsr], dma=ld)
        P.op("t", lambda t: t.matmul(qs_bc[0:16, :], onesr[0:1, 0:16], qsr[:, :],
                                     start=True, stop=True),
             reads=[bor, bqsr], writes=[bqsb])
        P.op("v", lambda v: v.tensor_tensor(
                hbw[:, :].rearrange("h (c n) -> h c n", n=128),
                hbcf[:, :].rearrange("h (c n) -> h c n", n=128),
                qs_bc[0:16, :].unsqueeze(1).broadcast_to([16, DC, 128]), ALU.mult),
             reads=[bhb, bqsb], writes=[bhw])
        P.op("y", lambda y, li=li: y.dma_start(out=b1c[:, :], in_=A["f_b1p"][li]),
             writes=[bb1], dma=ld)
        P.op("y", lambda y, li=li: y.dma_start(out=b2c[:, :], in_=A["f_b2"][li]),
             writes=[bb2], dma=ld)
        P.op("y", lambda y, li=li: y.dma_start(out=lngc[:, :], in_=A["f_lng"][li]),
             writes=[blg], dma=ld)
        P.op("y", lambda y, li=li: y.dma_start(out=lnbc[:, :], in_=A["f_lnb"][li]),
             writes=[blb], dma=ld)
        k.pop_scope()

        k.push_scope()
        hb = [k.sb(f"fhb_{i}", [128, SDH], F32) for i in range(2)]
        bhbb = [Buf(f"fhb_{i}") for i in range(2)]
        for ch in range(DC):
            sl = ch % 2
            P.op("y", lambda y, ch=ch, sl=sl: y.dma_start(
                    out=hb[sl][:, :], in_=sd[128 * ch:128 * (ch + 1), TOK:TOK + SDH]),
                 reads=[bsd], writes=[bhbb[sl]], dma=lhb)
            P.op("y", lambda y, ch=ch, sl=sl: y.dma_start(
                    out=A["hal_in"][128 * ch:128 * (ch + 1), :], in_=hb[sl][:, :]),
                 reads=[bhbb[sl]], writes=[bhal], dma=shal)
        P.op("g", lambda g: g.collective_compute("AllGather", ALU.bypass,
                                                 replica_groups=GRP_BATCH,
                                                 ins=[A["hal_in"]], outs=[A["hal_ag"]]),
             reads=[bhal], writes=[bhag], dma=cch)
        for ch in range(DC):
            sl = ch % 2
            P.op("v", lambda v, sl=sl: v.memset(hb[sl][:, :], 0.0), writes=[bhbb[sl]])
            P.op("g", lambda g, ch=ch, sl=sl: g.indirect_dma_start(
                    out=hb[sl][:, :], out_offset=None,
                    in_=A["hal_ag"],
                    in_offset=IndirectOffsetOnAxis(ap=hrowu[:, ch:ch + 1], axis=0),
                    bounds_check=k.breg(g, 4 * D - 1), oob_is_err=False),
                 reads=[bhru, bhag, bhbb[sl]], writes=[bhbb[sl]], dma=ghal)
            P.op("v", lambda v, sl=sl: v.tensor_scalar_mul(hb[sl][:, :], hb[sl][:, :],
                                                           hgate[:, 0:1]),
                 reads=[bhg, bhbb[sl]], writes=[bhbb[sl]])
            P.op("y", lambda y, ch=ch, sl=sl: y.dma_start(
                    out=sd[128 * ch:128 * (ch + 1), 0:SDH], in_=hb[sl][:, :]),
                 reads=[bhbb[sl]], writes=[bsd], dma=swr)
        k.pop_scope()

        k.push_scope()
        wq = [k.sb(f"fwq_{i}", [128, 3072], BF16) for i in range(DC)]
        bwq = [Buf(f"fwq_{i}") for i in range(DC)]
        for i in range(DC):
            P.op("g", lambda g, i=i, li=li: g.dma_start(
                    out=wq[i][:, :], in_=A["f_wqkv"][li, 128 * i:128 * (i + 1), :]),
                 writes=[bwq[i]], dma=lwq)
        wo = [k.sb(f"fwo_{i}", [128, D], BF16) for i in range(DC)]
        bwo = [Buf(f"fwo_{i}") for i in range(DC)]
        for i in range(DC):
            P.op("g", lambda g, i=i, li=li: g.dma_start(
                    out=wo[i][:, :], in_=A["f_wo"][li, 128 * i:128 * (i + 1), :]),
                 writes=[bwo[i]], dma=lwo)

        xshw = [k.sb(f"fxsh_{i}", [128, WIN], BF16) for i in range(DC)]
        bxw = [Buf(f"fxsh_{i}") for i in range(DC)]
        qsb = [[k.sb(f"fqsb_{p}_{i}", [128, WIN], BF16) for i in range(DC)]
               for p in range(2)]
        bqs = [[Buf(f"fqsb_{p}_{i}") for i in range(DC)] for p in range(2)]
        ksl = [[k.sb(f"fksl_{p}_{i}", [128, WIN], BF16) for i in range(DC)]
               for p in range(3)]
        bks = [[Buf(f"fksl_{p}_{i}") for i in range(DC)] for p in range(3)]
        vsl = [[k.sb(f"fvsl_{p}_{i}", [128, D], BF16) for i in range(2)]
               for p in range(3)]
        bvs = [[Buf(f"fvsl_{p}_{i}") for i in range(2)] for p in range(3)]
        tabs = k.sb("ftabs", [128, 4, WIN], F32); btab = Buf("ftabs")
        rot1 = k.sb("frot1", [128, WIN], F32); br1 = Buf("frot1")
        rot3 = k.sb("frot3", [128, WIN], F32); br3 = Buf("frot3")
        sqsc = k.sb("fsqsc", [128, WIN], F32); bsqc = Buf("fsqsc")
        nrm = k.sb("fnrm", [16, WIN], F32); bnr = Buf("fnrm")
        rkf = k.sb("frkf", [16, WIN], F32); brk = Buf("frkf")
        rqt = [k.sb(f"frqt_{p}", [128, 2, 16], F32) for p in range(2)]
        brq = [Buf(f"frqt_{p}") for p in range(2)]
        stt = [k.sb("fstt", [128, 8], F32) for _ in range(2)]
        bstt = [Buf("fstt") for _ in range(2)]
        Pw = [k.sb("fPw", [128, 512], BF16) for _ in range(2)]
        bPw = [Buf("fPw") for _ in range(2)]
        PTs = [k.sb("fPTs", [128, 512], BF16) for _ in range(2)]
        bPT = [Buf("fPTs") for _ in range(2)]
        aosb = k.sb("faosb", [128, D], BF16); bao = Buf("faosb")
        aoT = [k.sb(f"faoT_{i}", [128, 128], BF16) for i in range(DC)]
        baoT = [Buf(f"faoT_{i}") for i in range(DC)]
        res = [k.sb(f"fres_{i}", [128, WIN], F32) for i in range(DC)]
        bres = [Buf(f"fres_{i}") for i in range(DC)]

        pj_ps = [k.ps(f"fpj_{i}", [128, 512], F32) for i in range(2)]
        bpj = [Buf(f"fpj_{i}") for i in range(2)]
        rh_ps = k.ps("frh", [128, WIN], F32); brh = Buf("frh")
        n2_ps = rh_ps; bn2 = brh
        rb_ps = rh_ps; brb = brh
        s_ps = [k.ps("fs", [128, 512], F32) for _ in range(2)]
        bS = [Buf("fs") for _ in range(2)]
        pt_ps = k.ps("fpt", [128, 512], BF16); bpt = Buf("fpt")
        av_ps = k.ps("fav", [128, D], F32); bav = Buf("fav")

        def load_xshw(w):
            for ch in range(DC):
                c0 = SDH + WIN * w - (1 if ch >= 4 else 0)
                P.op("g", lambda g, ch=ch, c0=c0: g.dma_start(
                        out=xshw[ch][:, :], in_=sd[128 * ch:128 * (ch + 1), c0:c0 + WIN]),
                     reads=[bsd], writes=[bxw[ch]], dma=lsd)

        def load_tabs(w):
            c0 = SDH + WIN * w
            for a, nm in enumerate(["rot_cs", "rot_ss", "rot_ci", "rot_si"]):
                P.op("y", lambda y, c0=c0, a=a, nm=nm: y.dma_start(
                        out=tabs[:, a, :], in_=A[nm][:, c0:c0 + WIN]),
                     writes=[btab], dma=ltb)

        def qkv_part(dst, bdst, col0):
            for m in range(DC):
                sl = m % 2
                for kk in range(DC):
                    P.op("t", lambda t, m=m, kk=kk, sl=sl, col0=col0: t.matmul(
                            pj_ps[sl][:, 0:WIN],
                            wq[kk][:, col0 + 128 * m:col0 + 128 * (m + 1)],
                            xshw[kk][:, :], start=(kk == 0), stop=(kk == DC - 1)),
                         reads=[bwq[kk], bxw[kk]], writes=[bpj[sl]], quiet=(kk != DC - 1))
                P.op("v", lambda v, m=m, sl=sl: v.tensor_copy(dst[m][:, :],
                                                              pj_ps[sl][:, 0:WIN]),
                     reads=[bpj[sl]], writes=[bdst[m]])

        def rotary(dst, bdst, permb, bpermb, scol, bscol, ic, isn, is_k, rqslot):
            for ch in range(DC):
                P.op("s", lambda s, ch=ch: s.activation(sqsc[:, :], dst[ch][:, :], AF.Square),
                     reads=[bdst[ch]], writes=[bsqc])
                P.op("t", lambda t, ch=ch: t.matmul(
                        n2_ps[0:16, :], hself[:, 16 * ch:16 * (ch + 1)], sqsc[:, :],
                        start=(ch == 0), stop=(ch == DC - 1)),
                     reads=[bhs, bsqc], writes=[bn2])
            P.op("s", lambda s: s.activation(nrm[:, :], n2_ps[0:16, :], AF.Sqrt),
                 reads=[bn2], writes=[bnr])
            P.op("v", lambda v: v.tensor_scalar_max(nrm[:, :], nrm[:, :], 1e-12),
                 reads=[bnr], writes=[bnr])
            if is_k:
                P.op("v", lambda v: v.reciprocal(rkf[:, :], nrm[:, :]),
                     reads=[bnr], writes=[brk])
            else:
                for q2 in range(2):
                    P.op("t", lambda t, q2=q2: t.transpose(
                            rb_ps[:, 0:16], nrm[0:16, 128 * q2:128 * (q2 + 1)],
                            identf[0:16, 0:16]),
                         reads=[bnr, bIf], writes=[brb])
                    P.op("v", lambda v, q2=q2, rqslot=rqslot: v.tensor_copy(
                            rqt[rqslot][:, q2, :], rb_ps[:, 0:16]),
                         reads=[brb], writes=[brq[rqslot]])
                P.op("v", lambda v, rqslot=rqslot: v.reciprocal(
                        rqt[rqslot][:, :, :], rqt[rqslot][:, :, :]),
                     reads=[brq[rqslot]], writes=[brq[rqslot]])
                P.op("v", lambda v, rqslot=rqslot: v.tensor_scalar_mul(
                        rqt[rqslot][:, :, :], rqt[rqslot][:, :, :], 8.0),
                     reads=[brq[rqslot]], writes=[brq[rqslot]])
                P.op("v", lambda v, rqslot=rqslot: v.tensor_scalar_min(
                        rqt[rqslot][:, :, :], rqt[rqslot][:, :, :], 1e6),
                     reads=[brq[rqslot]], writes=[brq[rqslot]])
            for ch in range(DC):
                P.op("t", lambda t, ch=ch: t.matmul(rh_ps[:, :], permb[:, :], dst[ch][:, :],
                                                    start=True, stop=True),
                     reads=[bpermb, bdst[ch]], writes=[brh])
                P.op("v", lambda v, ch=ch, ic=ic: v.scalar_tensor_tensor(
                        rot1[:, :], dst[ch][:, :], scol[:, 0:1], tabs[:, ic, :],
                        ALU.mult, ALU.mult),
                     reads=[bdst[ch], bscol, btab], writes=[br1])
                P.op("v", lambda v, ch=ch, isn=isn: v.tensor_tensor(
                        rot3[:, :], rh_ps[:, :], tabs[:, isn, :], ALU.mult),
                     reads=[brh, btab], writes=[br3])
                if is_k:
                    P.op("v", lambda v: v.tensor_tensor(rot1[:, :], rot1[:, :],
                                                        rot3[:, :], ALU.add),
                         reads=[br3, br1], writes=[br1])
                    P.op("t", lambda t, ch=ch: t.matmul(
                            rb_ps[:, :], hbw[0:16, 128 * ch:128 * (ch + 1)], rkf[:, :],
                            start=True, stop=True),
                         reads=[bhw, brk], writes=[brb])
                    P.op("v", lambda v, ch=ch: v.tensor_tensor(
                            dst[ch][:, :], rot1[:, :], rb_ps[:, :], ALU.mult),
                         reads=[br1, brb, bdst[ch]], writes=[bdst[ch]])
                else:
                    P.op("v", lambda v, ch=ch: v.tensor_tensor(
                            dst[ch][:, :], rot1[:, :], rot3[:, :], ALU.add),
                         reads=[br3, br1, bdst[ch]], writes=[bdst[ch]])

        def stage(w):
            load_xshw(w)
            load_tabs(w)
            p3, p2 = w % 3, w % 2
            qkv_part(ksl[p3], bks[p3], D)
            rotary(ksl[p3], bks[p3], pk, bpk, ksc, bksc, 2, 3, True, None)
            if w >= -1:
                qkv_part(qsb[p2], bqs[p2], 0)
                rotary(qsb[p2], bqs[p2], pq, bpq, qsc, bqsc, 0, 1, False, p2)
            for t2 in range(2):
                sl = t2 % 2
                for nn in range(2):
                    for kk in range(DC):
                        P.op("t", lambda t, t2=t2, nn=nn, kk=kk, sl=sl: t.matmul(
                                pj_ps[sl][:, :],
                                xshw[kk][:, 128 * t2:128 * (t2 + 1)],
                                wq[kk][:, 2 * D + 512 * nn:2 * D + 512 * (nn + 1)],
                                start=(kk == 0), stop=(kk == DC - 1)),
                             reads=[bxw[kk], bwq[kk]], writes=[bpj[sl]],
                             quiet=(kk != DC - 1))
                    P.op("v", lambda v, t2=t2, nn=nn, sl=sl, p3=p3: v.tensor_copy(
                            vsl[p3][t2][:, 512 * nn:512 * (nn + 1)], pj_ps[sl][:, :]),
                         reads=[bpj[sl]], writes=[bvs[p3][t2]])

        def attn_tile(w, qt2, mski, extra):
            parp, parc = (w - 1) % 3, w % 3
            p2 = w % 2
            for h in range(HEADS1):
                ch, hh = h // 2, h % 2
                pp = h % 2
                r0 = 64 * hh
                P.op("t", lambda t, ch=ch, r0=r0, qt2=qt2, p2=p2, parp=parp, pp=pp: t.matmul(
                        s_ps[pp][:, 0:WIN],
                        qsb[p2][ch][r0:r0 + 64, 128 * qt2:128 * (qt2 + 1)],
                        ksl[parp][ch][r0:r0 + 64, :], start=True, stop=False),
                     reads=[bqs[p2][ch], bks[parp][ch]], writes=[bS[pp]])
                P.op("t", lambda t, ch=ch, r0=r0, qt2=qt2, p2=p2, parc=parc, pp=pp: t.matmul(
                        s_ps[pp][:, WIN:2 * WIN],
                        qsb[p2][ch][r0:r0 + 64, 128 * qt2:128 * (qt2 + 1)],
                        ksl[parc][ch][r0:r0 + 64, :], start=False, stop=False),
                     reads=[bqs[p2][ch], bks[parc][ch]], writes=[bS[pp]])
                P.op("t", lambda t, mski=mski, pp=pp: t.matmul(
                        s_ps[pp][:, :], identb[:, :], msk[:, mski, :],
                        start=False, stop=True),
                     reads=[bIb, bmk], writes=[bS[pp]])
                P.op("v", lambda v, pp=pp: v.reduce_max(stt[pp][:, 0:1], s_ps[pp][:, :],
                                                        axis=AX.X),
                     reads=[bS[pp]], writes=[bstt[pp]])
                P.op("v", lambda v, h=h, qt2=qt2, p2=p2, pp=pp: v.scalar_tensor_tensor(
                        stt[pp][:, 1:2], rqt[p2][:, qt2, h:h + 1], -1.0, stt[pp][:, 0:1],
                        ALU.mult, ALU.mult),
                     reads=[brq[p2], bstt[pp]], writes=[bstt[pp]])
                P.op("s", lambda s, h=h, qt2=qt2, p2=p2, pp=pp: s.activation(
                        Pw[pp][:, :], s_ps[pp][:, :], AF.Exp, bias=stt[pp][:, 1:2],
                        scale=rqt[p2][:, qt2, h:h + 1], accum_out=stt[pp][:, 2:3]),
                     reads=[bS[pp], bstt[pp], brq[p2]], writes=[bPw[pp], bstt[pp]])
                P.op("v", lambda v, pp=pp: v.reciprocal(stt[pp][:, 3:4], stt[pp][:, 2:3]),
                     reads=[bstt[pp]], writes=[bstt[pp]])
                P.op("v", lambda v, pp=pp: v.tensor_scalar_mul(Pw[pp][:, :], Pw[pp][:, :],
                                                               stt[pp][:, 3:4]),
                     reads=[bstt[pp], bPw[pp]], writes=[bPw[pp]])
                for u in range(4):
                    P.op("t", lambda t, u=u, pp=pp: t.transpose(
                            pt_ps[:, 128 * u:128 * (u + 1)],
                            Pw[pp][:, 128 * u:128 * (u + 1)], identb[:, :]),
                         reads=[bPw[pp], bIb], writes=[bpt], quiet=(u != 3))
                P.op("v", lambda v, pp=pp: v.tensor_copy(PTs[pp][:, :], pt_ps[:, :]),
                     reads=[bpt], writes=[bPT[pp]])
                for u in range(4):
                    par = parp if u < 2 else parc
                    t2 = u % 2
                    P.op("t", lambda t, u=u, h=h, par=par, t2=t2, pp=pp: t.matmul(
                            av_ps[:, 64 * h:64 * (h + 1)], PTs[pp][:, 128 * u:128 * (u + 1)],
                            vsl[par][t2][:, 64 * h:64 * (h + 1)],
                            start=(h % 8 == 0 and u == 0), stop=(h % 8 == 7 and u == 3)),
                         reads=[bPT[pp], bvs[par][t2]], writes=[bav],
                         quiet=not (h == HEADS1 - 1 and u == 3))
            P.op("v", lambda v: v.tensor_copy(aosb[:, :], av_ps[:, :]),
                 reads=[bav], writes=[bao])
            for ch in range(DC):
                P.op("t", lambda t, ch=ch: t.transpose(
                        pt_ps[:, 0:128], aosb[:, 128 * ch:128 * (ch + 1)], identb[:, :]),
                     reads=[bao, bIb], writes=[bpt])
                P.op("v", lambda v, ch=ch: v.tensor_copy(aoT[ch][:, :], pt_ps[:, 0:128]),
                     reads=[bpt], writes=[baoT[ch]])
            for m in range(DC):
                for kk in range(DC):
                    P.op("t", lambda t, m=m, kk=kk: t.matmul(
                            pj_ps[0][:, 0:128], wo[kk][:, 128 * m:128 * (m + 1)],
                            aoT[kk][:, :], start=(kk == 0), stop=(kk == DC - 1)),
                         reads=[bwo[kk], baoT[kk]], writes=[bpj[0]], quiet=(kk != DC - 1))
                if not extra:
                    c0 = SDH + WIN * w + 128 * qt2
                    P.op("y", lambda y, m=m, c0=c0: y.dma_start(
                            out=res[m][:, 0:128], in_=sd[128 * m:128 * (m + 1), c0:c0 + 128]),
                         reads=[bsd], writes=[bres[m]], dma=lres)
                    P.op("v", lambda v, m=m: v.tensor_tensor(
                            res[m][:, 0:128], res[m][:, 0:128], pj_ps[0][:, 0:128], ALU.add),
                         reads=[bpj[0], bres[m]], writes=[bres[m]])
                    P.op("y", lambda y, m=m, c0=c0: y.dma_start(
                            out=sd[128 * m:128 * (m + 1), c0:c0 + 128], in_=res[m][:, 0:128]),
                         reads=[bres[m]], writes=[bsd], dma=swr)
                else:
                    P.op("y", lambda y, m=m: y.dma_start(
                            out=res[m][:, 0:1], in_=sd[128 * m:128 * (m + 1), SDH - 1:SDH],
                            allow_slow_non_contiguous=True),
                         reads=[bsd], writes=[bres[m]], dma=lres)
                    P.op("v", lambda v, m=m: v.tensor_tensor(
                            res[m][:, 0:1], res[m][:, 0:1], pj_ps[0][:, 127:128], ALU.add),
                         reads=[bpj[0], bres[m]], writes=[bres[m]])
                    P.op("y", lambda y, m=m: y.dma_start(
                            out=sd[128 * m:128 * (m + 1), SDH - 1:SDH], in_=res[m][:, 0:1],
                            allow_slow_non_contiguous=True),
                         reads=[bres[m]], writes=[bsd], dma=swr)

        stage(-2)
        stage(-1)
        stage(0)
        attn_tile(-1, 1, 4, True)
        for w in range(NWIN):
            if w + 1 < NWIN:
                stage(w + 1)
            mski0 = 0 if w == 0 else 2
            attn_tile(w, 0, mski0, False)
            attn_tile(w, 1, mski0 + 1, False)
        k.pop_scope()

        k.push_scope()
        xnb = [k.sb(f"fxnb_{i}", [128, TOK], BF16) for i in range(DC)]
        bxnb = [Buf(f"fxnb_{i}") for i in range(DC)]
        aG = k.sb("faG", [128, (F2P // 128) * TOK], BF16); baG = Buf("faG")
        xsh = [k.sb(f"fxshf_{i}", [128, 512], BF16) for i in range(2)]
        bxsh = [Buf(f"fxshf_{i}") for i in range(2)]
        x2t = k.sb("fx2t", [128, 512], BF16); bx2 = Buf("fx2t")
        srow = k.sb("fsrow", [1, 4, TOK], F32); bsr = Buf("fsrow")
        ab_row = k.sb("fabrow", [1, 2, TOK], F32); babr = Buf("fabrow")
        w1b = [k.sb(f"fw1b_{i}", [128, DC, 256], BF16) for i in range(2)]
        bw1b = [Buf(f"fw1b_{i}") for i in range(2)]
        w2b = [k.sb(f"fw2b_{i}", [128, D], BF16) for i in range(2)]
        bw2b = [Buf(f"fw2b_{i}") for i in range(2)]
        glb = k.sb("fglb", [128, 512], BF16); bgl = Buf("fglb")
        ffr = [k.sb(f"ffr_{i}", [128, 512], F32) for i in range(2)]
        bffr = [Buf(f"ffr_{i}") for i in range(2)]

        onescb = k.sb("fonescb", [128, 1], BF16); bocb = Buf("fonescb")
        P.op("v", lambda v: v.tensor_copy(onescb[:, :], onesc[:, :]),
             reads=[boc], writes=[bocb])
        k.push_scope()
        st_ps = k.ps("fst", [1, 512], F32); bst2 = Buf("fst")
        sq_ps = k.ps("fsq", [1, 512], F32); bsq2 = Buf("fsq")
        for t4 in range(4):
            c0 = SDH + 512 * t4
            for ch in range(DC):
                sl = ch % 2
                cs = c0 - (1 if ch >= 4 else 0)
                P.op("g", lambda g, ch=ch, cs=cs, sl=sl: g.dma_start(
                        out=xsh[sl][:, :], in_=sd[128 * ch:128 * (ch + 1), cs:cs + 512]),
                     reads=[bsd], writes=[bxsh[sl]], dma=lsd)
                P.op("v", lambda v, sl=sl: v.tensor_tensor(
                        x2t[:, :], xsh[sl][:, :], xsh[sl][:, :], ALU.mult),
                     reads=[bxsh[sl]], writes=[bx2])
                P.op("t", lambda t, ch=ch, sl=sl: t.matmul(
                        st_ps[0:1, :], onescb[:, :], xsh[sl][:, :],
                        start=(ch == 0), stop=(ch == DC - 1)),
                     reads=[bocb, bxsh[sl]], writes=[bst2])
                P.op("t", lambda t, ch=ch: t.matmul(
                        sq_ps[0:1, :], onescb[:, :], x2t[:, :],
                        start=(ch == 0), stop=(ch == DC - 1)),
                     reads=[bocb, bx2], writes=[bsq2])
                P.op("v", lambda v, ch=ch, t4=t4, sl=sl: v.tensor_copy(
                        xnb[ch][:, 512 * t4:512 * (t4 + 1)], xsh[sl][:, :]),
                     reads=[bxsh[sl]], writes=[bxnb[ch]])
            P.op("v", lambda v, t4=t4: v.tensor_scalar_mul(
                    srow[0:1, 0, 512 * t4:512 * (t4 + 1)], st_ps[0:1, :], 1.0 / D),
                 reads=[bst2], writes=[bsr])
            P.op("v", lambda v, t4=t4: v.tensor_copy(
                    srow[0:1, 1, 512 * t4:512 * (t4 + 1)], sq_ps[0:1, :]),
                 reads=[bsq2], writes=[bsr])
        P.op("v", lambda v: v.tensor_tensor(srow[0:1, 2, :], srow[0:1, 0, :],
                                            srow[0:1, 0, :], ALU.mult),
             reads=[bsr], writes=[bsr])
        P.op("v", lambda v: v.scalar_tensor_tensor(srow[0:1, 2, :], srow[0:1, 1, :],
                                                   1.0 / D, srow[0:1, 2, :],
                                                   ALU.mult, ALU.subtract),
             reads=[bsr], writes=[bsr])
        P.op("v", lambda v: v.tensor_scalar_add(srow[0:1, 2, :], srow[0:1, 2, :], 1e-5),
             reads=[bsr], writes=[bsr])
        P.op("s", lambda s: s.activation(srow[0:1, 2, :], srow[0:1, 2, :], AF.Sqrt),
             reads=[bsr], writes=[bsr])
        P.op("v", lambda v: v.reciprocal(ab_row[0:1, 0, :], srow[0:1, 2, :]),
             reads=[bsr], writes=[babr])
        P.op("v", lambda v: v.scalar_tensor_tensor(ab_row[0:1, 1, :], srow[0:1, 0, :],
                                                   -1.0, ab_row[0:1, 0, :],
                                                   ALU.mult, ALU.mult),
             reads=[babr, bsr], writes=[babr])
        abc_ps = k.ps("fabc", [128, 512], F32); babc = Buf("fabc")
        bbc_ps = k.ps("fbbc", [128, 512], F32); bbbc = Buf("fbbc")
        for t4 in range(4):
            P.op("t", lambda t, t4=t4: t.matmul(
                    abc_ps[:, :], onesr[:, :], ab_row[0:1, 0, 512 * t4:512 * (t4 + 1)],
                    start=True, stop=True),
                 reads=[bor, babr], writes=[babc])
            P.op("t", lambda t, t4=t4: t.matmul(
                    bbc_ps[:, :], onesr[:, :], ab_row[0:1, 1, 512 * t4:512 * (t4 + 1)],
                    start=True, stop=True),
                 reads=[bor, babr], writes=[bbbc])
            for ch in range(DC):
                P.op("v", lambda v, ch=ch, t4=t4: v.tensor_tensor(
                        xnb[ch][:, 512 * t4:512 * (t4 + 1)],
                        xnb[ch][:, 512 * t4:512 * (t4 + 1)], abc_ps[:, :], ALU.mult),
                     reads=[babc, bxnb[ch]], writes=[bxnb[ch]])
                P.op("v", lambda v, ch=ch, t4=t4: v.tensor_tensor(
                        xnb[ch][:, 512 * t4:512 * (t4 + 1)],
                        xnb[ch][:, 512 * t4:512 * (t4 + 1)], bbc_ps[:, :], ALU.add),
                     reads=[bbbc, bxnb[ch]], writes=[bxnb[ch]])
                P.op("v", lambda v, ch=ch, t4=t4: v.tensor_scalar(
                        xnb[ch][:, 512 * t4:512 * (t4 + 1)],
                        xnb[ch][:, 512 * t4:512 * (t4 + 1)],
                        lngc[:, ch:ch + 1], lnbc[:, ch:ch + 1], ALU.mult, ALU.add),
                     reads=[blg, blb, bxnb[ch]], writes=[bxnb[ch]])

        k.pop_scope()

        k.push_scope()
        h_ps = [k.ps(f"fh_{i}", [128, 512], F32) for i in range(2)]
        bh2 = [Buf(f"fh_{i}") for i in range(2)]
        for fp in range(F1P // 256):
            sl = fp % 2
            P.op("g", lambda g, fp=fp, sl=sl, li=li: g.dma_start(
                    out=w1b[sl][:, :, :],
                    in_=A["f_w1p"][li][:, 256 * fp:256 * (fp + 1)].rearrange(
                        "(c p) f -> p c f", p=128)),
                 writes=[bw1b[sl]], dma=lw1)
            for nn in range(4):
                for kk in range(DC):
                    P.op("t", lambda t, nn=nn, kk=kk, sl=sl: t.matmul(
                            h_ps[0][:, :], w1b[sl][:, kk, 0:128],
                            xnb[kk][:, 512 * nn:512 * (nn + 1)],
                            start=(kk == 0), stop=(kk == DC - 1)),
                         reads=[bw1b[sl], bxnb[kk]], writes=[bh2[0]],
                         quiet=(kk != DC - 1))
                    P.op("t", lambda t, nn=nn, kk=kk, sl=sl: t.matmul(
                            h_ps[1][:, :], w1b[sl][:, kk, 128:256],
                            xnb[kk][:, 512 * nn:512 * (nn + 1)],
                            start=(kk == 0), stop=(kk == DC - 1)),
                         reads=[bw1b[sl], bxnb[kk]], writes=[bh2[1]],
                         quiet=(kk != DC - 1))
                P.op("s", lambda s, fp=fp: s.activation(
                        glb[:, :], h_ps[1][:, :], AF.Gelu,
                        bias=b1c[:, 2 * fp + 1:2 * fp + 2]),
                     reads=[bh2[1], bb1], writes=[bgl])
                P.op("v", lambda v, fp=fp, nn=nn: v.scalar_tensor_tensor(
                        aG[:, TOK * fp + 512 * nn:TOK * fp + 512 * (nn + 1)],
                        h_ps[0][:, :], b1c[:, 2 * fp:2 * fp + 1], glb[:, :],
                        ALU.add, ALU.mult),
                     reads=[bh2[0], bb1, bgl], writes=[baG])

        k.pop_scope()

        k.push_scope()
        o_ps = [k.ps(f"fo_{i}", [128, 512], F32) for i in range(DC)]
        bo2 = [Buf(f"fo_{i}") for i in range(DC)]
        NKF = F2P // 128
        for nn in range(4):
            for kf in range(NKF):
                sl = kf % 2
                P.op("g", lambda g, kf=kf, sl=sl, li=li: g.dma_start(
                        out=w2b[sl][:, :], in_=A["f_w2p"][li, 128 * kf:128 * (kf + 1), :]),
                     writes=[bw2b[sl]], dma=lw2)
                for m in range(DC):
                    P.op("t", lambda t, kf=kf, m=m, nn=nn, sl=sl: t.matmul(
                            o_ps[m][:, :], w2b[sl][:, 128 * m:128 * (m + 1)],
                            aG[:, TOK * kf + 512 * nn:TOK * kf + 512 * (nn + 1)],
                            start=(kf == 0), stop=(kf == NKF - 1)),
                         reads=[bw2b[sl], baG], writes=[bo2[m]],
                         quiet=(m != DC - 1))
            for m in range(DC):
                sl = m % 2
                c0 = SDH + 512 * nn
                P.op("y", lambda y, m=m, c0=c0, sl=sl: y.dma_start(
                        out=ffr[sl][:, :], in_=sd[128 * m:128 * (m + 1), c0:c0 + 512]),
                     reads=[bsd], writes=[bffr[sl]], dma=lres)
                P.op("v", lambda v, m=m, sl=sl: v.scalar_tensor_tensor(
                        ffr[sl][:, :], o_ps[m][:, :], 1.0, ffr[sl][:, :],
                        ALU.mult, ALU.add),
                     reads=[bo2[m], bffr[sl]], writes=[bffr[sl]])
                P.op("v", lambda v, m=m, sl=sl: v.tensor_scalar_add(
                        ffr[sl][:, :], ffr[sl][:, :], b2c[:, m:m + 1]),
                     reads=[bb2, bffr[sl]], writes=[bffr[sl]])
                P.op("y", lambda y, m=m, c0=c0, sl=sl: y.dma_start(
                        out=sd[128 * m:128 * (m + 1), c0:c0 + 512], in_=ffr[sl][:, :]),
                     reads=[bffr[sl]], writes=[bsd], dma=swr)
        k.pop_scope()
        k.pop_scope()

    k.push_scope()
    dch = [k.sb(f"fdch_{i}", [128, TOK], F32) for i in range(DC)]
    bdch = [Buf(f"fdch_{i}") for i in range(DC)]
    pch = k.sb("fpch", [128, TOK], F32); bpch = Buf("fpch")
    ytl = [k.sb(f"fytl_{i}", [128, D], F32) for i in range(2)]
    byt = [Buf(f"fytl_{i}") for i in range(2)]
    x1t = k.sb("fx1t", [128, D], F32); bx1t = Buf("fx1t")
    tr_pf = k.ps("ftrf", [128, 512], F32); btrf = Buf("ftrf")
    for ch in range(DC):
        P.op("y", lambda y, ch=ch: y.dma_start(
                out=dch[ch][:, :], in_=sd[128 * ch:128 * (ch + 1), SDH:SDH + TOK]),
             reads=[bsd], writes=[bdch[ch]], dma=lfin)
        P.op("y", lambda y, ch=ch: y.dma_start(
                out=pch[:, :], in_=A["part_d"][128 * ch:128 * (ch + 1), :]),
             writes=[bpch], dma=lfin)
        P.op("v", lambda v, ch=ch: v.tensor_tensor(dch[ch][:, :], dch[ch][:, :],
                                                   pch[:, :], ALU.add),
             reads=[bpch, bdch[ch]], writes=[bdch[ch]])
    for t in range(NT):
        sl = t % 2
        P.op("y", lambda y, t=t: y.dma_start(
                out=x1t[:, :], in_=A["x1_d"][128 * t:128 * (t + 1), :]),
             writes=[bx1t], dma=lfin)
        for ch in range(DC):
            q = ch % 4
            P.op("t", lambda te, ch=ch, t=t, q=q: te.transpose(
                    tr_pf[:, 128 * q:128 * (q + 1)],
                    dch[ch][:, 128 * t:128 * (t + 1)], identf[:, :]),
                 reads=[bdch[ch], bIf], writes=[btrf])
            P.op("v", lambda v, ch=ch, q=q, sl=sl: v.tensor_copy(
                    ytl[sl][:, 128 * ch:128 * (ch + 1)], tr_pf[:, 128 * q:128 * (q + 1)]),
                 reads=[btrf], writes=[byt[sl]])
        P.op("v", lambda v, sl=sl: v.tensor_tensor(ytl[sl][:, :], ytl[sl][:, :],
                                                   x1t[:, :], ALU.add),
             reads=[bx1t, byt[sl]], writes=[byt[sl]])
        P.op("y", lambda y, t=t, sl=sl: y.dma_start(
                out=A["y"][128 * t:128 * (t + 1), :], in_=ytl[sl][:, :]),
             reads=[byt[sl]], dma=sy)
    k.pop_scope(fence=False)
    P.raw_wait("y", (sy, sy.count))


def build_fused():
    global _LAST_PLAN
    nc = bass.Bass("TRN2", target_bir_lowering=False)
    P = Plan()
    _LAST_PLAN = P
    ctx = contextlib.ExitStack()
    k = K(nc, ctx, P)
    A = {}
    ins = ["x_own", "x_full", "wqkvT0", "wg0", "ident", "moe_w1", "moe_w2",
           "prefmask", "gid1", "ut", "ones_col", "ones_row", "l1a_w", "l1a_wo",
           "l1a_bo", "perm", "hsel", "hbc", "msk_w0", "msk_int", "msk_ext",
           "halo_gate", "halo_rows", "rot_cs", "rot_ss", "rot_ci", "rot_si",
           "f_wqkv", "f_wo", "f_w1p", "f_b1p", "f_w2p", "f_b2", "f_lng",
           "f_lnb", "f_qs", "f_ks", "f_qs_c", "f_ks_c"]
    A["x_own"] = k.inp("x_own", [TOK, D])
    A["x_full"] = k.inp("x_full", [B * N, D])
    A["wqkvT0"] = k.inp("wqkvT0", [D, 3 * D])
    A["wg0"] = k.inp("wg0", [D, 16])
    A["ident"] = k.inp("ident", [128, 128])
    A["moe_w1"] = k.inp("moe_w1", [2, D, FF_MOE])
    A["moe_w2"] = k.inp("moe_w2", [2, FF_MOE, D])
    A["prefmask"] = k.inp("prefmask", [8, 1])
    A["gid1"] = k.inp("gid1", [128, NT])
    A["ut"] = k.inp("ut", [128, 128])
    A["ones_col"] = k.inp("ones_col", [128, 1])
    A["ones_row"] = k.inp("ones_row", [1, 128])
    A["l1a_w"] = k.inp("l1a_w", [D, 1536])
    A["l1a_wo"] = k.inp("l1a_wo", [512, D])
    A["l1a_bo"] = k.inp("l1a_bo", [128, DC])
    A["perm"] = k.inp("perm", [128, 128])
    A["hsel"] = k.inp("hsel", [128, DC * 16])
    A["hbc"] = k.inp("hbc", [16, DC * 128])
    A["msk_w0"] = k.inp("msk_w0", [2, 128, 512])
    A["msk_int"] = k.inp("msk_int", [2, 128, 512])
    A["msk_ext"] = k.inp("msk_ext", [128, 512])
    A["halo_gate"] = k.inp("halo_gate", [128, 1])
    A["halo_rows"] = k.inp("halo_rows", [128, DC])
    A["rot_cs"] = k.inp("rot_cs", [128, SDH + TOK])
    A["rot_ss"] = k.inp("rot_ss", [128, SDH + TOK])
    A["rot_ci"] = k.inp("rot_ci", [128, SDH + TOK])
    A["rot_si"] = k.inp("rot_si", [128, SDH + TOK])
    A["f_wqkv"] = k.inp("f_wqkv", [DEPTH1, D, 3072])
    A["f_wo"] = k.inp("f_wo", [DEPTH1, D, D])
    A["f_w1p"] = k.inp("f_w1p", [DEPTH1, D, F1P])
    A["f_b1p"] = k.inp("f_b1p", [DEPTH1, 128, F1P // 128])
    A["f_w2p"] = k.inp("f_w2p", [DEPTH1, F2P, D])
    A["f_b2"] = k.inp("f_b2", [DEPTH1, 128, DC])
    A["f_lng"] = k.inp("f_lng", [DEPTH1, 128, DC])
    A["f_lnb"] = k.inp("f_lnb", [DEPTH1, 128, DC])
    A["f_qs"] = k.inp("f_qs", [DEPTH1, 1, 128])
    A["f_ks"] = k.inp("f_ks", [DEPTH1, 1, 128])
    A["f_qs_c"] = k.inp("f_qs_c", [DEPTH1, 128, 1])
    A["f_ks_c"] = k.inp("f_ks_c", [DEPTH1, 128, 1])
    A["y"] = k.outp("y", [TOK, D])
    A["a0_d"] = k.dram("a0_d", [TOK, D])
    A["rt_d"] = k.dram("rt_d", [128, 32])
    A["moe_d"] = k.dram("moe_d", [TOK, D])
    A["x1_d"] = k.dram("x1_d", [TOK, D])
    A["xn1fm_d"] = k.dram("xn1fm_d", [D, TOK])
    A["part_d"] = k.dram("part_d", [D, TOK])
    A["kag_in"] = k.dram("kag_in", [D, TOK], BF16)
    A["kag"] = k.dram("kag", [4 * D, TOK], BF16)
    A["vag_in"] = k.dram("vag_in", [TOK, D], BF16)
    A["vag"] = k.dram("vag", [4 * TOK, D], BF16)
    A["cnt_in"] = k.dram("cnt_in", [1, 16])
    A["cnt_ag"] = k.dram("cnt_ag", [8, 16], shared=True)
    A["tbl_d"] = k.dram("tbl_d", [B * N, 8])
    A["tbl_rs"] = k.dram("tbl_rs", [TOK, 8])
    A["eo_d"] = k.dram("eo_d", [TOK, D])
    A["eo_ag"] = k.dram("eo_ag", [B * N, D], shared=True)
    A["ctx_in"] = k.dram("ctx_in", [64, 512])
    A["ctx_ar"] = k.dram("ctx_ar", [64, 512])
    A["sd"] = k.dram("sd", [D, SDH + TOK])
    A["hal_in"] = k.dram("hal_in", [D, SDH])
    A["hal_ag"] = k.dram("hal_ag", [4 * D, SDH])
    with ctx:
        k.push_scope()
        emit_l0(k, A, GRP_BATCH)
        k.pop_scope()
        k.push_scope()
        emit_moe(k, A)
        k.pop_scope()
        k.push_scope()
        emit_l1a(k, A)
        k.pop_scope()
        k.push_scope()
        emit_l1f(k, A)
        k.pop_scope(fence=False)
        with nc.Block() as block:
            P.emit(nc, ctx, block)
    return nc, ins, ["y"]


_LAST_PLAN = None


def build_launch_l1f():
    global _LAST_PLAN
    nc = bass.Bass("TRN2", target_bir_lowering=False)
    P = Plan()
    _LAST_PLAN = P
    ctx = contextlib.ExitStack()
    k = K(nc, ctx, P)
    A = {}
    A["xn1fm_d"] = k.inp("xn1fm_d", [D, TOK])
    A["part_d"] = k.inp("part_d", [D, TOK])
    A["x1_d"] = k.inp("x1_d", [TOK, D])
    A["ident"] = k.inp("ident", [128, 128])
    A["ones_row"] = k.inp("ones_row", [1, 128])
    A["ones_col"] = k.inp("ones_col", [128, 1])
    A["perm"] = k.inp("perm", [128, 128])
    A["hsel"] = k.inp("hsel", [128, DC * 16])
    A["hbc"] = k.inp("hbc", [16, DC * 128])
    A["msk_w0"] = k.inp("msk_w0", [2, 128, 512])
    A["msk_int"] = k.inp("msk_int", [2, 128, 512])
    A["msk_ext"] = k.inp("msk_ext", [128, 512])
    A["halo_gate"] = k.inp("halo_gate", [128, 1])
    A["halo_rows"] = k.inp("halo_rows", [128, DC])
    A["rot_cs"] = k.inp("rot_cs", [128, SDH + TOK])
    A["rot_ss"] = k.inp("rot_ss", [128, SDH + TOK])
    A["rot_ci"] = k.inp("rot_ci", [128, SDH + TOK])
    A["rot_si"] = k.inp("rot_si", [128, SDH + TOK])
    A["f_wqkv"] = k.inp("f_wqkv", [DEPTH1, D, 3072])
    A["f_wo"] = k.inp("f_wo", [DEPTH1, D, D])
    A["f_w1p"] = k.inp("f_w1p", [DEPTH1, D, F1P])
    A["f_b1p"] = k.inp("f_b1p", [DEPTH1, 128, F1P // 128])
    A["f_w2p"] = k.inp("f_w2p", [DEPTH1, F2P, D])
    A["f_b2"] = k.inp("f_b2", [DEPTH1, 128, DC])
    A["f_lng"] = k.inp("f_lng", [DEPTH1, 128, DC])
    A["f_lnb"] = k.inp("f_lnb", [DEPTH1, 128, DC])
    A["f_qs"] = k.inp("f_qs", [DEPTH1, 1, 128])
    A["f_ks"] = k.inp("f_ks", [DEPTH1, 1, 128])
    A["f_qs_c"] = k.inp("f_qs_c", [DEPTH1, 128, 1])
    A["f_ks_c"] = k.inp("f_ks_c", [DEPTH1, 128, 1])
    A["y"] = k.outp("y", [TOK, D])
    A["sd"] = k.dram("sd", [D, SDH + TOK])
    A["hal_in"] = k.dram("hal_in", [D, SDH])
    A["hal_ag"] = k.dram("hal_ag", [4 * D, SDH])
    with ctx:
        emit_l1f(k, A)
        with nc.Block() as block:
            P.emit(nc, ctx, block)
    ins = ["xn1fm_d", "part_d", "x1_d", "ident", "ones_row", "ones_col", "perm",
           "hsel", "hbc", "msk_w0", "msk_int", "msk_ext", "halo_gate", "halo_rows",
           "rot_cs", "rot_ss", "rot_ci", "rot_si", "f_wqkv", "f_wo", "f_w1p",
           "f_b1p", "f_w2p", "f_b2", "f_lng", "f_lnb", "f_qs", "f_ks",
           "f_qs_c", "f_ks_c"]
    return nc, ins, ["y"]


GRP_BATCH = [[0, 1, 2, 3], [4, 5, 6, 7]]
GRP_ALL = [list(range(8))]

PHASE_MODE = "split"
_DEV_PHASES = None


def build_launch_l0():
    nc = bass.Bass("TRN2", target_bir_lowering=False)
    P = Plan()
    ctx = contextlib.ExitStack()
    k = K(nc, ctx, P)
    A = {}
    A["x_own"] = k.inp("x_own", [TOK, D])
    A["wqkvT0"] = k.inp("wqkvT0", [D, 3 * D])
    A["wg0"] = k.inp("wg0", [D, 16])
    A["ident"] = k.inp("ident", [128, 128])
    A["a0_d"] = k.outp("a0_d", [TOK, D])
    A["rt_d"] = k.outp("rt_d", [128, 32])
    A["kag_in"] = k.dram("kag_in", [D, TOK], BF16)
    A["kag"] = k.dram("kag", [4 * D, TOK], BF16)
    A["vag_in"] = k.dram("vag_in", [TOK, D], BF16)
    A["vag"] = k.dram("vag", [4 * TOK, D], BF16)
    with ctx:
        emit_l0(k, A, GRP_BATCH)
        with nc.Block() as block:
            P.emit(nc, ctx, block)
    return nc, ["x_own", "wqkvT0", "wg0", "ident"], ["a0_d", "rt_d"]


def build_launch_moe():
    nc = bass.Bass("TRN2", target_bir_lowering=False)
    P = Plan()
    ctx = contextlib.ExitStack()
    k = K(nc, ctx, P)
    A = {}
    A["rt_d"] = k.inp("rt_d", [128, 32])
    A["x_full"] = k.inp("x_full", [B * N, D])
    A["moe_w1"] = k.inp("moe_w1", [2, D, FF_MOE])
    A["moe_w2"] = k.inp("moe_w2", [2, FF_MOE, D])
    A["prefmask"] = k.inp("prefmask", [8, 1])
    A["gid1"] = k.inp("gid1", [128, NT])
    A["ident"] = k.inp("ident", [128, 128])
    A["ut"] = k.inp("ut", [128, 128])
    A["ones_col"] = k.inp("ones_col", [128, 1])
    A["ones_row"] = k.inp("ones_row", [1, 128])
    A["moe_d"] = k.outp("moe_d", [TOK, D])
    A["dbg_d"] = k.outp("dbg_d", [128, 3 * NT])
    A["eo_dbg"] = k.outp("eo_dbg", [TOK, D])
    A["cnt_in"] = k.dram("cnt_in", [1, 16])
    A["cnt_ag"] = k.dram("cnt_ag", [8, 16], shared=True)
    A["tbl_d"] = k.dram("tbl_d", [B * N, 8])
    A["tbl_rs"] = k.dram("tbl_rs", [TOK, 8])
    A["eo_d"] = k.dram("eo_d", [TOK, D])
    A["eo_ag"] = k.dram("eo_ag", [B * N, D], shared=True)
    with ctx:
        emit_moe(k, A)
        with nc.Block() as block:
            P.emit(nc, ctx, block)
    return nc, ["rt_d", "x_full", "moe_w1", "moe_w2", "prefmask", "gid1",
                "ident", "ut", "ones_col", "ones_row"], ["moe_d", "dbg_d", "eo_dbg"]


_BUILD_CACHE = {}


def _run_launch(name, builder, in_maps):
    if name not in _BUILD_CACHE:
        _BUILD_CACHE[name] = builder()
    nc, in_names, out_names = _BUILD_CACHE[name]
    maps = [{n: im[n] for n in in_names} for im in in_maps]
    res = run_bass_kernel_spmd(nc, maps, core_ids=list(range(CORES)))
    outs = []
    for c in range(CORES):
        outs.append({n: np.asarray(res.results[c][n]) for n in out_names})
    return outs


FUSED = True


def kernel(**inputs):
    in_maps = prep_inputs(inputs)
    if FUSED:
        lf = _run_launch("fused", build_fused, in_maps)
        out = np.empty((B, N, D), np.float32)
        for c in range(CORES):
            b, j = divmod(c, 4)
            out[b, j * TOK:(j + 1) * TOK] = lf[c]["y"]
        return out
    l0 = _run_launch("l0", build_launch_l0, in_maps)
    for c in range(CORES):
        in_maps[c]["rt_d"] = l0[c]["rt_d"]
        in_maps[c]["a0_d"] = l0[c]["a0_d"]
    mo = _run_launch("moe", build_launch_moe, in_maps)
    for c in range(CORES):
        in_maps[c]["moe_d"] = mo[c]["moe_d"]
    la = _run_launch("l1a", build_launch_l1a, in_maps)
    for c in range(CORES):
        for n in ["x1_d", "xn1fm_d", "part_d"]:
            in_maps[c][n] = la[c][n]
    lf = _run_launch("l1f", build_launch_l1f, in_maps)
    out = np.empty((B, N, D), np.float32)
    for c in range(CORES):
        b, j = divmod(c, 4)
        out[b, j * TOK:(j + 1) * TOK] = lf[c]["y"]
    return out
